# version 06
# baseline (speedup 1.0000x reference)
import math
from collections import ChainMap
from contextlib import ExitStack

import numpy as np
import concourse.bass as bass
import concourse.mybir as mybir
from concourse.bass_utils import run_bass_kernel_spmd

F32 = mybir.dt.float32
BF16 = mybir.dt.bfloat16
AF = mybir.ActivationFunctionType
ALU = mybir.AluOpType
AX = mybir.AxisListType

NCORES = 8
SEQS = 2
T = 2048
D = 1024
NCH = 8
NTT = 4
MAGIC = 1.5 * (2.0 ** 23)
TWO_PI = 2.0 * math.pi
LAMBDA_INIT = 0.8 - 0.6 * math.exp(-0.3 * 12)

ENGINES = ("tensor", "vector", "scalar", "gpsimd", "sync")
ENG_SEM = {"tensor": "pe", "vector": "ve", "scalar": "se", "gpsimd": "ge", "sync": "sy"}
DMA_SEM = {"sync": "dx", "gpsimd": "dw", "scalar": "ds", "vector": "dv", "tensor": "dt"}
NDMA_LANES = 4
DMA_LANE_SEMS = [q + str(l) for q in DMA_SEM.values() for l in range(NDMA_LANES)]


class Buf:
    __slots__ = ("wr", "rd", "name", "excl")

    def __init__(self, name="", excl=False):
        self.wr = None
        self.rd = []
        self.name = name
        self.excl = excl


class Sched:
    def __init__(self):
        self.cmds = {e: [] for e in ENGINES}
        self.count = {}
        self.seen = {e: {} for e in ENGINES}
        self.dma_lane = {}
        self.cur_inst = -1

    def _bump(self, sem, n):
        self.count[sem] = self.count.get(sem, 0) + n
        return self.count[sem]

    def _waits_for(self, eng, reads, writes, waw=True):
        own = ENG_SEM[eng]
        tags = []
        for b in reads:
            if b.wr is not None:
                tags.append(b.wr)
            if b.excl:
                tags.extend(b.rd)
        for b in writes:
            if b.wr is not None and waw:
                tags.append(b.wr)
            tags.extend(b.rd)
        need = {}
        for sem, val, strict in tags:
            if sem == own and not strict:
                continue
            if val > need.get(sem, 0):
                need[sem] = val
        out = []
        for sem, val in need.items():
            if self.seen[eng].get(sem, 0) < val:
                out.append((sem, val))
                self.seen[eng][sem] = val
        return out

    def step(self, eng, fns, reads=(), writes=(), dma=False, tag_writes=True, small=False,
             waw=True):
        if not isinstance(fns, (list, tuple)):
            fns = [fns]
        waits = self._waits_for(eng, reads, writes, waw=waw)
        if dma:
            base = DMA_SEM[eng]
            lane = self.dma_lane.get(base, 0)
            self.dma_lane[base] = (lane + 1) % NDMA_LANES
            sem = base + str(lane)
            prev = self.count.get(sem, 0)
            if prev > 0 and self.seen[eng].get(sem, 0) < prev:
                waits.append((sem, prev))
                self.seen[eng][sem] = prev
            val = self._bump(sem, 16)
            tag = (sem, val, False)
            self.cmds[eng].append((waits, fns, (sem, 16), self.cur_inst))
        else:
            sem = ENG_SEM[eng]
            val = self._bump(sem, 1)
            tag = (sem, val, bool(small))
            self.cmds[eng].append((waits, fns, (sem, 1), self.cur_inst))
        for b in reads:
            b.rd.append(tag)
        if tag_writes:
            for b in writes:
                b.wr = tag
                b.rd = []
        return tag

    def full_barrier(self):
        vals = [(ENG_SEM[e2], self.count.get(ENG_SEM[e2], 0)) for e2 in ENGINES]
        vals += [(s, self.count[s]) for s in DMA_LANE_SEMS if self.count.get(s, 0)]
        for e in ENGINES:
            for sem, v in vals:
                if v > 0 and sem != ENG_SEM[e]:
                    self.barrier_wait(e, (sem, v))

    def barrier_wait(self, eng, tag):
        sem, val = tag
        if self.seen[eng].get(sem, 0) < val:
            self.seen[eng][sem] = val
            self.cmds[eng].append(([(sem, val)], [], None, -1))

    def finalize(self, nc, sems, ctx=None):
        with nc.Block() as block:
            for engname in ENGINES:
                def make(en):
                    def body(eng):
                        for waits, fns, inc, inst in self.cmds[en]:
                            if ctx is not None:
                                ctx.activate(inst)
                            for s, v in waits:
                                eng.wait_ge(sems[s], v)
                            ins = None
                            for f in fns:
                                ins = f(eng)
                            if inc is not None and ins is not None:
                                ins.then_inc(sems[inc[0]], inc[1])
                    return body
                getattr(block, engname)(make(engname))


def _chunks(w):
    din, dout = w.shape
    return np.ascontiguousarray(w.reshape(din // 128, 128, dout))


def _pairswap(w, block):
    dout = w.shape[1]
    idx = np.arange(dout)
    within = idx % block
    sw = idx - within + (within ^ 1)
    return w[:, sw]


def prep_consts():
    c = {}
    RIEMANN = np.array([14.134725142, 21.022039639, 25.01085758, 30.424876126,
                        32.935061588, 37.586178159, 40.918719012, 43.327073281,
                        48.005150881, 49.773832478, 52.970321478, 56.446247697,
                        59.347044003, 60.831778525, 65.112544048, 67.079810529,
                        69.546401711, 72.067157674, 75.704690699, 77.144840069,
                        79.33737502, 82.910380854, 84.735492981, 87.425274613,
                        88.809111208, 92.491899271, 94.651344041, 95.870634228,
                        98.831194218, 101.317851006, 103.72553804, 105.446623052],
                       dtype=np.float32)
    inv_freq = (1.0 / (RIEMANN / RIEMANN[0])).astype(np.float32)
    pos = np.arange(T, dtype=np.float32)
    th = pos[None, :] * inv_freq[:, None]
    c["theta64"] = np.ascontiguousarray(np.repeat(th, 2, axis=0).astype(np.float32))
    sgn = np.tile(np.array([-1.0, 1.0], np.float32), 64).reshape(128, 1)
    c["sign128"] = np.ascontiguousarray(np.concatenate([sgn, sgn], axis=1))
    kk2, qq2 = np.meshgrid(np.arange(128), np.arange(128), indexing="ij")
    c["tril"] = np.where(qq2 >= kk2, 1.0, 0.0).astype(np.float32)
    hd = 32
    inv = (1.0 / 10000.0 ** (np.arange(0, hd, 2, dtype=np.float32) / hd)).astype(np.float32)
    fr = np.outer(np.arange(T, dtype=np.float32), inv)
    cos3 = np.cos(fr).T
    sin3 = np.sin(fr).T
    A32 = np.repeat(cos3, 2, axis=0)
    B32 = np.repeat(sin3, 2, axis=0) * np.tile(np.array([-1.0, 1.0], np.float32), 16)[:, None]
    c["s3_A"] = np.ascontiguousarray(np.tile(A32, (4, 1)).astype(np.float32))
    c["s3_B"] = np.ascontiguousarray(np.tile(B32, (4, 1)).astype(np.float32))
    c["ident"] = np.eye(128, dtype=np.float32)
    i64 = np.eye(64, dtype=np.float32)
    c["ident64d"] = np.ascontiguousarray(np.vstack([i64, i64]))
    kk, qq = np.meshgrid(np.arange(128), np.arange(128), indexing="ij")
    c["diagmask"] = np.where(qq >= kk, 0.0, -1e30).astype(np.float32)
    return c


def prep_inputs(inp, core):
    d = {}
    x = inp["L0_attn_a_x"]
    xs = x[core * SEQS:(core + 1) * SEQS]
    d["xin"] = np.ascontiguousarray(xs.transpose(0, 2, 1)).astype(np.float32)

    wqkv = inp["L0_attn_a_w_qkv"]
    wq, wk, wv = wqkv[:, 0:D], wqkv[:, D:2 * D], wqkv[:, 2 * D:3 * D]
    d["s1_wq"] = _chunks(wq)
    d["s1_wq_sw"] = _chunks(_pairswap(wq, 64))
    d["s1_wk"] = _chunks(wk)
    d["s1_wk_sw"] = _chunks(_pairswap(wk, 64))
    d["s1_wv"] = _chunks(wv)
    d["s1_wo"] = _chunks(inp["L0_attn_a_w_o"])
    d["s1_logscale"] = inp["L0_attn_a_log_scale"].reshape(1, 16).astype(np.float32)

    for nm, key in (("wq", "L0_attn_b_Wq"), ("wk", "L0_attn_b_Wk")):
        wT = np.ascontiguousarray(inp[key].T)
        d[f"s3_{nm}"] = _chunks(wT)
        d[f"s3_{nm}_sw"] = _chunks(_pairswap(wT, 32))
    d["s3_wv"] = _chunks(np.ascontiguousarray(inp["L0_attn_b_Wv"].T))
    d["s3_wo"] = _chunks(np.ascontiguousarray(inp["L0_attn_b_Wo"].T))
    d["s3_lam"] = np.concatenate(
        [inp["L0_attn_b_lambda_q1"], inp["L0_attn_b_lambda_k1"],
         inp["L0_attn_b_lambda_q2"], inp["L0_attn_b_lambda_k2"]]).reshape(1, 128).astype(np.float32)
    d["s3_gamma"] = inp["L0_attn_b_gamma"].reshape(64, 1).astype(np.float32)

    d["gwT"] = np.ascontiguousarray(inp["L0_ffn_gate_w"].T)
    d["gbias"] = inp["L0_ffn_gate_bias"].reshape(1, 16).astype(np.float32)
    w1, w2, w3 = inp["L0_ffn_w1"], inp["L0_ffn_w2"], inp["L0_ffn_w3"]
    d["moe_w1T"] = np.ascontiguousarray(w1.transpose(0, 2, 1).reshape(16, 8, 128, 768))
    d["moe_w3T"] = np.ascontiguousarray(w3.transpose(0, 2, 1).reshape(16, 8, 128, 768))
    d["moe_w2T"] = np.ascontiguousarray(w2.transpose(0, 2, 1).reshape(16, 6, 128, 1024))
    d["ws1T"] = _chunks(np.ascontiguousarray(inp["L0_ffn_ws1"].T))
    d["ws3T"] = _chunks(np.ascontiguousarray(inp["L0_ffn_ws3"].T))
    d["ws2T"] = np.ascontiguousarray(inp["L0_ffn_ws2"].T.reshape(6, 128, 1024))

    wqkvT = np.ascontiguousarray(inp["L1_attn_a_Wqkv"].T)
    d["s5_wq"] = _chunks(np.ascontiguousarray(wqkvT[:, 0:D]))
    d["s5_wk"] = _chunks(np.ascontiguousarray(wqkvT[:, D:2 * D]))
    d["s5_wv"] = _chunks(np.ascontiguousarray(wqkvT[:, 2 * D:3 * D]))
    d["s5_wproj"] = _chunks(np.ascontiguousarray(inp["L1_attn_a_Wproj"].T))

    d["W1c"] = _chunks(inp["L1_ffn_W1"])
    d["W2c"] = _chunks(inp["L1_ffn_W2"])
    d["b1c"] = np.ascontiguousarray(inp["L1_ffn_b1"].reshape(32, 128).T)
    d["b2c"] = np.ascontiguousarray(inp["L1_ffn_b2"].reshape(8, 128).T)
    d["qth"] = np.ascontiguousarray(inp["L1_ffn_quantum_phases"].reshape(32, 128).T)
    d["qwr"] = np.ascontiguousarray(inp["L1_ffn_quantum_weights_real"].reshape(32, 128).T)
    d["qwi"] = np.ascontiguousarray(inp["L1_ffn_quantum_weights_imag"].reshape(32, 128).T)

    d["s7_wq"] = _chunks(np.ascontiguousarray(inp["L1_attn_b_Wq"].T))
    d["s7_wk"] = _chunks(np.ascontiguousarray(inp["L1_attn_b_Wk"].T))
    d["s7_wv"] = _chunks(np.ascontiguousarray(inp["L1_attn_b_Wv"].T))
    d["s7_wu"] = _chunks(np.ascontiguousarray(inp["L1_attn_b_Wu"].T))
    d["s7_bu"] = np.ascontiguousarray(inp["L1_attn_b_bu"].reshape(8, 128).T)

    d.update(prep_consts())
    return {k: np.ascontiguousarray(v).astype(np.float32) for k, v in d.items()}


class Ctx:
    def __init__(self, nc, sch, stack):
        self.nc = nc
        self.sch = sch
        self.stack = stack
        self.common_sb = {}
        self.common_buf = {}
        self.stage_sb = {}
        self.stage_buf = {}
        self.sb = self.common_sb
        self.buf = self.common_buf

    def activate(self, inst):
        self.inst = inst
        if inst < 0:
            self.sb = self.common_sb
            self.buf = self.common_buf
        else:
            sd = self.stage_sb.setdefault(inst, {})
            bd = self.stage_buf.setdefault(inst, {})
            self.sb = ChainMap(sd, self.common_sb)
            self.buf = ChainMap(bd, self.common_buf)

    inst = -1

    def _uname(self, name):
        return name if self.inst < 0 else "%s_i%d" % (name, self.inst)

    def sbuf(self, name, shape, dtype):
        t = self.stack.enter_context(self.nc.sbuf_tensor(self._uname(name), list(shape), dtype))
        self.sb[name] = t
        self.buf[name] = Buf(name)
        return t

    def psum(self, name, shape, dtype):
        t = self.stack.enter_context(self.nc.psum_tensor(self._uname(name), list(shape), dtype))
        self.sb[name] = t
        self.buf[name] = Buf(name, excl=True)
        return t

    def b(self, name):
        return self.buf[name]

    def t(self, name):
        return self.sb[name]


def alloc_common(ctx):
    ctx.sbuf("xbf", [128, NCH, T], BF16)
    for c in range(NCH):
        ctx.buf["xbf_%d" % c] = Buf("xbf_%d" % c)
    ctx.sbuf("sqtt", [128, NCH, 512], BF16)
    ctx.sbuf("xln", [128, NCH, T], BF16)
    ctx.sbuf("xio", [128, 2, 512], F32)
    for s in range(2):
        ctx.buf["xio_%d" % s] = Buf("xio_%d" % s)
    ctx.sbuf("sst", [1, 1536], F32)
    ctx.sbuf("tsub", [128, 2, 512], BF16)
    ctx.sbuf("rbs", [128, 512], BF16)
    ctx.sbuf("onesb", [128, 2], BF16)
    ctx.sbuf("onesrow", [1, 128], BF16)
    ctx.sbuf("epscol", [1, 1], F32)
    ctx.psum("ps_sum", [128, 512], F32)
    ctx.psum("ps_sq", [128, 512], F32)
    ctx.psum("pbc_m", [128, 512], F32)
    ctx.psum("pbc_r", [128, 512], F32)


def emit_init(ctx):
    sch = ctx.sch
    sch.step("vector", lambda e: e.memset(ctx.t("onesb")[:, :], 1.0), writes=(ctx.b("onesb"),), small=True)
    sch.step("vector", lambda e: e.memset(ctx.t("onesrow")[0:1, :], 1.0), writes=(ctx.b("onesrow"),), small=True)
    sch.step("vector", lambda e: e.memset(ctx.t("epscol")[0:1, 0:1], 1e-5), writes=(ctx.b("epscol"),), small=True)


def emit_ln(ctx, xin_ap, seq):
    sch = ctx.sch
    for c in range(NCH):
        sch.step("gpsimd", (lambda c=c: (lambda e: e.dma_start(
            ctx.t("xbf")[:, c, :], xin_ap[seq, c * 128:(c + 1) * 128, :])))(),
            writes=(ctx.b("xbf_%d" % c),), dma=True, waw=(c == 0))
    SUMROW = [("ps_sum", 0), ("ps_sum", 32), ("ps_sum", 64), ("pbc_m", 0)]
    SQROW = [("ps_sq", 0), ("ps_sq", 32), ("ps_sq", 64), ("pbc_r", 0)]
    xbufs = tuple(ctx.b("xbf_%d" % c) for c in range(NCH))
    for tt in range(NTT):
        tn, p = SUMROW[tt]
        for c in range(NCH):
            sch.step("scalar", (lambda c=c, tt=tt: (lambda e: e.activation(
                ctx.t("sqtt")[:, c, :], ctx.t("xbf")[:, c, tt * 512:(tt + 1) * 512], AF.Square)))(),
                reads=(ctx.b("xbf_%d" % c),), writes=(ctx.b("sqtt"),), tag_writes=(c == NCH - 1))
        fns = []
        for c in range(NCH):
            fns.append((lambda c=c, p=p, tt=tt, tn=tn: (lambda e: e.matmul(
                ctx.t(tn)[p:p + 1, :], ctx.t("onesb")[:, 0:1],
                ctx.t("xbf")[:, c, tt * 512:(tt + 1) * 512],
                start=(c == 0), stop=(c == NCH - 1))))())
        sch.step("tensor", fns, reads=xbufs + (ctx.b("onesb"),), writes=(ctx.b(tn),),
                 tag_writes=(tt >= NTT - 2))
        tn2, p2 = SQROW[tt]
        fns2 = []
        for c in range(NCH):
            fns2.append((lambda c=c, p2=p2, tn2=tn2: (lambda e: e.matmul(
                ctx.t(tn2)[p2:p2 + 1, :], ctx.t("onesb")[:, 0:1],
                ctx.t("sqtt")[:, c, :],
                start=(c == 0), stop=(c == NCH - 1))))())
        sch.step("tensor", fns2, reads=(ctx.b("sqtt"), ctx.b("onesb")), writes=(ctx.b(tn2),),
                 tag_writes=(tt >= NTT - 2))
    S = ctx.t("sst")
    SB_ = ctx.b("sst")
    RBF = ctx.t("rows_bf")
    RBFb = ctx.b("rows_bf")
    for tt in range(NTT):
        tn, p = SUMROW[tt]
        tn2, p2 = SQROW[tt]
        sch.step("vector", (lambda p=p, tn=tn: (lambda e: e.tensor_scalar(
            S[0:1, 0:512], ctx.t(tn)[p:p + 1, :], 1.0 / D, None, op0=ALU.mult)))(),
            reads=(ctx.b(tn),), writes=(SB_,), small=True)
        sch.step("vector", (lambda p2=p2, tn2=tn2: (lambda e: e.tensor_scalar(
            S[0:1, 512:1024], ctx.t(tn2)[p2:p2 + 1, :], 1.0 / D, None, op0=ALU.mult)))(),
            reads=(ctx.b(tn2),), writes=(SB_,), small=True)
        sch.step("vector", lambda e: e.tensor_mul(S[0:1, 1024:1536], S[0:1, 0:512], S[0:1, 0:512]),
                 reads=(SB_,), writes=(SB_,), small=True)
        sch.step("vector", lambda e: e.tensor_sub(S[0:1, 512:1024], S[0:1, 512:1024], S[0:1, 1024:1536]),
                 reads=(SB_,), writes=(SB_,), small=True)
        sch.step("scalar", lambda e: e.activation(S[0:1, 1024:1536], S[0:1, 512:1024], AF.Sqrt,
                                                  bias=ctx.t("epscol")[0:1, 0:1], scale=1.0),
                 reads=(SB_, ctx.b("epscol")), writes=(SB_,), small=True)
        sch.step("vector", lambda e: e.reciprocal(S[0:1, 512:1024], S[0:1, 1024:1536]),
                 reads=(SB_,), writes=(SB_,), small=True)
        sch.step("vector", (lambda tt=tt: (lambda e: e.tensor_copy(
            RBF[0:1, 0, tt * 512:(tt + 1) * 512], S[0:1, 0:512])))(),
            reads=(SB_,), writes=(RBFb,), tag_writes=False, small=True)
        sch.step("vector", (lambda tt=tt: (lambda e: e.tensor_copy(
            RBF[0:1, 1, tt * 512:(tt + 1) * 512], S[0:1, 512:1024])))(),
            reads=(SB_,), writes=(RBFb,), small=True)
    for tt in range(NTT):
        sch.step("tensor", (lambda tt=tt: (lambda e: e.matmul(
            ctx.t("pbc_m")[:, :], ctx.t("onesrow")[0:1, :],
            ctx.t("rows_bf")[0:1, 0, tt * 512:(tt + 1) * 512], start=True, stop=True)))(),
            reads=(ctx.b("rows_bf"), ctx.b("onesrow")), writes=(ctx.b("pbc_m"),))
        sch.step("tensor", (lambda tt=tt: (lambda e: e.matmul(
            ctx.t("pbc_r")[:, :], ctx.t("onesrow")[0:1, :],
            ctx.t("rows_bf")[0:1, 1, tt * 512:(tt + 1) * 512], start=True, stop=True)))(),
            reads=(ctx.b("rows_bf"), ctx.b("onesrow")), writes=(ctx.b("pbc_r"),))
        sch.step("scalar", lambda e: e.copy(ctx.t("rbs")[:, :], ctx.t("pbc_r")[:, :]),
                 reads=(ctx.b("pbc_r"),), writes=(ctx.b("rbs"),))
        for c in range(NCH):
            sl = c % 2
            sch.step("vector", (lambda c=c, tt=tt, sl=sl: (lambda e: e.tensor_sub(
                ctx.t("tsub")[:, sl, :], ctx.t("xbf")[:, c, tt * 512:(tt + 1) * 512],
                ctx.t("pbc_m")[:, :])))(),
                reads=(ctx.b("xbf_%d" % c), ctx.b("pbc_m")), writes=(ctx.b("tsub_%d" % sl),))
            sch.step("gpsimd", (lambda c=c, tt=tt, sl=sl: (lambda e: e.tensor_mul(
                ctx.t("xln")[:, c, tt * 512:(tt + 1) * 512], ctx.t("tsub")[:, sl, :],
                ctx.t("rbs")[:, :])))(),
                reads=(ctx.b("tsub_%d" % sl), ctx.b("rbs")), writes=(ctx.b("xln"),),
                tag_writes=(tt == NTT - 1 and c == NCH - 1))


def _ensure_rows_bf(ctx):
    if "rows_bf" not in ctx.sb:
        ctx.sbuf("rows_bf", [1, 2, T], BF16)


def alloc_stage_ffn1(ctx, din):
    _ensure_rows_bf(ctx)
    for nm, w in (("qth", 32), ("qwr", 32), ("qwi", 32), ("b1c", 32), ("b2c", 8)):
        t = ctx.sbuf(nm + "_s", [128, w], F32)
        ctx.sch.step("sync", (lambda t=t, nm=nm: (lambda e: e.dma_start(t[:, :], din[nm])))(),
                     writes=(ctx.b(nm + "_s"),), dma=True)
    ctx.sbuf("qt1", [128, 32], F32)
    ctx.sbuf("qt2", [128, 32], F32)
    ctx.sbuf("qsin", [128, 32], F32)
    ctx.sbuf("qcos", [128, 32], F32)
    ctx.sbuf("qA", [128, 32], F32)
    ctx.sbuf("qbeta", [128, 32], F32)
    ctx.sbuf("halfpicol", [128, 1], F32)
    ctx.sch.step("vector", lambda e: e.memset(ctx.t("halfpicol")[:, 0:1], math.pi / 2),
                 writes=(ctx.b("halfpicol"),), small=True)
    ctx.sbuf("w1s0", [128, 8, 1024], BF16)
    ctx.sbuf("w1s1", [128, 8, 1024], BF16)
    ctx.sbuf("w2s0", [128, 8, 1024], BF16)
    xbf_t = ctx.sb["xbf"]
    ctx.sb["w2s1"] = xbf_t.ap()[:, :, 0:1024]
    w2al = Buf("ffn1_w2alias")
    ctx.buf["w2s1"] = w2al
    for c in range(NCH):
        ctx.buf["xbf_%d" % c] = w2al
    ctx.sbuf("hq", [128, 8, 1024], BF16)
    ctx.sbuf("sinv", [128, 1, 512], BF16)
    ctx.sbuf("qv", [128, 1, 512], F32)
    ctx.sbuf("yf", [128, NCH, 1024], F32)
    ctx.sbuf("fprs", [128, 2, 512], F32)
    for s in range(2):
        ctx.buf["fprs_%d" % s] = Buf()
    ctx.psum("ph0", [128, 512], F32)
    ctx.psum("ph1", [128, 512], F32)
    ctx.psum("py0", [128, 512], F32)
    ctx.psum("py1", [128, 512], F32)


def _wrap_cols(ctx, outname, inname, extra_quarter=0.0):
    sch = ctx.sch
    ot, ob = ctx.t(outname), ctx.b(outname)
    it, ib = ctx.t(inname), ctx.b(inname)
    q1t, q1b = ctx.t("qt1"), ctx.b("qt1")
    sch.step("vector", lambda e: e.tensor_scalar(
        q1t[:, :], it[:, :], 1.0 / TWO_PI, MAGIC + extra_quarter,
        op0=ALU.mult, op1=ALU.add),
        reads=(ib,), writes=(q1b,), small=True)
    sch.step("vector", lambda e: e.tensor_scalar(
        q1t[:, :], q1t[:, :], MAGIC, None, op0=ALU.subtract),
        reads=(q1b,), writes=(q1b,), small=True)
    sch.step("vector", lambda e: e.scalar_tensor_tensor(
        ot[:, :], q1t[:, :], -TWO_PI, it[:, :],
        op0=ALU.mult, op1=ALU.add),
        reads=(q1b, ib), writes=(ob,), small=True)


def emit_stage_ffn1(ctx, din, xin_ap, xout_ap):
    sch = ctx.sch
    _wrap_cols(ctx, "qt2", "qth_s")
    sch.step("scalar", lambda e: e.activation(ctx.t("qsin")[:, :], ctx.t("qt2")[:, :], AF.Sin),
             reads=(ctx.b("qt2"),), writes=(ctx.b("qsin"),), small=True)
    _wrap_cols(ctx, "qt2", "qth_s", extra_quarter=0.25)
    sch.step("scalar", lambda e: e.activation(ctx.t("qcos")[:, :], ctx.t("qt2")[:, :], AF.Sin,
                                              bias=ctx.t("halfpicol")[:, 0:1], scale=1.0),
             reads=(ctx.b("qt2"), ctx.b("halfpicol")), writes=(ctx.b("qcos"),), small=True)
    sch.step("vector", lambda e: e.tensor_mul(ctx.t("qt1")[:, :], ctx.t("qcos")[:, :], ctx.t("qwr_s")[:, :]),
             reads=(ctx.b("qcos"), ctx.b("qwr_s")), writes=(ctx.b("qt1"),), small=True)
    sch.step("vector", lambda e: e.tensor_mul(ctx.t("qt2")[:, :], ctx.t("qsin")[:, :], ctx.t("qwi_s")[:, :]),
             reads=(ctx.b("qsin"), ctx.b("qwi_s")), writes=(ctx.b("qt2"),), small=True)
    sch.step("vector", lambda e: e.tensor_add(ctx.t("qt1")[:, :], ctx.t("qt1")[:, :], ctx.t("qt2")[:, :]),
             reads=(ctx.b("qt1"), ctx.b("qt2")), writes=(ctx.b("qt1"),), small=True)
    sch.step("vector", lambda e: e.tensor_mul(ctx.t("qA")[:, :], ctx.t("qt1")[:, :], ctx.t("qwi_s")[:, :]),
             reads=(ctx.b("qt1"), ctx.b("qwi_s")), writes=(ctx.b("qA"),), small=True)
    sch.step("vector", lambda e: e.tensor_scalar(ctx.t("qA")[:, :], ctx.t("qA")[:, :], 0.1, None, op0=ALU.mult),
             reads=(ctx.b("qA"),), writes=(ctx.b("qA"),), small=True)
    sch.step("vector", lambda e: e.scalar_tensor_tensor(
        ctx.t("qt2")[:, :], ctx.t("b1c_s")[:, :], 0.1, ctx.t("qth_s")[:, :], op0=ALU.mult, op1=ALU.add),
        reads=(ctx.b("b1c_s"), ctx.b("qth_s")), writes=(ctx.b("qt2"),), small=True)
    ctx.sb["qarg"] = ctx.sb["qt2"]
    ctx.buf["qarg"] = ctx.buf["qt2"]
    _wrap_cols(ctx, "qbeta", "qarg")

    for seq in range(SEQS):
        emit_ln(ctx, xin_ap, seq)
        for th in range(2):
            tho = th * 1024
            def f1_dmas(qq):
                ws = qq % 2
                sch.step("gpsimd", (lambda qq=qq, ws=ws: (lambda e: e.dma_start(
                    ctx.t("w1s%d" % ws)[:, :, :],
                    din["W1c"][:, :, qq * 1024:(qq + 1) * 1024].rearrange("c p o -> p c o"))))(),
                    writes=(ctx.b("w1s%d" % ws),), dma=True)
                sch.step("gpsimd", (lambda qq=qq, ws=ws: (lambda e: e.dma_start(
                    ctx.t("w2s%d" % ws)[:, :, :],
                    din["W2c"][qq * 8:(qq + 1) * 8, :, :].rearrange("c p o -> p c o"))))(),
                    writes=(ctx.b("w2s%d" % ws),), dma=True)
            f1_dmas(0)
            for qtr in range(4):
                wslot = qtr % 2
                w1b, w2b = ctx.b("w1s%d" % wslot), ctx.b("w2s%d" % wslot)
                if qtr < 3:
                    f1_dmas(qtr + 1)
                for fc in range(8):
                    gfc = qtr * 8 + fc
                    pb = "ph%d" % (fc % 2)
                    for tt2 in range(2):
                        tsl = slice(tho + tt2 * 512, tho + (tt2 + 1) * 512)
                        sl = tt2
                        fns = []
                        for dc in range(NCH):
                            fns.append((lambda dc=dc, fc=fc, tsl=tsl, pb=pb, wslot=wslot: (lambda e: e.matmul(
                                ctx.t(pb)[:, :], ctx.t("w1s%d" % wslot)[:, dc, fc * 128:(fc + 1) * 128],
                                ctx.t("xln")[:, dc, tsl],
                                start=(dc == 0), stop=(dc == NCH - 1))))())
                        sch.step("tensor", fns, reads=(w1b, ctx.b("xln")), writes=(ctx.b(pb),))
                        sch.step("scalar", (lambda pb=pb, gfc=gfc: (lambda e: e.activation(
                            ctx.t("sinv")[:, 0, :], ctx.t(pb)[:, :], AF.Sin,
                            bias=ctx.t("qbeta")[:, gfc:gfc + 1], scale=0.1)))(),
                            reads=(ctx.b(pb), ctx.b("qbeta")), writes=(ctx.b("sinv"),))
                        sch.step("vector", (lambda pb=pb, gfc=gfc: (lambda e: e.scalar_tensor_tensor(
                            ctx.t("qv")[:, 0, :], ctx.t("sinv")[:, 0, :], ctx.t("qA")[:, gfc:gfc + 1],
                            ctx.t(pb)[:, :], op0=ALU.mult, op1=ALU.add)))(),
                            reads=(ctx.b("sinv"), ctx.b(pb), ctx.b("qA")), writes=(ctx.b("qv"),))
                        sch.step("gpsimd", (lambda gfc=gfc, fc=fc, tt2=tt2: (lambda e: e.tensor_scalar(
                            ctx.t("hq")[:, fc, tt2 * 512:(tt2 + 1) * 512], ctx.t("qv")[:, 0, :],
                            ctx.t("b1c_s")[:, gfc:gfc + 1], 0.0, op0=ALU.add, op1=ALU.max)))(),
                            reads=(ctx.b("qv"), ctx.b("b1c_s")), writes=(ctx.b("hq"),),
                            tag_writes=(fc == 7 and tt2 == 1))
                for oc in range(NCH):
                    for tt2 in range(2):
                        pb = "py%d" % ((oc * 2 + tt2) % 2)
                        fns = []
                        for fc in range(8):
                            fns.append((lambda fc=fc, oc=oc, tt2=tt2, pb=pb, wslot=wslot: (lambda e: e.matmul(
                                ctx.t(pb)[:, :], ctx.t("w2s%d" % wslot)[:, fc, oc * 128:(oc + 1) * 128],
                                ctx.t("hq")[:, fc, tt2 * 512:(tt2 + 1) * 512],
                                start=(fc == 0), stop=(fc == 7))))())
                        sch.step("tensor", fns, reads=(w2b, ctx.b("hq")), writes=(ctx.b(pb),))
                        ysl = (lambda oc=oc, tt2=tt2: ctx.t("yf")[:, oc, tt2 * 512:(tt2 + 1) * 512])
                        if qtr == 0:
                            sch.step("scalar", (lambda pb=pb, oc=oc, tt2=tt2: (lambda e: e.copy(
                                ctx.t("yf")[:, oc, tt2 * 512:(tt2 + 1) * 512], ctx.t(pb)[:, :])))(),
                                reads=(ctx.b(pb),), writes=(ctx.b("yf"),), tag_writes=(oc == NCH - 1 and tt2 == 1))
                        elif qtr < 3:
                            sch.step("vector", (lambda pb=pb, oc=oc, tt2=tt2: (lambda e: e.tensor_add(
                                ctx.t("yf")[:, oc, tt2 * 512:(tt2 + 1) * 512], ctx.t(pb)[:, :],
                                ctx.t("yf")[:, oc, tt2 * 512:(tt2 + 1) * 512])))(),
                                reads=(ctx.b(pb), ctx.b("yf")), writes=(ctx.b("yf"),), tag_writes=(oc == NCH - 1 and tt2 == 1))
                        else:
                            sch.step("vector", (lambda pb=pb, oc=oc, tt2=tt2: (lambda e: e.scalar_tensor_tensor(
                                ctx.t("yf")[:, oc, tt2 * 512:(tt2 + 1) * 512], ctx.t(pb)[:, :],
                                ctx.t("b2c_s")[:, oc:oc + 1],
                                ctx.t("yf")[:, oc, tt2 * 512:(tt2 + 1) * 512], op0=ALU.add, op1=ALU.add)))(),
                                reads=(ctx.b(pb), ctx.b("yf"), ctx.b("b2c_s")), writes=(ctx.b("yf"),),
                                tag_writes=(oc == NCH - 1 and tt2 == 1))
            for oc in range(NCH):
                for t2 in range(2):
                    sl = t2
                    sch.step("sync", (lambda oc=oc, seq=seq, sl=sl, tho=tho, t2=t2: (lambda e: e.dma_start(
                        ctx.t("xio")[:, sl, :],
                        xin_ap[seq, oc * 128:(oc + 1) * 128, tho + t2 * 512:tho + (t2 + 1) * 512])))(),
                        writes=(ctx.b("xio_%d" % sl),), dma=True)
                    sch.step("vector", (lambda oc=oc, sl=sl, t2=t2: (lambda e: e.tensor_add(
                        ctx.t("fprs")[:, sl, :],
                        ctx.t("yf")[:, oc, t2 * 512:(t2 + 1) * 512], ctx.t("xio")[:, sl, :])))(),
                        reads=(ctx.b("xio_%d" % sl), ctx.b("yf")), writes=(ctx.b("fprs_%d" % sl),))
                    sch.step("sync", (lambda oc=oc, seq=seq, tho=tho, t2=t2, sl=sl: (lambda e: e.dma_start(
                        xout_ap[seq, oc * 128:(oc + 1) * 128, tho + t2 * 512:tho + (t2 + 1) * 512],
                        ctx.t("fprs")[:, sl, :])))(),
                        reads=(ctx.b("fprs_%d" % sl),), writes=(), dma=True)


STAGE_INPUTS = {"ffn1": ["W1c", "W2c", "b1c", "b2c", "qth", "qwr", "qwi"]}
STAGE_ALLOC = {"ffn1": alloc_stage_ffn1}
STAGE_EMIT = {"ffn1": emit_stage_ffn1}


def build_program(stages, dshapes):
    nc = bass.Bass("TRN2", target_bir_lowering=False)
    din = {}
    din["xin"] = nc.dram_tensor("xin", [SEQS, D, T], F32, kind="ExternalInput").ap()
    xout = nc.dram_tensor("xout", [SEQS, D, T], F32, kind="ExternalOutput").ap()
    needed = set()
    for s in stages:
        needed.update(STAGE_INPUTS[s])
    for nm in sorted(needed):
        shape, dtype = dshapes[nm]
        din[nm] = nc.dram_tensor(nm, list(shape), dtype, kind="ExternalInput").ap()
    xmid = []
    for i in range(min(max(0, len(stages) - 1), 2)):
        xmid.append(nc.dram_tensor("xmid%d" % i, [SEQS, D, T], F32).ap())

    sch = Sched()
    semnames = sorted(set(ENG_SEM.values()) | set(DMA_LANE_SEMS))
    with ExitStack() as stack:
        ctx = Ctx(nc, sch, stack)
        alloc_common(ctx)
        for s in range(2):
            ctx.buf["tsub_%d" % s] = Buf()
        for c in range(NCH):
            ctx.buf["yf_out_%d" % c] = Buf()
        emit_init(ctx)
        cur_in = din["xin"]
        for i, s in enumerate(stages):
            cur_out = xout if i == len(stages) - 1 else xmid[i % 2]
            ctx.activate(i)
            sch.cur_inst = i
            with ExitStack() as sstack:
                old_stack, ctx.stack = ctx.stack, sstack
                STAGE_ALLOC[s](ctx, din)
                STAGE_EMIT[s](ctx, din, cur_in, cur_out)
                ctx.stack = old_stack
            sch.cur_inst = -1
            ctx.activate(-1)
            sch.full_barrier()
            cur_in = cur_out
        fw = [(s, sch.count[s]) for s in DMA_LANE_SEMS if sch.count.get(s, 0)]
        sch.cmds["sync"].append((fw, [], None, -1))
        sems = {}
        for nm in semnames:
            sems[nm] = stack.enter_context(nc.semaphore(nm))
        sch.finalize(nc, sems, ctx=ctx)
    return nc


def input_shapes(prepped):
    return {k: (v.shape, F32) for k, v in prepped.items()}


def run_stages(stages, inputs, x_np=None, per_core=None):
    if per_core is None:
        per_core = [prep_inputs(inputs, core) for core in range(NCORES)]
    if x_np is not None:
        for core in range(NCORES):
            xs = x_np[core * SEQS:(core + 1) * SEQS]
            per_core[core]["xin"] = np.ascontiguousarray(xs.transpose(0, 2, 1)).astype(np.float32)
    dshapes = input_shapes(per_core[0])
    nc = build_program(stages, dshapes)
    needed = set(["xin"])
    for s in stages:
        needed.update(STAGE_INPUTS[s])
    in_maps = [{k: v for k, v in d.items() if k in needed} for d in per_core]
    res = run_bass_kernel_spmd(nc, in_maps, core_ids=list(range(NCORES)))
    outs = []
    for r in res.results:
        outs.append(np.ascontiguousarray(r["xout"].transpose(0, 2, 1)))
    return np.concatenate(outs, axis=0).astype(np.float32)


ORDER = ["attn_a0", "moe", "attn_b0", "moe", "attn_a1", "ffn1", "attn_b1", "ffn1"]


def kernel(**inputs):
    return run_stages(ORDER, inputs)


def alloc_stage_attn(ctx, din, cfg):
    p = cfg["prefix"]
    sch = ctx.sch
    ctx.sbuf(p + "V65", [128, 16, 16 * 65], BF16)
    for sl in range(2):
        ctx.sbuf(p + "QA%d" % sl, [65, T], BF16)
        ctx.sbuf(p + "KA%d" % sl, [65, T], BF16)
    ctx.sbuf(p + "NST", [33, 2, T], BF16)
    if "rows_bf" not in ctx.sb:
        ctx.sb["rows_bf"] = ctx.sb[p + "NST"]
        ctx.buf["rows_bf"] = ctx.buf[p + "NST"]
    ctx.sbuf(p + "KM", [1, 64], F32)
    ctx.sbuf(p + "sqn", [128, 512], BF16)
    ctx.sbuf(p + "pt", [128, 2, 512], BF16)
    ctx.sbuf(p + "prs", [128, 2, 512], F32)
    ctx.sbuf(p + "nbs", [64, 2, 512], BF16)
    ctx.sbuf(p + "rrow", [1, 2, 512], BF16)
    ctx.sbuf(p + "wqo", [128, 2, NCH, 4, 128], BF16)
    ctx.sbuf(p + "wsA", [128, 8, 1024], BF16)
    t = ctx.sbuf(p + "tril", [128, 128], BF16)
    sch.step("gpsimd", (lambda t=t: (lambda e: e.dma_start(t[:, :], din["tril"])))(),
             writes=(ctx.b(p + "tril"),), dma=True)
    if cfg.get("rope"):
        ctx.sb["_din_theta64"] = din["theta64"]
        ctx.sbuf(p + "Atab", [128, T], BF16)
        ctx.sbuf(p + "Btab", [128, T], BF16)
        ctx.sbuf(p + "Wth", [128, T], F32)
        t = ctx.sbuf(p + "sgn", [128, 2], F32)
        sch.step("sync", (lambda t=t: (lambda e: e.dma_start(t[:, :], din["sign128"])))(),
                 writes=(ctx.b(p + "sgn"),), dma=True)
        ctx.sbuf(p + "SCf", [128, 16], F32)
        t = ctx.sbuf(p + "lsrow", [1, 16], F32)
        sch.step("sync", (lambda t=t: (lambda e: e.dma_start(t[0:1, :], din[cfg["logscale"]])))(),
                 writes=(ctx.b(p + "lsrow"),), dma=True)
        ctx.sbuf(p + "onesf", [1, 128], F32)
        sch.step("vector", lambda e: e.memset(ctx.t(p + "onesf")[0:1, :], 1.0),
                 writes=(ctx.b(p + "onesf"),), small=True)
        ctx.sbuf(p + "hpi", [128, 1], F32)
        sch.step("vector", lambda e: e.memset(ctx.t(p + "hpi")[:, 0:1], math.pi / 2),
                 writes=(ctx.b(p + "hpi"),), small=True)
        ctx.sbuf(p + "e1", [128, 512], BF16)
        ctx.sbuf(p + "e2", [128, 512], BF16)
    if cfg.get("bias"):
        t = ctx.sbuf(p + "bia", [128, 8], F32)
        sch.step("sync", (lambda t=t: (lambda e: e.dma_start(t[:, :], din[cfg["bias"]])))(),
                 writes=(ctx.b(p + "bia"),), dma=True)
    ctx.sb[p + "OUT8"] = ctx.sb["xbf"]
    shared = Buf(p + "out8_alias")
    ctx.buf[p + "OUT8"] = shared
    for c in range(NCH):
        ctx.buf["xbf_%d" % c] = shared
    ctx.psum(p + "pA", [128, 512], F32)
    ctx.psum(p + "pB", [128, 512], F32)
    ctx.psum(p + "pC", [128, 512], F32)
    ctx.psum(p + "pD", [128, 512], F32)


def _recip_bf(ctx, e, out_ap, in_ap):
    with ctx.nc.allow_low_precision(reason="bf16 softmax normalizer; bf16 blessed here"):
        return e.reciprocal(out_ap, in_ap)


def _rope_tables(ctx, p, oc):
    sch = ctx.sch
    A, B = ctx.t(p + "Atab"), ctx.t(p + "Btab")
    Ab, Bb = ctx.b(p + "Atab"), ctx.b(p + "Btab")
    W, Wb = ctx.t(p + "Wth"), ctx.b(p + "Wth")
    SC, SCb = ctx.t(p + "SCf"), ctx.b(p + "SCf")
    PR, PRb = ctx.t(p + "prs"), ctx.b(p + "prs")
    din = ctx.sb["_din_theta64"]
    for r in range(2):
        sch.step("sync", (lambda r=r: (lambda e: e.dma_start(W[64 * r:64 * r + 64, :], din)))(),
                 writes=(Wb,), dma=True, tag_writes=(r == 1))
    for r in range(2):
        sch.step("vector", (lambda r=r, oc=oc: (lambda e: e.tensor_scalar(
            W[64 * r:64 * r + 64, :], W[64 * r:64 * r + 64, :],
            SC[0:64, 2 * oc + r:2 * oc + r + 1], None, op0=ALU.mult)))(),
            reads=(SCb,), writes=(Wb,), tag_writes=(r == 1))
    for tt in range(NTT):
        ws = (lambda tt=tt: W[:, tt * 512:(tt + 1) * 512])()
        p0, p1 = PR[:, 0, :], PR[:, 1, :]
        sch.step("vector", (lambda ws=ws, p0=p0: (lambda e: e.tensor_scalar(
            p0, ws, 1.0 / TWO_PI, MAGIC, op0=ALU.mult, op1=ALU.add)))(),
            reads=(Wb,), writes=(PRb,))
        sch.step("vector", (lambda p0=p0: (lambda e: e.tensor_scalar(
            p0, p0, MAGIC, None, op0=ALU.subtract)))(),
            reads=(PRb,), writes=(PRb,))
        sch.step("vector", (lambda ws=ws, p0=p0, p1=p1: (lambda e: e.scalar_tensor_tensor(
            p1, p0, -TWO_PI, ws, op0=ALU.mult, op1=ALU.add)))(),
            reads=(PRb, Wb), writes=(PRb,))
        sch.step("vector", (lambda p0=p0, p1=p1: (lambda e: e.tensor_scalar(
            p0, p1, math.pi / 2, None, op0=ALU.is_gt)))(),
            reads=(PRb,), writes=(PRb,))
        sch.step("vector", (lambda p0=p0, p1=p1: (lambda e: e.scalar_tensor_tensor(
            p0, p0, -TWO_PI, p1, op0=ALU.mult, op1=ALU.add)))(),
            reads=(PRb,), writes=(PRb,))
        sch.step("scalar", (lambda p0=p0, tt=tt: (lambda e: e.activation(
            A[:, tt * 512:(tt + 1) * 512], p0, AF.Sin,
            bias=ctx.t(p + "hpi")[:, 0:1], scale=1.0)))(),
            reads=(PRb, ctx.b(p + "hpi")), writes=(Ab,), tag_writes=(tt == NTT - 1))
        sch.step("scalar", (lambda p0=p0, p1=p1: (lambda e: e.activation(
            p0, p1, AF.Sin)))(),
            reads=(PRb,), writes=(PRb,))
        sch.step("vector", (lambda p0=p0, tt=tt: (lambda e: e.tensor_scalar(
            B[:, tt * 512:(tt + 1) * 512], p0, ctx.t(p + "sgn")[:, 0:1], None, op0=ALU.mult)))(),
            reads=(PRb, ctx.b(p + "sgn")), writes=(Bb,), tag_writes=(tt == NTT - 1))


def _emit_vphase(ctx, din, p, wv_name):
    sch = ctx.sch
    V65 = ctx.t(p + "V65")
    V65b = ctx.b(p + "V65")
    V65r = V65.ap().rearrange("p t (h c) -> p t h c", h=16)
    pA, pB = ctx.t(p + "pA"), ctx.t(p + "pB")
    pAb, pBb = ctx.b(p + "pA"), ctx.b(p + "pB")
    sch.step("gpsimd", lambda e: e.dma_start(
        ctx.t(p + "wsA")[:, :, :], din[wv_name].rearrange("c p o -> p c o")),
        writes=(ctx.b(p + "wsA"),), dma=True)
    sch.step("vector", lambda e: e.memset(V65r[:, :, :, 64:65], 1.0), writes=(V65b,), small=True)
    for tk in range(16):
        for o2 in range(2):
            pb, pbb = (pA, pAb) if (tk * 2 + o2) % 2 == 0 else (pB, pBb)
            fns = []
            for dc in range(NCH):
                fns.append((lambda dc=dc, tk=tk, o2=o2, pb=pb: (lambda e: e.matmul(
                    pb[:, :], ctx.t("xln")[:, dc, tk * 128:(tk + 1) * 128],
                    ctx.t(p + "wsA")[:, dc, o2 * 512:(o2 + 1) * 512],
                    start=(dc == 0), stop=(dc == NCH - 1))))())
            sch.step("tensor", fns, reads=(ctx.b("xln"), ctx.b(p + "wsA")), writes=(pbb,))
            sch.step("vector", (lambda tk=tk, o2=o2, pb=pb: (lambda e: e.tensor_copy(
                V65r[:, tk, o2 * 8:(o2 + 1) * 8, 0:64], pb[:, :])))(),
                reads=(pbb,), writes=(V65b,), tag_writes=(tk == 15 and o2 == 1))


def emit_attn(ctx, din, xin_ap, xout_ap, cfg):
    sch = ctx.sch
    p = cfg["prefix"]
    rope = cfg.get("rope")
    causal = cfg["causal"]
    scale = cfg["scale"]
    V65 = ctx.t(p + "V65")
    V65b = ctx.b(p + "V65")
    V65r = V65.ap().rearrange("p t (h c) -> p t h c", h=16)
    pA, pB, pC, pD = (ctx.t(p + x) for x in ("pA", "pB", "pC", "pD"))
    pAb, pBb, pCb, pDb = (ctx.b(p + x) for x in ("pA", "pB", "pC", "pD"))

    for sl in range(2):
        sch.step("vector", (lambda sl=sl: (lambda e: e.memset(ctx.t(p + "KA%d" % sl)[64:65, :], 1.0)))(),
                 writes=(ctx.b(p + "KA%d" % sl),))
    if rope:
        sch.step("scalar", lambda e: e.activation(ctx.t(p + "lsrow")[0:1, :], ctx.t(p + "lsrow")[0:1, :], AF.Exp),
                 reads=(ctx.b(p + "lsrow"),), writes=(ctx.b(p + "lsrow"),), small=True)
        sch.step("tensor", lambda e: e.matmul(pD[:, 0:16], ctx.t(p + "onesf")[0:1, :],
                                              ctx.t(p + "lsrow")[0:1, :], start=True, stop=True),
                 reads=(ctx.b(p + "lsrow"), ctx.b(p + "onesf")), writes=(pDb,))
        sch.step("vector", lambda e: e.tensor_copy(ctx.t(p + "SCf")[:, :], pD[:, 0:16]),
                 reads=(pDb,), writes=(ctx.b(p + "SCf"),), small=True)

    for seq in range(SEQS):
        emit_ln(ctx, xin_ap, seq)
        _emit_vphase(ctx, din, p, cfg["wv"])
        for oc in range(NCH):
            wsl = oc % 2
            names = ([cfg["wq"], cfg["wq_sw"], cfg["wk"], cfg["wk_sw"]] if rope
                     else [cfg["wq"], cfg["wk"]])
            for wi, wn in enumerate(names):
                sch.step("gpsimd", (lambda wn=wn, wi=wi, oc=oc, wsl=wsl: (lambda e: e.dma_start(
                    ctx.t(p + "wqo")[:, wsl, :, wi, :],
                    din[wn][:, :, oc * 128:(oc + 1) * 128].rearrange("c p o -> p c o"))))(),
                    writes=(ctx.b(p + "wqo"),), dma=True, tag_writes=(wi == len(names) - 1))
            if rope:
                _rope_tables(ctx, p, oc)
            for phase in range(2):
                base_wi = (2 if rope else 1) * phase
                dname = (p + "QA%d") if phase == 0 else (p + "KA%d")
                for tt in range(NTT):
                    fns = []
                    for dc in range(NCH):
                        fns.append((lambda dc=dc, tt=tt, wi=base_wi, wsl=wsl: (lambda e: e.matmul(
                            pA[:, :], ctx.t(p + "wqo")[:, wsl, dc, wi, :],
                            ctx.t("xln")[:, dc, tt * 512:(tt + 1) * 512],
                            start=(dc == 0), stop=(dc == NCH - 1))))())
                    sch.step("tensor", fns, reads=(ctx.b(p + "wqo"), ctx.b("xln")), writes=(pAb,))
                    if rope:
                        fns = []
                        for dc in range(NCH):
                            fns.append((lambda dc=dc, tt=tt, wi=base_wi + 1, wsl=wsl: (lambda e: e.matmul(
                                pB[:, :], ctx.t(p + "wqo")[:, wsl, dc, wi, :],
                                ctx.t("xln")[:, dc, tt * 512:(tt + 1) * 512],
                                start=(dc == 0), stop=(dc == NCH - 1))))())
                        sch.step("tensor", fns, reads=(ctx.b(p + "wqo"), ctx.b("xln")), writes=(pBb,))
                    sch.step("scalar", lambda e: e.activation(ctx.t(p + "sqn")[:, :], pA[:, :], AF.Square),
                             reads=(pAb,), writes=(ctx.b(p + "sqn"),))
                    for r in range(2):
                        sch.step("tensor", (lambda r=r: (lambda e: e.matmul(
                            pD[32 * r:32 * r + 1, :], ctx.t("onesb")[64 * r:64 * r + 64, 0:1],
                            ctx.t(p + "sqn")[64 * r:64 * r + 64, :], start=True, stop=True)))(),
                            reads=(ctx.b(p + "sqn"), ctx.b("onesb")), writes=(pDb,), tag_writes=(r == 1))
                    for r in range(2):
                        sch.step("scalar", (lambda r=r, tt=tt, phase=phase: (lambda e: e.copy(
                            ctx.t(p + "NST")[32 * r:32 * r + 1, phase, tt * 512:(tt + 1) * 512],
                            pD[32 * r:32 * r + 1, :])))(),
                            reads=(pDb,), writes=(ctx.b(p + "NST"),), tag_writes=(r == 1))
                    if rope:
                        sch.step("vector", (lambda tt=tt: (lambda e: e.tensor_mul(
                            ctx.t(p + "e1")[:, :], pA[:, :], ctx.t(p + "Atab")[:, tt * 512:(tt + 1) * 512])))(),
                            reads=(pAb, ctx.b(p + "Atab")), writes=(ctx.b(p + "e1"),))
                        sch.step("vector", (lambda tt=tt: (lambda e: e.tensor_mul(
                            ctx.t(p + "e2")[:, :], pB[:, :], ctx.t(p + "Btab")[:, tt * 512:(tt + 1) * 512])))(),
                            reads=(pBb, ctx.b(p + "Btab")), writes=(ctx.b(p + "e2"),))
                        for r in range(2):
                            sch.step("vector", (lambda r=r, tt=tt, dname=dname: (lambda e: e.tensor_add(
                                ctx.t(dname % r)[0:64, tt * 512:(tt + 1) * 512],
                                ctx.t(p + "e1")[64 * r:64 * r + 64, :],
                                ctx.t(p + "e2")[64 * r:64 * r + 64, :])))(),
                                reads=(ctx.b(p + "e1"), ctx.b(p + "e2")), writes=(ctx.b(dname % r),))
                    else:
                        for r in range(2):
                            sch.step("scalar", (lambda r=r, tt=tt, dname=dname: (lambda e: e.copy(
                                ctx.t(dname % r)[0:64, tt * 512:(tt + 1) * 512],
                                pA[64 * r:64 * r + 64, :])))(),
                                reads=(pAb,), writes=(ctx.b(dname % r),))
            for r in range(2):
                rb = 32 * r
                j = oc * 2 + r
                sch.step("scalar", (lambda rb=rb: (lambda e: e.activation(
                    ctx.t(p + "NST")[rb:rb + 1, 0, :], ctx.t(p + "NST")[rb:rb + 1, 0, :], AF.Sqrt)))(),
                    reads=(ctx.b(p + "NST"),), writes=(ctx.b(p + "NST"),))
                sch.step("vector", (lambda rb=rb, j=j: (lambda e: e.reduce_max(
                    ctx.t(p + "KM")[0:1, j:j + 1], ctx.t(p + "NST")[rb:rb + 1, 1, :], axis=AX.X)))(),
                    reads=(ctx.b(p + "NST"),), writes=(ctx.b(p + "KM"),), small=True)
                sch.step("scalar", (lambda j=j: (lambda e: e.activation(
                    ctx.t(p + "KM")[0:1, j + 32:j + 33], ctx.t(p + "KM")[0:1, j:j + 1], AF.Sqrt)))(),
                    reads=(ctx.b(p + "KM"),), writes=(ctx.b(p + "KM"),), small=True)
                sch.step("vector", (lambda rb=rb, j=j, r=r: (lambda e: e.tensor_scalar(
                    ctx.t(p + "QA%d" % r)[64:65, :], ctx.t(p + "NST")[rb:rb + 1, 0, :],
                    ctx.t(p + "KM")[0:1, j + 32:j + 33], -1.0, op0=ALU.mult, op1=ALU.mult)))(),
                    reads=(ctx.b(p + "NST"), ctx.b(p + "KM")), writes=(ctx.b(p + "QA%d" % r),))
            for qt in range(NTT):
                kts = list(range(4 * qt + 4 if causal else 16))
                for ki, kt in enumerate(kts):
                    off = max(0, kt * 128 - qt * 512) if causal else 0
                    for r in range(2):
                        h = 2 * oc + r
                        QA, QAb = ctx.t(p + "QA%d" % r), ctx.b(p + "QA%d" % r)
                        KA, KAb = ctx.t(p + "KA%d" % r), ctx.b(p + "KA%d" % r)
                        psb, psbb = (pA, pAb) if r == 0 else (pB, pBb)
                        pvp, pvpb = (pC, pCb) if r == 0 else (pD, pDb)
                        sch.step("tensor", (lambda psb=psb, kt=kt, qt=qt, off=off, QA=QA, KA=KA: (lambda e: e.matmul(
                            psb[:, off:512], KA[:, kt * 128:(kt + 1) * 128],
                            QA[:, qt * 512 + off:(qt + 1) * 512], start=True, stop=True)))(),
                            reads=(QAb, KAb), writes=(psbb,))
                        sch.step("scalar", (lambda psb=psb, r=r, off=off: (lambda e: e.activation(
                            ctx.t(p + "pt")[:, r, off:512], psb[:, off:512], AF.Exp, scale=scale)))(),
                            reads=(psbb,), writes=(ctx.b(p + "pt"),))
                        if causal and kt >= 4 * qt:
                            sch.step("vector", (lambda r=r, off=off: (lambda e: e.tensor_mul(
                                ctx.t(p + "pt")[:, r, off:off + 128],
                                ctx.t(p + "pt")[:, r, off:off + 128],
                                ctx.t(p + "tril")[:, :])))(),
                                reads=(ctx.b(p + "pt"), ctx.b(p + "tril")), writes=(ctx.b(p + "pt"),))
                        sch.step("tensor", (lambda kt=kt, r=r, off=off, h=h, pvp=pvp, first=(ki == 0), last=(ki == len(kts) - 1): (lambda e: e.matmul(
                            pvp[0:65, off:512], V65[:, kt, h * 65:h * 65 + 65],
                            ctx.t(p + "pt")[:, r, off:512], start=first, stop=last)))(),
                            reads=(V65b, ctx.b(p + "pt")), writes=(pvpb,), tag_writes=(ki == len(kts) - 1))
                for r in range(2):
                    h = 2 * oc + r
                    pvp, pvpb = (pC, pCb) if r == 0 else (pD, pDb)
                    sch.step("vector", (lambda pvp=pvp, r=r: (lambda e: _recip_bf(
                        ctx, e, ctx.t(p + "rrow")[0:1, r, :], pvp[64:65, :])))(),
                        reads=(pvpb,), writes=(ctx.b(p + "rrow"),))
                    sch.step("tensor", (lambda r=r: (lambda e: e.matmul(
                        pA[0:64, :], ctx.t("onesrow")[0:1, 0:64],
                        ctx.t(p + "rrow")[0:1, r, :], start=True, stop=True)))(),
                        reads=(ctx.b(p + "rrow"), ctx.b("onesrow")), writes=(pAb,))
                    sch.step("scalar", (lambda r=r: (lambda e: e.copy(
                        ctx.t(p + "nbs")[:, r, :], pA[0:64, :])))(),
                        reads=(pAb,), writes=(ctx.b(p + "nbs"),))
                    sch.step("vector", (lambda h=h, qt=qt, pvp=pvp, pvpb=pvpb, r=r: (lambda e: e.tensor_mul(
                        ctx.t(p + "OUT8")[(h % 2) * 64:(h % 2) * 64 + 64, h // 2, qt * 512:(qt + 1) * 512],
                        pvp[0:64, :], ctx.t(p + "nbs")[:, r, :])))(),
                        reads=(pvpb, ctx.b(p + "nbs")), writes=(ctx.b(p + "OUT8"),),
                        tag_writes=(qt == NTT - 1 and oc == NCH - 1 and r == 1))
        _emit_proj(ctx, din, p, xin_ap, xout_ap, seq, cfg["wo"], cfg.get("bias"))


def _emit_proj(ctx, din, p, xin_ap, xout_ap, seq, wo_name, bias):
    sch = ctx.sch
    second = (p + "pB") if (p + "pB") in ctx.sb else (p + "pC")
    pA, pB = ctx.t(p + "pA"), ctx.t(second)
    pAb, pBb = ctx.b(p + "pA"), ctx.b(second)
    if True:
        sch.step("gpsimd", lambda e: e.dma_start(
            ctx.t(p + "wsA")[:, :, :], din[wo_name].rearrange("c p o -> p c o")),
            writes=(ctx.b(p + "wsA"),), dma=True)
        for oc2 in range(NCH):
            for tt in range(NTT):
                sl = (oc2 * NTT + tt) % 2
                sch.step("sync", (lambda oc2=oc2, seq=seq, sl=sl, tt=tt: (lambda e: e.dma_start(
                    ctx.t("xio")[:, sl, :],
                    xin_ap[seq, oc2 * 128:(oc2 + 1) * 128, tt * 512:(tt + 1) * 512])))(),
                    writes=(ctx.b("xio_%d" % sl),), dma=True)
                pb, pbb = (pA, pAb) if tt % 2 == 0 else (pB, pBb)
                fns = []
                for c in range(NCH):
                    fns.append((lambda c=c, oc2=oc2, tt=tt, pb=pb: (lambda e: e.matmul(
                        pb[:, :], ctx.t(p + "wsA")[:, c, oc2 * 128:(oc2 + 1) * 128],
                        ctx.t(p + "OUT8")[:, c, tt * 512:(tt + 1) * 512],
                        start=(c == 0), stop=(c == NCH - 1))))())
                sch.step("tensor", fns, reads=(ctx.b(p + "wsA"), ctx.b(p + "OUT8")), writes=(pbb,))
                psl = tt % 2
                if bias:
                    sch.step("vector", (lambda pb=pb, oc2=oc2, tt=tt, sl=sl, psl=psl: (lambda e: e.scalar_tensor_tensor(
                        ctx.t(p + "prs")[:, psl, :], pb[:, :], ctx.t(p + "bia")[:, oc2:oc2 + 1],
                        ctx.t("xio")[:, sl, :], op0=ALU.add, op1=ALU.add)))(),
                        reads=(pbb, ctx.b(p + "bia"), ctx.b("xio_%d" % sl)), writes=(ctx.b(p + "prs"),))
                else:
                    sch.step("vector", (lambda pb=pb, oc2=oc2, tt=tt, sl=sl, psl=psl: (lambda e: e.tensor_add(
                        ctx.t(p + "prs")[:, psl, :], pb[:, :],
                        ctx.t("xio")[:, sl, :])))(),
                        reads=(pbb, ctx.b("xio_%d" % sl)), writes=(ctx.b(p + "prs"),))
                sch.step("sync", (lambda oc2=oc2, tt=tt, seq=seq, psl=psl: (lambda e: e.dma_start(
                    xout_ap[seq, oc2 * 128:(oc2 + 1) * 128, tt * 512:(tt + 1) * 512],
                    ctx.t(p + "prs")[:, psl, :])))(),
                    reads=(ctx.b(p + "prs"),), writes=(), dma=True)


CFG_ATTN_A0 = dict(prefix="a0", rope=True, causal=True, scale=0.125,
                   wq="s1_wq", wq_sw="s1_wq_sw", wk="s1_wk", wk_sw="s1_wk_sw",
                   wv="s1_wv", wo="s1_wo", logscale="s1_logscale", bias=None)
CFG_ATTN_B1 = dict(prefix="b1", rope=False, causal=False, scale=1.0 / 32.0,
                   wq="s7_wq", wk="s7_wk", wv="s7_wv", wo="s7_wu", bias="s7_bu")

STAGE_INPUTS["attn_a0"] = ["s1_wq", "s1_wq_sw", "s1_wk", "s1_wk_sw", "s1_wv", "s1_wo",
                           "s1_logscale", "theta64", "sign128", "tril"]
STAGE_INPUTS["attn_b1"] = ["s7_wq", "s7_wk", "s7_wv", "s7_wu", "s7_bu", "tril"]
STAGE_ALLOC["attn_a0"] = lambda ctx, din: alloc_stage_attn(ctx, din, CFG_ATTN_A0)
STAGE_ALLOC["attn_b1"] = lambda ctx, din: alloc_stage_attn(ctx, din, CFG_ATTN_B1)
STAGE_EMIT["attn_a0"] = lambda ctx, din, xi, xo: emit_attn(ctx, din, xi, xo, CFG_ATTN_A0)
STAGE_EMIT["attn_b1"] = lambda ctx, din, xi, xo: emit_attn(ctx, din, xi, xo, CFG_ATTN_B1)


def alloc_stage_attn_b0(ctx, din):
    p = "b0"
    sch = ctx.sch
    ctx.sbuf(p + "V65", [128, 16, 16 * 65], BF16)
    for j in range(4):
        ctx.sbuf(p + "QA%d" % j, [33, T], BF16)
        ctx.sbuf(p + "KA%d" % j, [33, T], BF16)
    ctx.sbuf(p + "NST", [33, 2, T], BF16)
    if "rows_bf" not in ctx.sb:
        ctx.sb["rows_bf"] = ctx.sb[p + "NST"]
        ctx.buf["rows_bf"] = ctx.buf[p + "NST"]
    ctx.sbuf(p + "KM", [1, 64], F32)
    ctx.sbuf(p + "sqn", [128, 512], BF16)
    ctx.sbuf(p + "pt", [128, 2, 512], BF16)
    ctx.sbuf(p + "prs", [128, 2, 512], F32)
    ctx.sbuf(p + "nbs", [64, 3, 512], BF16)
    ctx.sbuf(p + "rrow", [1, 3, 512], BF16)
    ctx.sbuf(p + "srw", [1, 512], F32)
    wp = ctx.sbuf(p + "wpool", [128, 8192], BF16)
    ctx.sb[p + "wsA"] = wp.ap().rearrange("p (c o) -> p c o", c=8)
    ctx.buf[p + "wsA"] = ctx.buf[p + "wpool"]
    ctx.sb[p + "wqo"] = wp.ap().rearrange("p (s c w o) -> p s c w o", s=2, c=8, w=4)
    ctx.buf[p + "wqo"] = ctx.buf[p + "wpool"]
    t = ctx.sbuf(p + "tril", [128, 128], BF16)
    sch.step("gpsimd", (lambda t=t: (lambda e: e.dma_start(t[:, :], din["tril"])))(),
             writes=(ctx.b(p + "tril"),), dma=True)
    t = ctx.sbuf(p + "A3", [128, T], BF16)
    sch.step("gpsimd", (lambda t=t: (lambda e: e.dma_start(t[:, :], din["s3_A"])))(),
             writes=(ctx.b(p + "A3"),), dma=True)
    t = ctx.sbuf(p + "B3", [128, T], BF16)
    sch.step("gpsimd", (lambda t=t: (lambda e: e.dma_start(t[:, :], din["s3_B"])))(),
             writes=(ctx.b(p + "B3"),), dma=True)
    t = ctx.sbuf(p + "lam", [1, 128], F32)
    sch.step("sync", (lambda t=t: (lambda e: e.dma_start(t[0:1, :], din["s3_lam"])))(),
             writes=(ctx.b(p + "lam"),), dma=True)
    t = ctx.sbuf(p + "gam", [64, 1], F32)
    sch.step("sync", (lambda t=t: (lambda e: e.dma_start(t[:, :], din["s3_gamma"])))(),
             writes=(ctx.b(p + "gam"),), dma=True)
    ctx.sbuf(p + "lsc", [1, 8], F32)
    t = ctx.sbuf(p + "bc3", [128, 1], BF16)
    sch.step("vector", (lambda t=t: (lambda e: e.memset(t[:, 0:1], 0.0)))(),
             writes=(ctx.b(p + "bc3"),), small=True)
    sch.step("vector", (lambda t=t: (lambda e: e.memset(t[96:128, 0:1], 1.0)))(),
             writes=(ctx.b(p + "bc3"),), small=True)
    ctx.sbuf(p + "sq64", [64, 512], BF16)
    ctx.sbuf(p + "wvs", [64, 2, 512], F32)
    ctx.sbuf(p + "e1", [128, 512], BF16)
    ctx.sbuf(p + "e2", [128, 512], BF16)
    ctx.sb[p + "OUT8"] = ctx.sb["xbf"]
    shared = Buf(p + "out8_alias")
    ctx.buf[p + "OUT8"] = shared
    for c in range(NCH):
        ctx.buf["xbf_%d" % c] = shared
    ctx.psum(p + "pA", [128, 512], F32)
    ctx.psum(p + "pB", [128, 512], F32)
    ctx.psum(p + "pC", [128, 512], F32)
    ctx.psum(p + "pD", [128, 512], F32)


def emit_attn_b0(ctx, din, xin_ap, xout_ap):
    p = "b0"
    sch = ctx.sch
    scale = 1.0 / math.sqrt(32.0)
    V65 = ctx.t(p + "V65")
    V65b = ctx.b(p + "V65")
    pA, pB, pC, pD = (ctx.t(p + x) for x in ("pA", "pB", "pC", "pD"))
    pAb, pBb, pCb, pDb = (ctx.b(p + x) for x in ("pA", "pB", "pC", "pD"))
    lsc, lscb = ctx.t(p + "lsc"), ctx.b(p + "lsc")
    lam, lamb = ctx.t(p + "lam"), ctx.b(p + "lam")

    sch.step("vector", lambda e: e.tensor_mul(lam[0:1, 0:32], lam[0:1, 0:32], lam[0:1, 32:64]),
             reads=(lamb,), writes=(lamb,), small=True)
    sch.step("vector", lambda e: e.tensor_mul(lam[0:1, 64:96], lam[0:1, 64:96], lam[0:1, 96:128]),
             reads=(lamb,), writes=(lamb,), small=True)
    sch.step("vector", lambda e: e.reduce_sum(lsc[0:1, 0:1], lam[0:1, 0:32], axis=AX.X),
             reads=(lamb,), writes=(lscb,), small=True)
    sch.step("vector", lambda e: e.reduce_sum(lsc[0:1, 1:2], lam[0:1, 64:96], axis=AX.X),
             reads=(lamb,), writes=(lscb,), small=True)
    sch.step("scalar", lambda e: e.activation(lsc[0:1, 2:4], lsc[0:1, 0:2], AF.Exp),
             reads=(lscb,), writes=(lscb,), small=True)
    sch.step("vector", lambda e: e.tensor_sub(lsc[0:1, 4:5], lsc[0:1, 2:3], lsc[0:1, 3:4]),
             reads=(lscb,), writes=(lscb,), small=True)
    sch.step("vector", lambda e: e.tensor_scalar(lsc[0:1, 5:6], lsc[0:1, 4:5], LAMBDA_INIT, -1.0,
                                                 op0=ALU.add, op1=ALU.mult),
             reads=(lscb,), writes=(lscb,), small=True)
    sch.step("vector", lambda e: e.tensor_scalar(ctx.t(p + "gam")[:, 0:1], ctx.t(p + "gam")[:, 0:1],
                                                 1.0 - LAMBDA_INIT, None, op0=ALU.mult),
             reads=(ctx.b(p + "gam"),), writes=(ctx.b(p + "gam"),), small=True)
    for j in range(4):
        sch.step("vector", (lambda j=j: (lambda e: e.memset(ctx.t(p + "KA%d" % j)[32:33, :], 1.0)))(),
                 writes=(ctx.b(p + "KA%d" % j),))

    for seq in range(SEQS):
        emit_ln(ctx, xin_ap, seq)
        _emit_vphase(ctx, din, p, "s3_wv")
        for oc in range(NCH):
            wsl = oc % 2
            for wi, wn in enumerate(["s3_wq", "s3_wq_sw", "s3_wk", "s3_wk_sw"]):
                sch.step("gpsimd", (lambda wn=wn, wi=wi, oc=oc, wsl=wsl: (lambda e: e.dma_start(
                    ctx.t(p + "wqo")[:, wsl, :, wi, :],
                    din[wn][:, :, oc * 128:(oc + 1) * 128].rearrange("c p o -> p c o"))))(),
                    writes=(ctx.b(p + "wqo"),), dma=True, tag_writes=(wi == 3))
            sch.step("vector", lambda e: e.memset(ctx.t(p + "KM")[0:1, 0:4], 0.0),
                     reads=(), writes=(ctx.b(p + "KM"),), small=True)
            for phase in range(2):
                dname = (p + "QA%d") if phase == 0 else (p + "KA%d")
                for tt in range(NTT):
                    fns = []
                    for dc in range(NCH):
                        fns.append((lambda dc=dc, tt=tt, wi=2 * phase, wsl=wsl: (lambda e: e.matmul(
                            pA[:, :], ctx.t(p + "wqo")[:, wsl, dc, wi, :],
                            ctx.t("xln")[:, dc, tt * 512:(tt + 1) * 512],
                            start=(dc == 0), stop=(dc == NCH - 1))))())
                    sch.step("tensor", fns, reads=(ctx.b(p + "wqo"), ctx.b("xln")), writes=(pAb,))
                    fns = []
                    for dc in range(NCH):
                        fns.append((lambda dc=dc, tt=tt, wi=2 * phase + 1, wsl=wsl: (lambda e: e.matmul(
                            pB[:, :], ctx.t(p + "wqo")[:, wsl, dc, wi, :],
                            ctx.t("xln")[:, dc, tt * 512:(tt + 1) * 512],
                            start=(dc == 0), stop=(dc == NCH - 1))))())
                    sch.step("tensor", fns, reads=(ctx.b(p + "wqo"), ctx.b("xln")), writes=(pBb,))
                    sch.step("scalar", lambda e: e.activation(ctx.t(p + "sqn")[:, :], pA[:, :], AF.Square),
                             reads=(pAb,), writes=(ctx.b(p + "sqn"),))
                    NR = [(pD, pDb, 0), (pC, pCb, 0), (pD, pDb, 32), (pC, pCb, 32)]
                    for j in range(4):
                        tgt, tgtb, row = NR[j]
                        if j < 3:
                            sch.step("tensor", (lambda j=j, tgt=tgt, row=row: (lambda e: e.matmul(
                                tgt[row:row + 1, :], ctx.t("onesb")[32 * j:32 * j + 32, 0:1],
                                ctx.t(p + "sqn")[32 * j:32 * j + 32, :], start=True, stop=True)))(),
                                reads=(ctx.b(p + "sqn"), ctx.b("onesb")), writes=(tgtb,),
                                tag_writes=(j == 2))
                        else:
                            sch.step("tensor", (lambda tgt=tgt, row=row: (lambda e: e.matmul(
                                tgt[row:row + 1, :], ctx.t(p + "bc3")[:, 0:1],
                                ctx.t(p + "sqn")[:, :], start=True, stop=True)))(),
                                reads=(ctx.b(p + "sqn"), ctx.b(p + "bc3")), writes=(tgtb,))
                    if phase == 0:
                        for j in range(4):
                            tgt, tgtb, row = NR[j]
                            sch.step("scalar", (lambda j=j, tgt=tgt, row=row, tt=tt: (lambda e: e.copy(
                                ctx.t(p + "NST")[32 * (j // 2):32 * (j // 2) + 1,
                                                 j % 2, tt * 512:(tt + 1) * 512],
                                tgt[row:row + 1, :])))(),
                                reads=(tgtb,), writes=(ctx.b(p + "NST"),), tag_writes=(j == 3))
                    else:
                        for j in range(4):
                            tgt, tgtb, row = NR[j]
                            sch.step("vector", (lambda j=j, tgt=tgt, row=row: (lambda e: e.reduce_max(
                                ctx.t(p + "KM")[0:1, 8 + j:9 + j], tgt[row:row + 1, :], axis=AX.X)))(),
                                reads=(tgtb,), writes=(ctx.b(p + "KM"),), small=True)
                            sch.step("vector", (lambda j=j: (lambda e: e.tensor_max(
                                ctx.t(p + "KM")[0:1, j:j + 1], ctx.t(p + "KM")[0:1, j:j + 1],
                                ctx.t(p + "KM")[0:1, 8 + j:9 + j])))(),
                                reads=(ctx.b(p + "KM"),), writes=(ctx.b(p + "KM"),), small=True)
                    sch.step("vector", (lambda tt=tt: (lambda e: e.tensor_mul(
                        ctx.t(p + "e1")[:, :], pA[:, :], ctx.t(p + "A3")[:, tt * 512:(tt + 1) * 512])))(),
                        reads=(pAb, ctx.b(p + "A3")), writes=(ctx.b(p + "e1"),))
                    sch.step("vector", (lambda tt=tt: (lambda e: e.tensor_mul(
                        ctx.t(p + "e2")[:, :], pB[:, :], ctx.t(p + "B3")[:, tt * 512:(tt + 1) * 512])))(),
                        reads=(pBb, ctx.b(p + "B3")), writes=(ctx.b(p + "e2"),))
                    for j in range(4):
                        sch.step("vector", (lambda j=j, tt=tt, dname=dname: (lambda e: e.tensor_add(
                            ctx.t(dname % j)[0:32, tt * 512:(tt + 1) * 512],
                            ctx.t(p + "e1")[32 * j:32 * j + 32, :],
                            ctx.t(p + "e2")[32 * j:32 * j + 32, :])))(),
                            reads=(ctx.b(p + "e1"), ctx.b(p + "e2")), writes=(ctx.b(dname % j),))
            for j in range(4):
                rb = 32 * (j // 2)
                pl = j % 2
                sch.step("scalar", (lambda rb=rb, pl=pl: (lambda e: e.activation(
                    ctx.t(p + "NST")[rb:rb + 1, pl, :], ctx.t(p + "NST")[rb:rb + 1, pl, :], AF.Sqrt)))(),
                    reads=(ctx.b(p + "NST"),), writes=(ctx.b(p + "NST"),))
                sch.step("scalar", (lambda j=j: (lambda e: e.activation(
                    ctx.t(p + "KM")[0:1, 16 + j:17 + j],
                    ctx.t(p + "KM")[0:1, j:j + 1], AF.Sqrt)))(),
                    reads=(ctx.b(p + "KM"),), writes=(ctx.b(p + "KM"),), small=True)
                sch.step("vector", (lambda rb=rb, pl=pl, j=j: (lambda e: e.tensor_scalar(
                    ctx.t(p + "QA%d" % j)[32:33, :], ctx.t(p + "NST")[rb:rb + 1, pl, :],
                    ctx.t(p + "KM")[0:1, 16 + j:17 + j], -1.0, op0=ALU.mult, op1=ALU.mult)))(),
                    reads=(ctx.b(p + "NST"), ctx.b(p + "KM")), writes=(ctx.b(p + "QA%d" % j),))
            for r in range(2):
                h = 2 * oc + r
                for qt in range(NTT):
                    kts = list(range(4 * qt + 4))
                    for ki, kt in enumerate(kts):
                        off = max(0, kt * 128 - qt * 512)
                        for br in range(2):
                            j = 2 * r + br
                            QA, QAb = ctx.t(p + "QA%d" % j), ctx.b(p + "QA%d" % j)
                            KA, KAb = ctx.t(p + "KA%d" % j), ctx.b(p + "KA%d" % j)
                            psb, psbb = (pA, pAb) if br == 0 else (pB, pBb)
                            pvp, pvpb = (pC, pCb) if br == 0 else (pD, pDb)
                            sch.step("tensor", (lambda psb=psb, kt=kt, qt=qt, off=off, QA=QA, KA=KA: (lambda e: e.matmul(
                                psb[:, off:512], KA[:, kt * 128:(kt + 1) * 128],
                                QA[:, qt * 512 + off:(qt + 1) * 512], start=True, stop=True)))(),
                                reads=(QAb, KAb), writes=(psbb,))
                            sch.step("scalar", (lambda psb=psb, br=br, off=off: (lambda e: e.activation(
                                ctx.t(p + "pt")[:, br, off:512], psb[:, off:512], AF.Exp, scale=scale)))(),
                                reads=(psbb,), writes=(ctx.b(p + "pt"),))
                            if kt >= 4 * qt:
                                sch.step("vector", (lambda br=br, off=off: (lambda e: e.tensor_mul(
                                    ctx.t(p + "pt")[:, br, off:off + 128],
                                    ctx.t(p + "pt")[:, br, off:off + 128],
                                    ctx.t(p + "tril")[:, :])))(),
                                    reads=(ctx.b(p + "pt"), ctx.b(p + "tril")), writes=(ctx.b(p + "pt"),))
                            sch.step("tensor", (lambda kt=kt, br=br, off=off, h=h, pvp=pvp, first=(ki == 0), last=(ki == len(kts) - 1): (lambda e: e.matmul(
                                pvp[0:65, off:512], V65[:, kt, h * 65:h * 65 + 65],
                                ctx.t(p + "pt")[:, br, off:512], start=first, stop=last)))(),
                                reads=(V65b, ctx.b(p + "pt")), writes=(pvpb,), tag_writes=(ki == len(kts) - 1))
                    A1, A1b = pC, pCb
                    sch.step("vector", lambda e: _recip_bf(ctx, e, ctx.t(p + "rrow")[0:1, 0, :], A1[64:65, :]),
                             reads=(A1b,), writes=(ctx.b(p + "rrow"),))
                    sch.step("vector", lambda e: _recip_bf(ctx, e, ctx.t(p + "rrow")[0:1, 1, :], pD[64:65, :]),
                             reads=(pDb,), writes=(ctx.b(p + "rrow"),))
                    sch.step("vector", lambda e: e.tensor_scalar(
                        ctx.t(p + "rrow")[0:1, 1, :], ctx.t(p + "rrow")[0:1, 1, :],
                        lsc[0:1, 5:6], None, op0=ALU.mult),
                        reads=(ctx.b(p + "rrow"), lscb), writes=(ctx.b(p + "rrow"),), small=True)
                    sch.step("tensor", lambda e: e.matmul(pA[0:64, :], ctx.t("onesrow")[0:1, 0:64],
                                                          ctx.t(p + "rrow")[0:1, 0, :], start=True, stop=True),
                             reads=(ctx.b(p + "rrow"), ctx.b("onesrow")), writes=(pAb,))
                    sch.step("scalar", lambda e: e.copy(ctx.t(p + "nbs")[:, 0, :], pA[0:64, :]),
                             reads=(pAb,), writes=(ctx.b(p + "nbs"),))
                    sch.step("tensor", lambda e: e.matmul(pA[0:64, :], ctx.t("onesrow")[0:1, 0:64],
                                                          ctx.t(p + "rrow")[0:1, 1, :], start=True, stop=True),
                             reads=(ctx.b(p + "rrow"), ctx.b("onesrow")), writes=(pAb,))
                    sch.step("scalar", lambda e: e.copy(ctx.t(p + "nbs")[:, 1, :], pA[0:64, :]),
                             reads=(pAb,), writes=(ctx.b(p + "nbs"),))
                    sch.step("vector", lambda e: e.tensor_mul(ctx.t(p + "wvs")[:, 0, :],
                                                              A1[0:64, :], ctx.t(p + "nbs")[:, 0, :]),
                             reads=(A1b, ctx.b(p + "nbs")), writes=(ctx.b(p + "wvs"),))
                    sch.step("vector", lambda e: e.tensor_mul(ctx.t(p + "wvs")[:, 1, :],
                                                              pD[0:64, :], ctx.t(p + "nbs")[:, 1, :]),
                             reads=(pDb, ctx.b(p + "nbs")), writes=(ctx.b(p + "wvs"),))
                    sch.step("vector", lambda e: e.tensor_add(ctx.t(p + "wvs")[:, 0, :],
                                                              ctx.t(p + "wvs")[:, 0, :], ctx.t(p + "wvs")[:, 1, :]),
                             reads=(ctx.b(p + "wvs"),), writes=(ctx.b(p + "wvs"),))
                    sch.step("scalar", lambda e: e.activation(ctx.t(p + "sq64")[:, :],
                                                              ctx.t(p + "wvs")[:, 0, :], AF.Square),
                             reads=(ctx.b(p + "wvs"),), writes=(ctx.b(p + "sq64"),))
                    sch.step("tensor", lambda e: e.matmul(pA[0:1, :], ctx.t("onesb")[0:64, 0:1],
                                                          ctx.t(p + "sq64")[:, :], start=True, stop=True),
                             reads=(ctx.b(p + "sq64"), ctx.b("onesb")), writes=(pAb,))
                    sch.step("vector", lambda e: e.tensor_scalar(ctx.t(p + "srw")[0:1, :], pA[0:1, :],
                                                                 1.0 / 64.0, None, op0=ALU.mult),
                             reads=(pAb,), writes=(ctx.b(p + "srw"),))
                    sch.step("scalar", lambda e: e.activation(ctx.t(p + "srw")[0:1, :], ctx.t(p + "srw")[0:1, :],
                                                              AF.Sqrt, bias=ctx.t("epscol")[0:1, 0:1], scale=1.0),
                             reads=(ctx.b(p + "srw"), ctx.b("epscol")), writes=(ctx.b(p + "srw"),))
                    sch.step("vector", lambda e: _recip_bf(ctx, e, ctx.t(p + "rrow")[0:1, 2, :], ctx.t(p + "srw")[0:1, :]),
                             reads=(ctx.b(p + "srw"),), writes=(ctx.b(p + "rrow"),))
                    sch.step("tensor", lambda e: e.matmul(pD[0:64, :], ctx.t("onesrow")[0:1, 0:64],
                                                          ctx.t(p + "rrow")[0:1, 2, :], start=True, stop=True),
                             reads=(ctx.b(p + "rrow"), ctx.b("onesrow")), writes=(pDb,))
                    sch.step("scalar", lambda e: e.copy(ctx.t(p + "nbs")[:, 2, :], pD[0:64, :]),
                             reads=(pDb,), writes=(ctx.b(p + "nbs"),))
                    sch.step("vector", lambda e: e.tensor_mul(ctx.t(p + "wvs")[:, 1, :],
                                                              ctx.t(p + "wvs")[:, 0, :], ctx.t(p + "nbs")[:, 2, :]),
                             reads=(ctx.b(p + "wvs"), ctx.b(p + "nbs")), writes=(ctx.b(p + "wvs"),))
                    sch.step("vector", (lambda h=h, qt=qt: (lambda e: e.tensor_scalar(
                        ctx.t(p + "OUT8")[(h % 2) * 64:(h % 2) * 64 + 64, h // 2, qt * 512:(qt + 1) * 512],
                        ctx.t(p + "wvs")[:, 1, :], ctx.t(p + "gam")[:, 0:1], None, op0=ALU.mult)))(),
                        reads=(ctx.b(p + "wvs"), ctx.b(p + "gam")), writes=(ctx.b(p + "OUT8"),),
                        tag_writes=(qt == NTT - 1 and oc == NCH - 1 and r == 1))
        _emit_proj(ctx, din, p, xin_ap, xout_ap, seq, "s3_wo", None)


STAGE_INPUTS["attn_b0"] = ["s3_wq", "s3_wq_sw", "s3_wk", "s3_wk_sw", "s3_wv", "s3_wo",
                           "s3_lam", "s3_gamma", "s3_A", "s3_B", "tril"]
STAGE_ALLOC["attn_b0"] = alloc_stage_attn_b0
STAGE_EMIT["attn_b0"] = emit_attn_b0


def alloc_stage_attn_a1(ctx, din):
    p = "a1"
    sch = ctx.sch
    ctx.sbuf(p + "QT", [128, T], BF16)
    ctx.sbuf(p + "KT", [128, T], BF16)
    ctx.sbuf(p + "VT", [128, T], BF16)
    ctx.sbuf(p + "Vr", [128, 2, 520], BF16)
    ctx.sbuf(p + "ptile", [128, 2, 256], BF16)
    ctx.sbuf(p + "NST", [33, 4, T], BF16)
    if "rows_bf" not in ctx.sb:
        ctx.sb["rows_bf"] = ctx.sb[p + "NST"]
        ctx.buf["rows_bf"] = ctx.buf[p + "NST"]
    ctx.sbuf(p + "wsc", [1, 2048], BF16)
    ctx.sbuf(p + "KMx", [1, 8], F32)
    ctx.sbuf(p + "sqn", [128, 512], BF16)
    ctx.sbuf(p + "prs", [128, 2, 512], F32)
    ctx.sbuf(p + "nbs", [128, 256], BF16)
    ctx.sbuf(p + "rrow", [1, 256], BF16)
    wp = ctx.sbuf(p + "wpool", [128, 8192], BF16)
    ctx.sb[p + "wsA"] = wp.ap().rearrange("p (c o) -> p c o", c=8)
    ctx.buf[p + "wsA"] = ctx.buf[p + "wpool"]
    ctx.sb[p + "wqo"] = wp.ap().rearrange("p (s c w o) -> p s c w o", s=2, c=8, w=4)
    ctx.buf[p + "wqo"] = ctx.buf[p + "wpool"]
    t = ctx.sbuf(p + "idb", [128, 64], BF16)
    sch.step("gpsimd", (lambda t=t: (lambda e: e.dma_start(t[:, :], din["ident64d"])))(),
             writes=(ctx.b(p + "idb"),), dma=True)
    ctx.sb[p + "OUT8"] = ctx.sb["xbf"]
    shared = Buf(p + "out8_alias")
    ctx.buf[p + "OUT8"] = shared
    for c in range(NCH):
        ctx.buf["xbf_%d" % c] = shared
    ctx.psum(p + "pA", [128, 512], F32)
    ctx.psum(p + "pC", [128, 512], F32)
    ctx.psum(p + "pD", [128, 512], F32)
    ctx.psum(p + "pT", [128, 1024], BF16)


def emit_attn_a1(ctx, din, xin_ap, xout_ap):
    p = "a1"
    sch = ctx.sch
    pA, pC, pD, pT = (ctx.t(p + x) for x in ("pA", "pC", "pD", "pT"))
    pAb, pCb, pDb, pTb = (ctx.b(p + x) for x in ("pA", "pC", "pD", "pT"))
    QT, KT, VT = ctx.t(p + "QT"), ctx.t(p + "KT"), ctx.t(p + "VT")
    QTb, KTb, VTb = ctx.b(p + "QT"), ctx.b(p + "KT"), ctx.b(p + "VT")
    Vr, Vrb = ctx.t(p + "Vr"), ctx.b(p + "Vr")
    W = ctx.t(p + "wsc")
    Wb = ctx.b(p + "wsc")
    for seq in range(SEQS):
        emit_ln(ctx, xin_ap, seq)
        sch.step("vector", lambda e: e.memset(Vr[:, :, 512:513], 1.0), writes=(Vrb,), small=True)
        for oc in range(NCH):
            wsl = oc % 2
            for wi, wn in enumerate(["s5_wq", "s5_wk", "s5_wv"]):
                sch.step("gpsimd", (lambda wn=wn, wi=wi, oc=oc, wsl=wsl: (lambda e: e.dma_start(
                    ctx.t(p + "wqo")[:, wsl, :, wi, :],
                    din[wn][:, :, oc * 128:(oc + 1) * 128].rearrange("c p o -> p c o"))))(),
                    writes=(ctx.b(p + "wqo"),), dma=True, tag_writes=(wi == 2))
            for wi, (dst, dstb) in enumerate(((QT, QTb), (KT, KTb), (VT, VTb))):
                for tt in range(NTT):
                    fns = []
                    for dc in range(NCH):
                        fns.append((lambda dc=dc, tt=tt, wi=wi, wsl=wsl: (lambda e: e.matmul(
                            pA[:, :], ctx.t(p + "wqo")[:, wsl, dc, wi, :],
                            ctx.t("xln")[:, dc, tt * 512:(tt + 1) * 512],
                            start=(dc == 0), stop=(dc == NCH - 1))))())
                    sch.step("tensor", fns, reads=(ctx.b(p + "wqo"), ctx.b("xln")), writes=(pAb,))
                    sch.step("scalar", (lambda tt=tt, dst=dst: (lambda e: e.copy(
                        dst[:, tt * 512:(tt + 1) * 512], pA[:, :])))(),
                        reads=(pAb,), writes=(dstb,), tag_writes=(tt == NTT - 1))
                    if wi < 2:
                        sch.step("scalar", (lambda tt=tt: (lambda e: e.activation(
                            ctx.t(p + "sqn")[:, :], pA[:, :], AF.Square)))(),
                            reads=(pAb,), writes=(ctx.b(p + "sqn"),))
                        for r in range(2):
                            tgt, tgtb = (pD, pDb) if r == 0 else (pC, pCb)
                            sch.step("tensor", (lambda r=r, tgt=tgt: (lambda e: e.matmul(
                                tgt[0:1, :], ctx.t("onesb")[64 * r:64 * r + 64, 0:1],
                                ctx.t(p + "sqn")[64 * r:64 * r + 64, :], start=True, stop=True)))(),
                                reads=(ctx.b(p + "sqn"), ctx.b("onesb")), writes=(tgtb,))
                        for r in range(2):
                            tgt, tgtb = (pD, pDb) if r == 0 else (pC, pCb)
                            sch.step("scalar", (lambda r=r, tt=tt, wi=wi, tgt=tgt: (lambda e: e.copy(
                                ctx.t(p + "NST")[0:1, 2 * wi + r, tt * 512:(tt + 1) * 512],
                                tgt[0:1, :])))(),
                                reads=(tgtb,), writes=(ctx.b(p + "NST"),), tag_writes=(r == 1))
            for r in range(2):
                h = 2 * oc + r
                rb = 32 * r
                for wi in range(2):
                    pl = 2 * wi + r
                    dstseg = 0 if wi == 0 else 256
                    sch.step("vector", (lambda pl=pl, dstseg=dstseg: (lambda e: e.tensor_copy(
                        W[0:1, dstseg:dstseg + 256], ctx.t(p + "NST")[0:1, pl, 0:T:8])))(),
                        reads=(ctx.b(p + "NST"),), writes=(Wb,), small=True)
                    for p8 in range(1, 8):
                        sch.step("vector", (lambda pl=pl, p8=p8, dstseg=dstseg: (lambda e: e.tensor_add(
                            W[0:1, dstseg:dstseg + 256], W[0:1, dstseg:dstseg + 256],
                            ctx.t(p + "NST")[0:1, pl, p8:T:8])))(),
                            reads=(ctx.b(p + "NST"), Wb), writes=(Wb,), small=True)
                sch.step("vector", lambda e: e.reduce_max(ctx.t(p + "KMx")[0:1, 0:1], W[0:1, 256:512], axis=AX.X),
                         reads=(Wb,), writes=(ctx.b(p + "KMx"),), small=True)
                sch.step("scalar", lambda e: e.activation(ctx.t(p + "KMx")[0:1, 1:2], ctx.t(p + "KMx")[0:1, 0:1], AF.Sqrt),
                         reads=(ctx.b(p + "KMx"),), writes=(ctx.b(p + "KMx"),), small=True)
                sch.step("scalar", lambda e: e.activation(W[0:1, 512:768], W[0:1, 0:256], AF.Sqrt),
                         reads=(Wb,), writes=(Wb,), small=True)
                sch.step("vector", lambda e: e.tensor_scalar(
                    W[0:1, 512:768], W[0:1, 512:768], ctx.t(p + "KMx")[0:1, 1:2], -1.0,
                    op0=ALU.mult, op1=ALU.mult),
                    reads=(Wb, ctx.b(p + "KMx")), writes=(Wb,), small=True)
                for wt in range(2):
                    for p8 in range(8):
                        sch.step("tensor", (lambda p8=p8, wt=wt, r=r: (lambda e: e.transpose(
                            pT[:, p8 * 64:(p8 + 1) * 64],
                            VT[64 * r:64 * r + 64, 1024 * wt + p8:1024 * (wt + 1):8],
                            ctx.t(p + "idb")[64 * r:64 * r + 64, 0:64])))(),
                            reads=(VTb, ctx.b(p + "idb")), writes=(pTb,), tag_writes=(p8 == 7))
                    sch.step("vector", (lambda wt=wt: (lambda e: e.tensor_copy(
                        Vr[:, wt, 0:512], pT[:, 0:512])))(),
                        reads=(pTb,), writes=(Vrb,), tag_writes=(wt == 1))
                for wpt in range(2):
                    fns = []
                    for p8 in range(8):
                        fns.append((lambda p8=p8, wpt=wpt, r=r: (lambda e: e.matmul(
                            pA[:, wpt * 256:(wpt + 1) * 256],
                            KT[64 * r:64 * r + 64, 1024 * wpt + p8:1024 * (wpt + 1):8],
                            QT[64 * r:64 * r + 64, p8:T:8],
                            start=(p8 == 0), stop=False)))())
                    fns.append((lambda wpt=wpt: (lambda e: e.matmul(
                        pA[:, wpt * 256:(wpt + 1) * 256], ctx.t("onesrow")[0:1, :],
                        W[0:1, 512:768], start=False, stop=True)))())
                    sch.step("tensor", fns, reads=(KTb, QTb, Wb, ctx.b("onesrow")), writes=(pAb,))
                    sch.step("scalar", (lambda wpt=wpt: (lambda e: e.activation(
                        ctx.t(p + "ptile")[:, wpt, :], pA[:, wpt * 256:(wpt + 1) * 256],
                        AF.Exp, scale=0.125)))(),
                        reads=(pAb,), writes=(ctx.b(p + "ptile"),), tag_writes=(wpt == 1))
                fns = []
                for wpt in range(2):
                    fns.append((lambda wpt=wpt: (lambda e: e.matmul(
                        pD[0:1, 0:256], Vr[:, wpt, 512:513], ctx.t(p + "ptile")[:, wpt, :],
                        start=(wpt == 0), stop=(wpt == 1))))())
                sch.step("tensor", fns, reads=(Vrb, ctx.b(p + "ptile")), writes=(pDb,))
                sch.step("vector", lambda e: _recip_bf(ctx, e, ctx.t(p + "rrow")[0:1, :], pD[0:1, 0:256]),
                         reads=(pDb,), writes=(ctx.b(p + "rrow"),))
                sch.step("tensor", lambda e: e.matmul(pD[:, 256:512], ctx.t("onesrow")[0:1, :],
                                                      ctx.t(p + "rrow")[0:1, :], start=True, stop=True),
                         reads=(ctx.b(p + "rrow"), ctx.b("onesrow")), writes=(pDb,))
                sch.step("scalar", lambda e: e.copy(ctx.t(p + "nbs")[:, :], pD[:, 256:512]),
                         reads=(pDb,), writes=(ctx.b(p + "nbs"),))
                for fc in range(4):
                    fns = []
                    for wpt in range(2):
                        fns.append((lambda fc=fc, wpt=wpt: (lambda e: e.matmul(
                            pC[:, 0:256], Vr[:, wpt, fc * 128:(fc + 1) * 128],
                            ctx.t(p + "ptile")[:, wpt, :],
                            start=(wpt == 0), stop=(wpt == 1))))())
                    sch.step("tensor", fns, reads=(Vrb, ctx.b(p + "ptile")), writes=(pCb,))
                    for e2 in range(2):
                        p8 = 2 * fc + e2
                        sch.step("vector", (lambda fc=fc, e2=e2, p8=p8, h=h: (lambda e: e.tensor_mul(
                            ctx.t(p + "OUT8")[(h % 2) * 64:(h % 2) * 64 + 64, h // 2, p8:T:8],
                            pC[64 * e2:64 * e2 + 64, 0:256],
                            ctx.t(p + "nbs")[64 * e2:64 * e2 + 64, :])))(),
                            reads=(pCb, ctx.b(p + "nbs")), writes=(ctx.b(p + "OUT8"),),
                            tag_writes=(oc == NCH - 1 and r == 1 and fc == 3 and e2 == 1))
        _emit_proj(ctx, din, p, xin_ap, xout_ap, seq, "s5_wproj", None)


STAGE_INPUTS["attn_a1"] = ["s5_wq", "s5_wk", "s5_wv", "s5_wproj", "ident64d"]
STAGE_ALLOC["attn_a1"] = alloc_stage_attn_a1
STAGE_EMIT["attn_a1"] = emit_attn_a1


def alloc_stage_moe(ctx, din):
    p = "mo"
    sch = ctx.sch
    if "rows_bf" not in ctx.sb:
        ctx.sbuf("rows_bf", [1, 2, T], BF16)
    ctx.sbuf(p + "ys", [128, NCH, T], F32)
    ctx.sbuf(p + "hs", [128, 6, 512], BF16)
    ctx.sbuf(p + "w1s0", [128, 8, 768], BF16)
    ctx.sbuf(p + "w3s0", [128, 8, 768], BF16)
    ctx.sbuf(p + "w2h0_0", [128, 6, 512], BF16)
    ctx.sbuf(p + "w2h1_0", [128, 6, 512], BF16)
    ctx.sbuf(p + "w2h0_1", [128, 6, 512], BF16)
    xbf_t = ctx.sb["xbf"]
    ctx.sb[p + "w1s1"] = xbf_t.ap()[:, :, 0:768]
    ctx.sb[p + "w3s1"] = xbf_t.ap()[:, :, 768:1536]
    ctx.sb[p + "w2h1_1"] = xbf_t.ap()[:, 0:6, 1536:2048]
    wal = Buf(p + "w13alias")
    ctx.buf[p + "w1s1"] = wal
    ctx.buf[p + "w3s1"] = wal
    ctx.buf[p + "w2h1_1"] = wal
    for c in range(NCH):
        ctx.buf["xbf_%d" % c] = wal
    ctx.sb[p + "cbs"] = ctx.sb["tsub"].ap()
    cal = Buf(p + "cbsalias")
    ctx.buf[p + "cbs"] = cal
    ctx.buf["tsub_0"] = cal
    ctx.buf["tsub_1"] = cal
    sq_t = ctx.sb["sqtt"]
    ctx.sb[p + "CT"] = sq_t.ap()[0:16, 3:7, :].rearrange("p a b -> p (a b)")
    ctx.buf[p + "CT"] = ctx.buf["sqtt"]
    t = ctx.sbuf(p + "gwTs", [128, 8, 16], BF16)
    sch.step("gpsimd", (lambda t=t: (lambda e: e.dma_start(
        t[:, :, :], din["gwT"].rearrange("(c p) e -> p c e", p=128))))(),
        writes=(ctx.b(p + "gwTs"),), dma=True)
    t = ctx.sbuf(p + "gbr", [1, 16], F32)
    sch.step("sync", (lambda t=t: (lambda e: e.dma_start(t[0:1, :], din["gbias"])))(),
             writes=(ctx.b(p + "gbr"),), dma=True)
    ctx.sb[p + "idf"] = ctx.sb["xio"].ap()[:, 0, 0:128]
    ctx.buf[p + "idf"] = ctx.buf["xio_0"]
    ctx.sbuf(p + "GBB", [128, 16], BF16)
    for nm, w in ((p + "SC", 16), (p + "SB2", 16), (p + "PS", 4), (p + "G2", 4),
                  (p + "EM", 12), (p + "MS", 16), (p + "MS2", 16), (p + "CMB", 16),
                  (p + "rsc", 8)):
        ctx.sbuf(nm, [128, w], F32)
    ctx.psum(p + "pA", [128, 512], F32)
    ctx.psum(p + "pB", [128, 512], F32)
    ctx.psum(p + "pC", [128, 512], F32)
    ctx.psum(p + "pD", [128, 512], F32)


def _emit_routing_tile(ctx, p, tl):
    sch = ctx.sch
    pA, pD = ctx.t(p + "pA"), ctx.t(p + "pD")
    pAb, pDb = ctx.b(p + "pA"), ctx.b(p + "pD")
    SC, SB2, PS, G2 = (ctx.t(p + x) for x in ("SC", "SB2", "PS", "G2"))
    EM, MS, MS2, CMB, rsc = (ctx.t(p + x) for x in ("EM", "MS", "MS2", "CMB", "rsc"))
    SCb, SB2b, PSb, G2b = (ctx.b(p + x) for x in ("SC", "SB2", "PS", "G2"))
    EMb, MSb, MS2b, CMBb, rscb = (ctx.b(p + x) for x in ("EM", "MS", "MS2", "CMB", "rsc"))
    fns = []
    for dc in range(NCH):
        fns.append((lambda dc=dc, tl=tl: (lambda e: e.matmul(
            pA[:, 0:16], ctx.t("xln")[:, dc, tl * 128:(tl + 1) * 128],
            ctx.t(p + "gwTs")[:, dc, :], start=(dc == 0), stop=(dc == NCH - 1))))())
    sch.step("tensor", fns, reads=(ctx.b("xln"), ctx.b(p + "gwTs")), writes=(pAb,))
    S = sch.step
    S("scalar", lambda e: e.activation(SC[:, :], pA[:, 0:16], AF.Sigmoid),
      reads=(pAb,), writes=(SCb,), small=True)
    S("vector", lambda e: e.tensor_add(SB2[:, :], SC[:, :], ctx.t(p + "GBB")[:, :]),
      reads=(SCb, ctx.b(p + "GBB")), writes=(SB2b,), small=True)
    pairs = [(0, 1), (0, 2), (0, 3), (1, 2), (1, 3), (2, 3)]
    for i, (a, b) in enumerate(pairs):
        if i == 0:
            S("vector", (lambda a=a, b=b: (lambda e: e.tensor_add(
                G2[:, :], SB2[:, a:16:4], SB2[:, b:16:4])))(),
              reads=(SB2b,), writes=(G2b,), small=True)
        else:
            S("vector", (lambda a=a, b=b: (lambda e: e.tensor_add(
                PS[:, 0:4], SB2[:, a:16:4], SB2[:, b:16:4])))(),
              reads=(SB2b,), writes=(PSb,), small=True)
            S("vector", lambda e: e.tensor_max(G2[:, :], G2[:, :], PS[:, 0:4]),
              reads=(PSb, G2b), writes=(G2b,), small=True)
    S("vector", lambda e: e.reduce_max(rsc[:, 0:1], G2[:, :], axis=AX.X),
      reads=(G2b,), writes=(rscb,), small=True)
    S("vector", lambda e: e.tensor_scalar(EM[:, 0:4], G2[:, :], rsc[:, 0:1], None, op0=ALU.is_ge),
      reads=(G2b, rscb), writes=(EMb,), small=True)
    S("vector", lambda e: e.scalar_tensor_tensor(EM[:, 4:8], EM[:, 0:4], -1e9, G2[:, :],
                                                 op0=ALU.mult, op1=ALU.add),
      reads=(EMb, G2b), writes=(EMb,), small=True)
    S("vector", lambda e: e.reduce_max(rsc[:, 1:2], EM[:, 4:8], axis=AX.X),
      reads=(EMb,), writes=(rscb,), small=True)
    S("vector", lambda e: e.tensor_scalar(EM[:, 8:12], G2[:, :], rsc[:, 1:2], None, op0=ALU.is_ge),
      reads=(G2b, rscb), writes=(EMb,), small=True)
    for a in range(4):
        S("vector", (lambda a=a: (lambda e: e.tensor_copy(MS2[:, a:16:4], EM[:, 8:12])))(),
          reads=(EMb,), writes=(MS2b,), small=True, tag_writes=(a == 3))
    S("vector", lambda e: e.tensor_scalar(MS[:, :], MS2[:, :], 1e9, -1e9, op0=ALU.mult, op1=ALU.add),
      reads=(MS2b,), writes=(MSb,), small=True)
    S("vector", lambda e: e.tensor_add(MS[:, :], MS[:, :], SB2[:, :]),
      reads=(MSb, SB2b), writes=(MSb,), small=True)
    S("vector", lambda e: e.tensor_copy(MS2[:, :], MS[:, :]),
      reads=(MSb,), writes=(MS2b,), small=True)
    for it in range(3):
        S("vector", (lambda it=it: (lambda e: e.reduce_max(rsc[:, 2 + it:3 + it], MS2[:, :], axis=AX.X)))(),
          reads=(MS2b,), writes=(rscb,), small=True)
        S("vector", (lambda it=it: (lambda e: e.tensor_scalar(
            EM[:, 12:16] if False else CMB[:, :], MS2[:, :], rsc[:, 2 + it:3 + it], None, op0=ALU.is_ge)))(),
          reads=(MS2b, rscb), writes=(CMBb,), small=True)
        S("vector", lambda e: e.scalar_tensor_tensor(MS2[:, :], CMB[:, :], -1e9, MS2[:, :],
                                                     op0=ALU.mult, op1=ALU.add),
          reads=(CMBb, MS2b), writes=(MS2b,), small=True)
    S("vector", lambda e: e.reduce_max(rsc[:, 5:6], MS2[:, :], axis=AX.X),
      reads=(MS2b,), writes=(rscb,), small=True)
    S("vector", lambda e: e.tensor_scalar(CMB[:, :], MS[:, :], rsc[:, 5:6], None, op0=ALU.is_ge),
      reads=(MSb, rscb), writes=(CMBb,), small=True)
    S("vector", lambda e: e.tensor_mul(CMB[:, :], CMB[:, :], SC[:, :]),
      reads=(CMBb, SCb), writes=(CMBb,), small=True)
    S("vector", lambda e: e.reduce_sum(rsc[:, 6:7], CMB[:, :], axis=AX.X),
      reads=(CMBb,), writes=(rscb,), small=True)
    S("vector", lambda e: e.tensor_scalar(rsc[:, 7:8], rsc[:, 6:7], 1e-20, None, op0=ALU.add),
      reads=(rscb,), writes=(rscb,), small=True)
    S("vector", lambda e: e.reciprocal(rsc[:, 6:7], rsc[:, 7:8]),
      reads=(rscb,), writes=(rscb,), small=True)
    S("vector", lambda e: e.tensor_scalar(CMB[:, :], CMB[:, :], rsc[:, 6:7], 2.5, op0=ALU.mult, op1=ALU.mult),
      reads=(CMBb, rscb), writes=(CMBb,), small=True)
    S("tensor", (lambda tl=tl: (lambda e: e.transpose(
        pD[0:16, 0:128], CMB[:, :], ctx.t(p + "idf")[:, :])))(),
      reads=(CMBb, ctx.b(p + "idf")), writes=(pDb,))
    S("scalar", (lambda tl=tl: (lambda e: e.copy(
        ctx.t(p + "CT")[0:16, tl * 128:(tl + 1) * 128], pD[0:16, 0:128])))(),
      reads=(pDb,), writes=(ctx.b(p + "CT"),), small=True)


def emit_moe(ctx, din, xin_ap, xout_ap):
    p = "mo"
    sch = ctx.sch
    pA, pB, pC, pD = (ctx.t(p + x) for x in ("pA", "pB", "pC", "pD"))
    pAb, pBb, pCb, pDb = (ctx.b(p + x) for x in ("pA", "pB", "pC", "pD"))
    ys, ysb = ctx.t(p + "ys"), ctx.b(p + "ys")
    hs, hsb = ctx.t(p + "hs"), ctx.b(p + "hs")
    sch.step("scalar", lambda e: e.copy(ctx.t("rows_bf")[0:1, 0, 0:16], ctx.t(p + "gbr")[0:1, :]),
             reads=(ctx.b(p + "gbr"),), writes=(ctx.b("rows_bf"),), small=True)
    sch.step("tensor", lambda e: e.matmul(pD[:, 0:16], ctx.t("onesrow")[0:1, :],
                                          ctx.t("rows_bf")[0:1, 0, 0:16], start=True, stop=True),
             reads=(ctx.b("rows_bf"), ctx.b("onesrow")), writes=(pDb,))
    sch.step("vector", lambda e: e.tensor_copy(ctx.t(p + "GBB")[:, :], pD[:, 0:16]),
             reads=(pDb,), writes=(ctx.b(p + "GBB"),), small=True)
    for seq in range(SEQS):
        emit_ln(ctx, xin_ap, seq)
        sch.step("sync", lambda e: e.dma_start(ctx.t(p + "idf")[:, :], din["ident"]),
                 writes=(ctx.b("xio_0"),), dma=True)
        for tl in range(16):
            _emit_routing_tile(ctx, p, tl)
        def w_dmas(e_j):
            if e_j < 0:
                w1ap, w3ap, w2ap = din["ws1T"], din["ws3T"], din["ws2T"]
            else:
                w1ap = din["moe_w1T"][e_j]
                w3ap = din["moe_w3T"][e_j]
                w2ap = din["moe_w2T"][e_j]
            sl = (e_j + 1) % 2
            sch.step("gpsimd", (lambda w1ap=w1ap, sl=sl: (lambda e: e.dma_start(
                ctx.t(p + "w1s%d" % sl)[:, :, :], w1ap.rearrange("c p o -> p c o"))))(),
                writes=(ctx.b(p + "w1s%d" % sl),), dma=True, waw=(sl == 0))
            sch.step("gpsimd", (lambda w3ap=w3ap, sl=sl: (lambda e: e.dma_start(
                ctx.t(p + "w3s%d" % sl)[:, :, :], w3ap.rearrange("c p o -> p c o"))))(),
                writes=(ctx.b(p + "w3s%d" % sl),), dma=True, waw=(sl == 0))
            for hh in range(2):
                sch.step("gpsimd", (lambda w2ap=w2ap, hh=hh, e_j=e_j: (lambda e: e.dma_start(
                    ctx.t(p + "w2h%d_%d" % (hh, (e_j + 1) % 2))[:, :, :],
                    w2ap[:, :, hh * 512:(hh + 1) * 512].rearrange("c p o -> p c o"))))(),
                    writes=(ctx.b(p + "w2h%d_%d" % (hh, (e_j + 1) % 2)),), dma=True,
                    waw=(hh == 0 or (e_j + 1) % 2 == 0))
        w_dmas(-1)
        for e_i in range(-1, 16):
            shared = (e_i < 0)
            sl = (e_i + 1) % 2
            if e_i < 15:
                w_dmas(e_i + 1)
            if not shared:
                sch.step("sync", (lambda e_i=e_i: (lambda e: e.dma_start(
                    ctx.t("rows_bf")[0:1, 0, :], ctx.t(p + "CT")[e_i:e_i + 1, :])))(),
                    reads=(ctx.b(p + "CT"),), writes=(ctx.b("rows_bf"),), dma=True)
            for tt in range(NTT):
                hsl = tt % 2
                if not shared:
                    sch.step("tensor", (lambda tt=tt: (lambda e: e.matmul(
                        pD[:, 0:512], ctx.t("onesrow")[0:1, :],
                        ctx.t("rows_bf")[0:1, 0, tt * 512:(tt + 1) * 512], start=True, stop=True)))(),
                        reads=(ctx.b("rows_bf"), ctx.b("onesrow")), writes=(pDb,))
                    sch.step("scalar", (lambda tt=tt, hsl=hsl: (lambda e: e.copy(
                        ctx.t(p + "cbs")[:, hsl, :], pD[:, 0:512])))(),
                        reads=(pDb,), writes=(ctx.b(p + "cbs"),))
                for ic in range(6):
                    gb_ = (pA, pAb) if ic % 2 == 0 else (pB, pBb)
                    fns = []
                    for dc in range(NCH):
                        fns.append((lambda dc=dc, ic=ic, tt=tt, gp=gb_[0], sl=sl: (lambda e: e.matmul(
                            gp[:, :], ctx.t(p + "w1s%d" % sl)[:, dc, ic * 128:(ic + 1) * 128],
                            ctx.t("xln")[:, dc, tt * 512:(tt + 1) * 512],
                            start=(dc == 0), stop=(dc == NCH - 1))))())
                    sch.step("tensor", fns, reads=(ctx.b(p + "w1s%d" % sl), ctx.b("xln")), writes=(gb_[1],))
                    sch.step("scalar", (lambda gp=gb_[0]: (lambda e: e.activation(
                        ctx.t("sqtt")[:, 0, :], gp[:, :], AF.Silu)))(),
                        reads=(gb_[1],), writes=(ctx.b("sqtt"),))
                    ub_ = (pB, pBb) if ic % 2 == 0 else (pA, pAb)
                    fns = []
                    for dc in range(NCH):
                        fns.append((lambda dc=dc, ic=ic, tt=tt, up=ub_[0], sl=sl: (lambda e: e.matmul(
                            up[:, :], ctx.t(p + "w3s%d" % sl)[:, dc, ic * 128:(ic + 1) * 128],
                            ctx.t("xln")[:, dc, tt * 512:(tt + 1) * 512],
                            start=(dc == 0), stop=(dc == NCH - 1))))())
                    sch.step("tensor", fns, reads=(ctx.b(p + "w3s%d" % sl), ctx.b("xln")), writes=(ub_[1],))
                    sch.step("scalar", (lambda up=ub_[0]: (lambda e: e.copy(
                        ctx.t("sqtt")[:, 1, :], up[:, :])))(),
                        reads=(ub_[1],), writes=(ctx.b("sqtt"),))
                    if shared:
                        sch.step("gpsimd", (lambda ic=ic, hsl=hsl: (lambda e: e.tensor_mul(
                            hs[:, ic, :], ctx.t("sqtt")[:, 0, :], ctx.t("sqtt")[:, 1, :])))(),
                            reads=(ctx.b("sqtt"), ctx.b("sqtt")), writes=(hsb,),
                            tag_writes=(ic == 5))
                    else:
                        sch.step("gpsimd", (lambda hsl=hsl: (lambda e: e.tensor_mul(
                            ctx.t("sqtt")[:, 2, :], ctx.t("sqtt")[:, 1, :], ctx.t(p + "cbs")[:, hsl, :])))(),
                            reads=(ctx.b("sqtt"), ctx.b(p + "cbs")), writes=(ctx.b("sqtt"),))
                        sch.step("gpsimd", (lambda ic=ic, hsl=hsl: (lambda e: e.tensor_mul(
                            hs[:, ic, :], ctx.t("sqtt")[:, 0, :], ctx.t("sqtt")[:, 2, :])))(),
                            reads=(ctx.b("sqtt"), ctx.b("sqtt")), writes=(hsb,),
                            tag_writes=(ic == 5))
                for oc in range(NCH):
                    yb_ = (pC, pCb) if oc % 2 == 0 else (pD, pDb)
                    fns = []
                    hh = oc // 4
                    w2n = p + "w2h%d_%d" % (hh, sl)
                    for ic in range(6):
                        fns.append((lambda ic=ic, oc=oc, tt=tt, hsl=hsl, yp=yb_[0], w2n=w2n: (lambda e: e.matmul(
                            yp[:, :], ctx.t(w2n)[:, ic, (oc % 4) * 128:(oc % 4 + 1) * 128],
                            hs[:, ic, :], start=(ic == 0), stop=(ic == 5))))())
                    sch.step("tensor", fns, reads=(ctx.b(w2n), hsb), writes=(yb_[1],))
                    if shared:
                        sch.step("scalar", (lambda oc=oc, tt=tt, yp=yb_[0]: (lambda e: e.copy(
                            ys[:, oc, tt * 512:(tt + 1) * 512], yp[:, :])))(),
                            reads=(yb_[1],), writes=(ysb,), tag_writes=(oc == NCH - 1 and tt == NTT - 1))
                    else:
                        sch.step("vector", (lambda oc=oc, tt=tt, yp=yb_[0]: (lambda e: e.tensor_add(
                            ys[:, oc, tt * 512:(tt + 1) * 512], yp[:, :],
                            ys[:, oc, tt * 512:(tt + 1) * 512])))(),
                            reads=(yb_[1], ysb), writes=(ysb,), tag_writes=(oc == NCH - 1 and tt == NTT - 1))
        for oc in range(NCH):
            for tt in range(NTT):
                sl = (oc * NTT + tt) % 2
                sch.step("sync", (lambda oc=oc, seq=seq, sl=sl, tt=tt: (lambda e: e.dma_start(
                    ctx.t("xio")[:, sl, :],
                    xin_ap[seq, oc * 128:(oc + 1) * 128, tt * 512:(tt + 1) * 512])))(),
                    writes=(ctx.b("xio_%d" % sl),), dma=True)
                sch.step("vector", (lambda oc=oc, tt=tt, sl=sl: (lambda e: e.tensor_add(
                    ctx.t("xio")[:, sl, :], ys[:, oc, tt * 512:(tt + 1) * 512],
                    ctx.t("xio")[:, sl, :])))(),
                    reads=(ysb,), writes=(ctx.b("xio_%d" % sl),))
                sch.step("sync", (lambda oc=oc, tt=tt, seq=seq, sl=sl: (lambda e: e.dma_start(
                    xout_ap[seq, oc * 128:(oc + 1) * 128, tt * 512:(tt + 1) * 512],
                    ctx.t("xio")[:, sl, :])))(),
                    reads=(ctx.b("xio_%d" % sl),), writes=(), dma=True)


def alloc_stage_moe_wrap(ctx, din):
    alloc_stage_moe(ctx, din)


STAGE_INPUTS["moe"] = ["gwT", "gbias", "moe_w1T", "moe_w3T", "moe_w2T",
                       "ws1T", "ws3T", "ws2T", "ident"]
STAGE_ALLOC["moe"] = alloc_stage_moe_wrap
STAGE_EMIT["moe"] = emit_moe


# version 07
# speedup vs baseline: 1.2937x; 1.2937x over previous
import math
from collections import ChainMap
from contextlib import ExitStack

import numpy as np
import concourse.bass as bass
import concourse.mybir as mybir
from concourse.bass_utils import run_bass_kernel_spmd

F32 = mybir.dt.float32
BF16 = mybir.dt.bfloat16
AF = mybir.ActivationFunctionType
ALU = mybir.AluOpType
AX = mybir.AxisListType

NCORES = 8
SEQS = 2
T = 2048
D = 1024
NCH = 8
NTT = 4
MAGIC = 1.5 * (2.0 ** 23)
TWO_PI = 2.0 * math.pi
LAMBDA_INIT = 0.8 - 0.6 * math.exp(-0.3 * 12)

ENGINES = ("tensor", "vector", "scalar", "gpsimd", "sync")
ENG_SEM = {"tensor": "pe", "vector": "ve", "scalar": "se", "gpsimd": "ge", "sync": "sy"}
DMA_SEM = {"sync": "dx", "gpsimd": "dw", "scalar": "ds", "vector": "dv", "tensor": "dt"}
NDMA_LANES = 4
DMA_LANE_SEMS = [q + str(l) for q in DMA_SEM.values() for l in range(NDMA_LANES)]


class Buf:
    __slots__ = ("wr", "rd", "name", "excl")

    def __init__(self, name="", excl=False):
        self.wr = None
        self.rd = []
        self.name = name
        self.excl = excl


class Sched:
    def __init__(self):
        self.cmds = {e: [] for e in ENGINES}
        self.count = {}
        self.seen = {e: {} for e in ENGINES}
        self.dma_lane = {}
        self.cur_inst = -1

    def _bump(self, sem, n):
        self.count[sem] = self.count.get(sem, 0) + n
        return self.count[sem]

    def _waits_for(self, eng, reads, writes, waw=True):
        own = ENG_SEM[eng]
        tags = []
        for b in reads:
            if b.wr is not None:
                tags.append(b.wr)
            if b.excl:
                tags.extend(b.rd)
        for b in writes:
            if b.wr is not None and waw:
                tags.append(b.wr)
            tags.extend(b.rd)
        need = {}
        for sem, val, strict in tags:
            if sem == own and not strict:
                continue
            if val > need.get(sem, 0):
                need[sem] = val
        out = []
        for sem, val in need.items():
            if self.seen[eng].get(sem, 0) < val:
                out.append((sem, val))
                self.seen[eng][sem] = val
        return out

    def step(self, eng, fns, reads=(), writes=(), dma=False, tag_writes=True, small=False,
             waw=True):
        if not isinstance(fns, (list, tuple)):
            fns = [fns]
        waits = self._waits_for(eng, reads, writes, waw=waw)
        if dma:
            base = DMA_SEM[eng]
            lane = self.dma_lane.get(base, 0)
            self.dma_lane[base] = (lane + 1) % NDMA_LANES
            sem = base + str(lane)
            prev = self.count.get(sem, 0)
            if prev > 0 and self.seen[eng].get(sem, 0) < prev:
                waits.append((sem, prev))
                self.seen[eng][sem] = prev
            val = self._bump(sem, 16)
            tag = (sem, val, False)
            self.cmds[eng].append((waits, fns, (sem, 16), self.cur_inst))
        else:
            sem = ENG_SEM[eng]
            val = self._bump(sem, 1)
            tag = (sem, val, bool(small))
            self.cmds[eng].append((waits, fns, (sem, 1), self.cur_inst))
        for b in reads:
            b.rd.append(tag)
        if tag_writes:
            for b in writes:
                b.wr = tag
                b.rd = []
        return tag

    def full_barrier(self):
        vals = [(ENG_SEM[e2], self.count.get(ENG_SEM[e2], 0)) for e2 in ENGINES]
        vals += [(s, self.count[s]) for s in DMA_LANE_SEMS if self.count.get(s, 0)]
        for e in ENGINES:
            for sem, v in vals:
                if v > 0 and sem != ENG_SEM[e]:
                    self.barrier_wait(e, (sem, v))

    def barrier_wait(self, eng, tag):
        sem, val = tag
        if self.seen[eng].get(sem, 0) < val:
            self.seen[eng][sem] = val
            self.cmds[eng].append(([(sem, val)], [], None, -1))

    def finalize(self, nc, sems, ctx=None):
        with nc.Block() as block:
            for engname in ENGINES:
                def make(en):
                    def body(eng):
                        for waits, fns, inc, inst in self.cmds[en]:
                            if ctx is not None:
                                ctx.activate(inst)
                            for s, v in waits:
                                eng.wait_ge(sems[s], v)
                            ins = None
                            for f in fns:
                                ins = f(eng)
                            if inc is not None and ins is not None:
                                ins.then_inc(sems[inc[0]], inc[1])
                    return body
                getattr(block, engname)(make(engname))


def _chunks(w):
    din, dout = w.shape
    return np.ascontiguousarray(w.reshape(din // 128, 128, dout))


def _pairswap(w, block):
    dout = w.shape[1]
    idx = np.arange(dout)
    within = idx % block
    sw = idx - within + (within ^ 1)
    return w[:, sw]


def prep_consts():
    c = {}
    RIEMANN = np.array([14.134725142, 21.022039639, 25.01085758, 30.424876126,
                        32.935061588, 37.586178159, 40.918719012, 43.327073281,
                        48.005150881, 49.773832478, 52.970321478, 56.446247697,
                        59.347044003, 60.831778525, 65.112544048, 67.079810529,
                        69.546401711, 72.067157674, 75.704690699, 77.144840069,
                        79.33737502, 82.910380854, 84.735492981, 87.425274613,
                        88.809111208, 92.491899271, 94.651344041, 95.870634228,
                        98.831194218, 101.317851006, 103.72553804, 105.446623052],
                       dtype=np.float32)
    inv_freq = (1.0 / (RIEMANN / RIEMANN[0])).astype(np.float32)
    pos = np.arange(T, dtype=np.float32)
    th = pos[None, :] * inv_freq[:, None]
    c["theta64"] = np.ascontiguousarray(np.repeat(th, 2, axis=0).astype(np.float32))
    sgn = np.tile(np.array([-1.0, 1.0], np.float32), 64).reshape(128, 1)
    c["sign128"] = np.ascontiguousarray(np.concatenate([sgn, sgn], axis=1))
    kk2, qq2 = np.meshgrid(np.arange(128), np.arange(128), indexing="ij")
    c["tril"] = np.where(qq2 >= kk2, 1.0, 0.0).astype(np.float32)
    hd = 32
    inv = (1.0 / 10000.0 ** (np.arange(0, hd, 2, dtype=np.float32) / hd)).astype(np.float32)
    fr = np.outer(np.arange(T, dtype=np.float32), inv)
    cos3 = np.cos(fr).T
    sin3 = np.sin(fr).T
    A32 = np.repeat(cos3, 2, axis=0)
    B32 = np.repeat(sin3, 2, axis=0) * np.tile(np.array([-1.0, 1.0], np.float32), 16)[:, None]
    c["s3_A"] = np.ascontiguousarray(np.tile(A32, (4, 1)).astype(np.float32))
    c["s3_B"] = np.ascontiguousarray(np.tile(B32, (4, 1)).astype(np.float32))
    c["ident"] = np.eye(128, dtype=np.float32)
    i64 = np.eye(64, dtype=np.float32)
    c["ident64d"] = np.ascontiguousarray(np.vstack([i64, i64]))
    kk, qq = np.meshgrid(np.arange(128), np.arange(128), indexing="ij")
    c["diagmask"] = np.where(qq >= kk, 0.0, -1e30).astype(np.float32)
    return c


def prep_inputs(inp, core):
    d = {}
    x = inp["L0_attn_a_x"]
    xs = x[core * SEQS:(core + 1) * SEQS]
    d["xin"] = np.ascontiguousarray(xs.transpose(0, 2, 1)).astype(np.float32)

    wqkv = inp["L0_attn_a_w_qkv"]
    wq, wk, wv = wqkv[:, 0:D], wqkv[:, D:2 * D], wqkv[:, 2 * D:3 * D]
    d["s1_wq"] = _chunks(wq)
    d["s1_wq_sw"] = _chunks(_pairswap(wq, 64))
    d["s1_wk"] = _chunks(wk)
    d["s1_wk_sw"] = _chunks(_pairswap(wk, 64))
    d["s1_wv"] = _chunks(wv)
    d["s1_wo"] = _chunks(inp["L0_attn_a_w_o"])
    d["s1_logscale"] = inp["L0_attn_a_log_scale"].reshape(1, 16).astype(np.float32)

    for nm, key in (("wq", "L0_attn_b_Wq"), ("wk", "L0_attn_b_Wk")):
        wT = np.ascontiguousarray(inp[key].T)
        d[f"s3_{nm}"] = _chunks(wT)
        d[f"s3_{nm}_sw"] = _chunks(_pairswap(wT, 32))
    d["s3_wv"] = _chunks(np.ascontiguousarray(inp["L0_attn_b_Wv"].T))
    d["s3_wo"] = _chunks(np.ascontiguousarray(inp["L0_attn_b_Wo"].T))
    d["s3_lam"] = np.concatenate(
        [inp["L0_attn_b_lambda_q1"], inp["L0_attn_b_lambda_k1"],
         inp["L0_attn_b_lambda_q2"], inp["L0_attn_b_lambda_k2"]]).reshape(1, 128).astype(np.float32)
    d["s3_gamma"] = inp["L0_attn_b_gamma"].reshape(64, 1).astype(np.float32)

    d["gwT"] = np.ascontiguousarray(inp["L0_ffn_gate_w"].T)
    d["gbias"] = inp["L0_ffn_gate_bias"].reshape(1, 16).astype(np.float32)
    w1, w2, w3 = inp["L0_ffn_w1"], inp["L0_ffn_w2"], inp["L0_ffn_w3"]
    d["moe_w1T"] = np.ascontiguousarray(w1.transpose(0, 2, 1).reshape(16, 8, 128, 768))
    d["moe_w3T"] = np.ascontiguousarray(w3.transpose(0, 2, 1).reshape(16, 8, 128, 768))
    d["moe_w2T"] = np.ascontiguousarray(w2.transpose(0, 2, 1).reshape(16, 6, 128, 1024))
    d["ws1T"] = _chunks(np.ascontiguousarray(inp["L0_ffn_ws1"].T))
    d["ws3T"] = _chunks(np.ascontiguousarray(inp["L0_ffn_ws3"].T))
    d["ws2T"] = np.ascontiguousarray(inp["L0_ffn_ws2"].T.reshape(6, 128, 1024))

    wqkvT = np.ascontiguousarray(inp["L1_attn_a_Wqkv"].T)
    d["s5_wq"] = _chunks(np.ascontiguousarray(wqkvT[:, 0:D]))
    d["s5_wk"] = _chunks(np.ascontiguousarray(wqkvT[:, D:2 * D]))
    d["s5_wv"] = _chunks(np.ascontiguousarray(wqkvT[:, 2 * D:3 * D]))
    d["s5_wproj"] = _chunks(np.ascontiguousarray(inp["L1_attn_a_Wproj"].T))

    d["W1c"] = _chunks(inp["L1_ffn_W1"])
    d["W2c"] = _chunks(inp["L1_ffn_W2"])
    d["b1c"] = np.ascontiguousarray(inp["L1_ffn_b1"].reshape(32, 128).T)
    d["b2c"] = np.ascontiguousarray(inp["L1_ffn_b2"].reshape(8, 128).T)
    d["qth"] = np.ascontiguousarray(inp["L1_ffn_quantum_phases"].reshape(32, 128).T)
    d["qwr"] = np.ascontiguousarray(inp["L1_ffn_quantum_weights_real"].reshape(32, 128).T)
    d["qwi"] = np.ascontiguousarray(inp["L1_ffn_quantum_weights_imag"].reshape(32, 128).T)

    d["s7_wq"] = _chunks(np.ascontiguousarray(inp["L1_attn_b_Wq"].T))
    d["s7_wk"] = _chunks(np.ascontiguousarray(inp["L1_attn_b_Wk"].T))
    d["s7_wv"] = _chunks(np.ascontiguousarray(inp["L1_attn_b_Wv"].T))
    d["s7_wu"] = _chunks(np.ascontiguousarray(inp["L1_attn_b_Wu"].T))
    d["s7_bu"] = np.ascontiguousarray(inp["L1_attn_b_bu"].reshape(8, 128).T)

    d.update(prep_consts())
    return {k: np.ascontiguousarray(v).astype(np.float32) for k, v in d.items()}


class Ctx:
    def __init__(self, nc, sch, stack):
        self.nc = nc
        self.sch = sch
        self.stack = stack
        self.common_sb = {}
        self.common_buf = {}
        self.stage_sb = {}
        self.stage_buf = {}
        self.sb = self.common_sb
        self.buf = self.common_buf

    def activate(self, inst):
        self.inst = inst
        if inst < 0:
            self.sb = self.common_sb
            self.buf = self.common_buf
        else:
            sd = self.stage_sb.setdefault(inst, {})
            bd = self.stage_buf.setdefault(inst, {})
            self.sb = ChainMap(sd, self.common_sb)
            self.buf = ChainMap(bd, self.common_buf)

    inst = -1

    def _uname(self, name):
        return name if self.inst < 0 else "%s_i%d" % (name, self.inst)

    def sbuf(self, name, shape, dtype):
        t = self.stack.enter_context(self.nc.sbuf_tensor(self._uname(name), list(shape), dtype))
        self.sb[name] = t
        self.buf[name] = Buf(name)
        return t

    def psum(self, name, shape, dtype):
        t = self.stack.enter_context(self.nc.psum_tensor(self._uname(name), list(shape), dtype))
        self.sb[name] = t
        self.buf[name] = Buf(name, excl=True)
        return t

    def b(self, name):
        return self.buf[name]

    def t(self, name):
        return self.sb[name]


def alloc_common(ctx):
    ctx.sbuf("xbf", [128, NCH, T], BF16)
    for c in range(NCH):
        ctx.buf["xbf_%d" % c] = Buf("xbf_%d" % c)
    ctx.sbuf("sqtt", [128, NCH, 512], BF16)
    ctx.sbuf("xln", [128, NCH, T], BF16)
    ctx.sbuf("xio", [128, 2, 512], F32)
    for s in range(2):
        ctx.buf["xio_%d" % s] = Buf("xio_%d" % s)
    ctx.sbuf("sst", [1, 1536], F32)
    ctx.sbuf("tsub", [128, 2, 512], BF16)
    ctx.sbuf("rbs", [128, 512], BF16)
    ctx.sbuf("onesb", [128, 2], BF16)
    ctx.sbuf("onesrow", [1, 128], BF16)
    ctx.sbuf("epscol", [1, 1], F32)
    ctx.psum("ps_sum", [128, 512], F32)
    ctx.psum("ps_sq", [128, 512], F32)
    ctx.psum("pbc_m", [128, 512], F32)
    ctx.psum("pbc_r", [128, 512], F32)


def emit_init(ctx):
    sch = ctx.sch
    sch.step("vector", lambda e: e.memset(ctx.t("onesb")[:, :], 1.0), writes=(ctx.b("onesb"),), small=True)
    sch.step("vector", lambda e: e.memset(ctx.t("onesrow")[0:1, :], 1.0), writes=(ctx.b("onesrow"),), small=True)
    sch.step("vector", lambda e: e.memset(ctx.t("epscol")[0:1, 0:1], 1e-5), writes=(ctx.b("epscol"),), small=True)


def emit_ln(ctx, xin_ap, seq):
    sch = ctx.sch
    for c in range(NCH):
        sch.step("gpsimd", (lambda c=c: (lambda e: e.dma_start(
            ctx.t("xbf")[:, c, :], xin_ap[seq, c * 128:(c + 1) * 128, :])))(),
            writes=(ctx.b("xbf_%d" % c),), dma=True, waw=(c == 0))
    SUMROW = [("ps_sum", 0), ("ps_sum", 32), ("ps_sum", 64), ("pbc_m", 0)]
    SQROW = [("ps_sq", 0), ("ps_sq", 32), ("ps_sq", 64), ("pbc_r", 0)]
    xbufs = tuple(ctx.b("xbf_%d" % c) for c in range(NCH))
    for tt in range(NTT):
        tn, p = SUMROW[tt]
        for c in range(NCH):
            sch.step("scalar", (lambda c=c, tt=tt: (lambda e: e.activation(
                ctx.t("sqtt")[:, c, :], ctx.t("xbf")[:, c, tt * 512:(tt + 1) * 512], AF.Square)))(),
                reads=(ctx.b("xbf_%d" % c),), writes=(ctx.b("sqtt"),), tag_writes=(c == NCH - 1))
        fns = []
        for c in range(NCH):
            fns.append((lambda c=c, p=p, tt=tt, tn=tn: (lambda e: e.matmul(
                ctx.t(tn)[p:p + 1, :], ctx.t("onesb")[:, 0:1],
                ctx.t("xbf")[:, c, tt * 512:(tt + 1) * 512],
                start=(c == 0), stop=(c == NCH - 1))))())
        sch.step("tensor", fns, reads=xbufs + (ctx.b("onesb"),), writes=(ctx.b(tn),),
                 tag_writes=(tt >= NTT - 2))
        tn2, p2 = SQROW[tt]
        fns2 = []
        for c in range(NCH):
            fns2.append((lambda c=c, p2=p2, tn2=tn2: (lambda e: e.matmul(
                ctx.t(tn2)[p2:p2 + 1, :], ctx.t("onesb")[:, 0:1],
                ctx.t("sqtt")[:, c, :],
                start=(c == 0), stop=(c == NCH - 1))))())
        sch.step("tensor", fns2, reads=(ctx.b("sqtt"), ctx.b("onesb")), writes=(ctx.b(tn2),),
                 tag_writes=(tt >= NTT - 2))
    S = ctx.t("sst")
    SB_ = ctx.b("sst")
    RBF = ctx.t("rows_bf")
    RBFb = ctx.b("rows_bf")
    for tt in range(NTT):
        tn, p = SUMROW[tt]
        tn2, p2 = SQROW[tt]
        sch.step("vector", (lambda p=p, tn=tn: (lambda e: e.tensor_scalar(
            S[0:1, 0:512], ctx.t(tn)[p:p + 1, :], 1.0 / D, None, op0=ALU.mult)))(),
            reads=(ctx.b(tn),), writes=(SB_,), small=True)
        sch.step("vector", (lambda p2=p2, tn2=tn2: (lambda e: e.tensor_scalar(
            S[0:1, 512:1024], ctx.t(tn2)[p2:p2 + 1, :], 1.0 / D, None, op0=ALU.mult)))(),
            reads=(ctx.b(tn2),), writes=(SB_,), small=True)
        sch.step("vector", lambda e: e.tensor_mul(S[0:1, 1024:1536], S[0:1, 0:512], S[0:1, 0:512]),
                 reads=(SB_,), writes=(SB_,), small=True)
        sch.step("vector", lambda e: e.tensor_sub(S[0:1, 512:1024], S[0:1, 512:1024], S[0:1, 1024:1536]),
                 reads=(SB_,), writes=(SB_,), small=True)
        sch.step("scalar", lambda e: e.activation(S[0:1, 1024:1536], S[0:1, 512:1024], AF.Sqrt,
                                                  bias=ctx.t("epscol")[0:1, 0:1], scale=1.0),
                 reads=(SB_, ctx.b("epscol")), writes=(SB_,), small=True)
        sch.step("vector", lambda e: e.reciprocal(S[0:1, 512:1024], S[0:1, 1024:1536]),
                 reads=(SB_,), writes=(SB_,), small=True)
        sch.step("vector", (lambda tt=tt: (lambda e: e.tensor_copy(
            RBF[0:1, 0, tt * 512:(tt + 1) * 512], S[0:1, 0:512])))(),
            reads=(SB_,), writes=(RBFb,), tag_writes=False, small=True)
        sch.step("vector", (lambda tt=tt: (lambda e: e.tensor_copy(
            RBF[0:1, 1, tt * 512:(tt + 1) * 512], S[0:1, 512:1024])))(),
            reads=(SB_,), writes=(RBFb,), small=True)
    for tt in range(NTT):
        sch.step("tensor", (lambda tt=tt: (lambda e: e.matmul(
            ctx.t("pbc_m")[:, :], ctx.t("onesrow")[0:1, :],
            ctx.t("rows_bf")[0:1, 0, tt * 512:(tt + 1) * 512], start=True, stop=True)))(),
            reads=(ctx.b("rows_bf"), ctx.b("onesrow")), writes=(ctx.b("pbc_m"),))
        sch.step("tensor", (lambda tt=tt: (lambda e: e.matmul(
            ctx.t("pbc_r")[:, :], ctx.t("onesrow")[0:1, :],
            ctx.t("rows_bf")[0:1, 1, tt * 512:(tt + 1) * 512], start=True, stop=True)))(),
            reads=(ctx.b("rows_bf"), ctx.b("onesrow")), writes=(ctx.b("pbc_r"),))
        sch.step("scalar", lambda e: e.copy(ctx.t("rbs")[:, :], ctx.t("pbc_r")[:, :]),
                 reads=(ctx.b("pbc_r"),), writes=(ctx.b("rbs"),))
        for c in range(NCH):
            sl = c % 2
            sch.step("vector", (lambda c=c, tt=tt, sl=sl: (lambda e: e.tensor_sub(
                ctx.t("tsub")[:, sl, :], ctx.t("xbf")[:, c, tt * 512:(tt + 1) * 512],
                ctx.t("pbc_m")[:, :])))(),
                reads=(ctx.b("xbf_%d" % c), ctx.b("pbc_m")), writes=(ctx.b("tsub_%d" % sl),))
            sch.step("vector", (lambda c=c, tt=tt, sl=sl: (lambda e: e.tensor_mul(
                ctx.t("xln")[:, c, tt * 512:(tt + 1) * 512], ctx.t("tsub")[:, sl, :],
                ctx.t("rbs")[:, :])))(),
                reads=(ctx.b("tsub_%d" % sl), ctx.b("rbs")), writes=(ctx.b("xln"),),
                tag_writes=(tt == NTT - 1 and c == NCH - 1))


def _ensure_rows_bf(ctx):
    if "rows_bf" not in ctx.sb:
        ctx.sbuf("rows_bf", [1, 2, T], BF16)


def alloc_stage_ffn1(ctx, din):
    _ensure_rows_bf(ctx)
    for nm, w in (("qth", 32), ("qwr", 32), ("qwi", 32), ("b1c", 32), ("b2c", 8)):
        t = ctx.sbuf(nm + "_s", [128, w], F32)
        ctx.sch.step("sync", (lambda t=t, nm=nm: (lambda e: e.dma_start(t[:, :], din[nm])))(),
                     writes=(ctx.b(nm + "_s"),), dma=True)
    ctx.sbuf("qt1", [128, 32], F32)
    ctx.sbuf("qt2", [128, 32], F32)
    ctx.sbuf("qsin", [128, 32], F32)
    ctx.sbuf("qcos", [128, 32], F32)
    ctx.sbuf("qA", [128, 32], F32)
    ctx.sbuf("qbeta", [128, 32], F32)
    ctx.sbuf("halfpicol", [128, 1], F32)
    ctx.sch.step("vector", lambda e: e.memset(ctx.t("halfpicol")[:, 0:1], math.pi / 2),
                 writes=(ctx.b("halfpicol"),), small=True)
    ctx.sbuf("w1s0", [128, 8, 1024], BF16)
    ctx.sbuf("w1s1", [128, 8, 1024], BF16)
    ctx.sbuf("w2s0", [128, 8, 1024], BF16)
    xbf_t = ctx.sb["xbf"]
    ctx.sb["w2s1"] = xbf_t.ap()[:, :, 0:1024]
    w2al = Buf("ffn1_w2alias")
    ctx.buf["w2s1"] = w2al
    for c in range(NCH):
        ctx.buf["xbf_%d" % c] = w2al
    ctx.sbuf("hq", [128, 8, 1024], BF16)
    ctx.sbuf("sinv", [128, 1, 512], BF16)
    ctx.sbuf("qv", [128, 1, 512], F32)
    ctx.sbuf("yf", [128, NCH, 1024], F32)
    ctx.sbuf("fprs", [128, 2, 512], F32)
    for s in range(2):
        ctx.buf["fprs_%d" % s] = Buf()
    ctx.psum("ph0", [128, 512], F32)
    ctx.psum("ph1", [128, 512], F32)
    ctx.psum("py0", [128, 512], F32)
    ctx.psum("py1", [128, 512], F32)


def _wrap_cols(ctx, outname, inname, extra_quarter=0.0):
    sch = ctx.sch
    ot, ob = ctx.t(outname), ctx.b(outname)
    it, ib = ctx.t(inname), ctx.b(inname)
    q1t, q1b = ctx.t("qt1"), ctx.b("qt1")
    sch.step("vector", lambda e: e.tensor_scalar(
        q1t[:, :], it[:, :], 1.0 / TWO_PI, MAGIC + extra_quarter,
        op0=ALU.mult, op1=ALU.add),
        reads=(ib,), writes=(q1b,), small=True)
    sch.step("vector", lambda e: e.tensor_scalar(
        q1t[:, :], q1t[:, :], MAGIC, None, op0=ALU.subtract),
        reads=(q1b,), writes=(q1b,), small=True)
    sch.step("vector", lambda e: e.scalar_tensor_tensor(
        ot[:, :], q1t[:, :], -TWO_PI, it[:, :],
        op0=ALU.mult, op1=ALU.add),
        reads=(q1b, ib), writes=(ob,), small=True)


def emit_stage_ffn1(ctx, din, xin_ap, xout_ap):
    sch = ctx.sch
    _wrap_cols(ctx, "qt2", "qth_s")
    sch.step("scalar", lambda e: e.activation(ctx.t("qsin")[:, :], ctx.t("qt2")[:, :], AF.Sin),
             reads=(ctx.b("qt2"),), writes=(ctx.b("qsin"),), small=True)
    _wrap_cols(ctx, "qt2", "qth_s", extra_quarter=0.25)
    sch.step("scalar", lambda e: e.activation(ctx.t("qcos")[:, :], ctx.t("qt2")[:, :], AF.Sin,
                                              bias=ctx.t("halfpicol")[:, 0:1], scale=1.0),
             reads=(ctx.b("qt2"), ctx.b("halfpicol")), writes=(ctx.b("qcos"),), small=True)
    sch.step("vector", lambda e: e.tensor_mul(ctx.t("qt1")[:, :], ctx.t("qcos")[:, :], ctx.t("qwr_s")[:, :]),
             reads=(ctx.b("qcos"), ctx.b("qwr_s")), writes=(ctx.b("qt1"),), small=True)
    sch.step("vector", lambda e: e.tensor_mul(ctx.t("qt2")[:, :], ctx.t("qsin")[:, :], ctx.t("qwi_s")[:, :]),
             reads=(ctx.b("qsin"), ctx.b("qwi_s")), writes=(ctx.b("qt2"),), small=True)
    sch.step("vector", lambda e: e.tensor_add(ctx.t("qt1")[:, :], ctx.t("qt1")[:, :], ctx.t("qt2")[:, :]),
             reads=(ctx.b("qt1"), ctx.b("qt2")), writes=(ctx.b("qt1"),), small=True)
    sch.step("vector", lambda e: e.tensor_mul(ctx.t("qA")[:, :], ctx.t("qt1")[:, :], ctx.t("qwi_s")[:, :]),
             reads=(ctx.b("qt1"), ctx.b("qwi_s")), writes=(ctx.b("qA"),), small=True)
    sch.step("vector", lambda e: e.tensor_scalar(ctx.t("qA")[:, :], ctx.t("qA")[:, :], 0.1, None, op0=ALU.mult),
             reads=(ctx.b("qA"),), writes=(ctx.b("qA"),), small=True)
    sch.step("vector", lambda e: e.scalar_tensor_tensor(
        ctx.t("qt2")[:, :], ctx.t("b1c_s")[:, :], 0.1, ctx.t("qth_s")[:, :], op0=ALU.mult, op1=ALU.add),
        reads=(ctx.b("b1c_s"), ctx.b("qth_s")), writes=(ctx.b("qt2"),), small=True)
    ctx.sb["qarg"] = ctx.sb["qt2"]
    ctx.buf["qarg"] = ctx.buf["qt2"]
    _wrap_cols(ctx, "qbeta", "qarg")

    for seq in range(SEQS):
        emit_ln(ctx, xin_ap, seq)
        for th in range(2):
            tho = th * 1024
            def f1_dmas(qq):
                ws = qq % 2
                sch.step("gpsimd", (lambda qq=qq, ws=ws: (lambda e: e.dma_start(
                    ctx.t("w1s%d" % ws)[:, :, :],
                    din["W1c"][:, :, qq * 1024:(qq + 1) * 1024].rearrange("c p o -> p c o"))))(),
                    writes=(ctx.b("w1s%d" % ws),), dma=True)
                sch.step("gpsimd", (lambda qq=qq, ws=ws: (lambda e: e.dma_start(
                    ctx.t("w2s%d" % ws)[:, :, :],
                    din["W2c"][qq * 8:(qq + 1) * 8, :, :].rearrange("c p o -> p c o"))))(),
                    writes=(ctx.b("w2s%d" % ws),), dma=True)
            f1_dmas(0)
            for qtr in range(4):
                wslot = qtr % 2
                w1b, w2b = ctx.b("w1s%d" % wslot), ctx.b("w2s%d" % wslot)
                if qtr < 3:
                    f1_dmas(qtr + 1)
                for fc in range(8):
                    gfc = qtr * 8 + fc
                    pb = "ph%d" % (fc % 2)
                    for tt2 in range(2):
                        tsl = slice(tho + tt2 * 512, tho + (tt2 + 1) * 512)
                        sl = tt2
                        fns = []
                        for dc in range(NCH):
                            fns.append((lambda dc=dc, fc=fc, tsl=tsl, pb=pb, wslot=wslot: (lambda e: e.matmul(
                                ctx.t(pb)[:, :], ctx.t("w1s%d" % wslot)[:, dc, fc * 128:(fc + 1) * 128],
                                ctx.t("xln")[:, dc, tsl],
                                start=(dc == 0), stop=(dc == NCH - 1))))())
                        sch.step("tensor", fns, reads=(w1b, ctx.b("xln")), writes=(ctx.b(pb),))
                        sch.step("scalar", (lambda pb=pb, gfc=gfc: (lambda e: e.activation(
                            ctx.t("sinv")[:, 0, :], ctx.t(pb)[:, :], AF.Sin,
                            bias=ctx.t("qbeta")[:, gfc:gfc + 1], scale=0.1)))(),
                            reads=(ctx.b(pb), ctx.b("qbeta")), writes=(ctx.b("sinv"),))
                        sch.step("vector", (lambda pb=pb, gfc=gfc: (lambda e: e.scalar_tensor_tensor(
                            ctx.t("qv")[:, 0, :], ctx.t("sinv")[:, 0, :], ctx.t("qA")[:, gfc:gfc + 1],
                            ctx.t(pb)[:, :], op0=ALU.mult, op1=ALU.add)))(),
                            reads=(ctx.b("sinv"), ctx.b(pb), ctx.b("qA")), writes=(ctx.b("qv"),))
                        sch.step("scalar", (lambda gfc=gfc, fc=fc, tt2=tt2: (lambda e: e.activation(
                            ctx.t("hq")[:, fc, tt2 * 512:(tt2 + 1) * 512], ctx.t("qv")[:, 0, :],
                            AF.Relu, bias=ctx.t("b1c_s")[:, gfc:gfc + 1], scale=1.0)))(),
                            reads=(ctx.b("qv"), ctx.b("b1c_s")), writes=(ctx.b("hq"),),
                            tag_writes=(fc == 7 and tt2 == 1))
                for oc in range(NCH):
                    for tt2 in range(2):
                        pb = "py%d" % ((oc * 2 + tt2) % 2)
                        fns = []
                        for fc in range(8):
                            fns.append((lambda fc=fc, oc=oc, tt2=tt2, pb=pb, wslot=wslot: (lambda e: e.matmul(
                                ctx.t(pb)[:, :], ctx.t("w2s%d" % wslot)[:, fc, oc * 128:(oc + 1) * 128],
                                ctx.t("hq")[:, fc, tt2 * 512:(tt2 + 1) * 512],
                                start=(fc == 0), stop=(fc == 7))))())
                        sch.step("tensor", fns, reads=(w2b, ctx.b("hq")), writes=(ctx.b(pb),))
                        ysl = (lambda oc=oc, tt2=tt2: ctx.t("yf")[:, oc, tt2 * 512:(tt2 + 1) * 512])
                        if qtr == 0:
                            sch.step("scalar", (lambda pb=pb, oc=oc, tt2=tt2: (lambda e: e.copy(
                                ctx.t("yf")[:, oc, tt2 * 512:(tt2 + 1) * 512], ctx.t(pb)[:, :])))(),
                                reads=(ctx.b(pb),), writes=(ctx.b("yf"),), tag_writes=(oc == NCH - 1 and tt2 == 1))
                        elif qtr < 3:
                            sch.step("vector", (lambda pb=pb, oc=oc, tt2=tt2: (lambda e: e.tensor_add(
                                ctx.t("yf")[:, oc, tt2 * 512:(tt2 + 1) * 512], ctx.t(pb)[:, :],
                                ctx.t("yf")[:, oc, tt2 * 512:(tt2 + 1) * 512])))(),
                                reads=(ctx.b(pb), ctx.b("yf")), writes=(ctx.b("yf"),), tag_writes=(oc == NCH - 1 and tt2 == 1))
                        else:
                            sch.step("vector", (lambda pb=pb, oc=oc, tt2=tt2: (lambda e: e.scalar_tensor_tensor(
                                ctx.t("yf")[:, oc, tt2 * 512:(tt2 + 1) * 512], ctx.t(pb)[:, :],
                                ctx.t("b2c_s")[:, oc:oc + 1],
                                ctx.t("yf")[:, oc, tt2 * 512:(tt2 + 1) * 512], op0=ALU.add, op1=ALU.add)))(),
                                reads=(ctx.b(pb), ctx.b("yf"), ctx.b("b2c_s")), writes=(ctx.b("yf"),),
                                tag_writes=(oc == NCH - 1 and tt2 == 1))
            for oc in range(NCH):
                for t2 in range(2):
                    sl = t2
                    sch.step("sync", (lambda oc=oc, seq=seq, sl=sl, tho=tho, t2=t2: (lambda e: e.dma_start(
                        ctx.t("xio")[:, sl, :],
                        xin_ap[seq, oc * 128:(oc + 1) * 128, tho + t2 * 512:tho + (t2 + 1) * 512])))(),
                        writes=(ctx.b("xio_%d" % sl),), dma=True)
                    sch.step("vector", (lambda oc=oc, sl=sl, t2=t2: (lambda e: e.tensor_add(
                        ctx.t("fprs")[:, sl, :],
                        ctx.t("yf")[:, oc, t2 * 512:(t2 + 1) * 512], ctx.t("xio")[:, sl, :])))(),
                        reads=(ctx.b("xio_%d" % sl), ctx.b("yf")), writes=(ctx.b("fprs_%d" % sl),))
                    sch.step("sync", (lambda oc=oc, seq=seq, tho=tho, t2=t2, sl=sl: (lambda e: e.dma_start(
                        xout_ap[seq, oc * 128:(oc + 1) * 128, tho + t2 * 512:tho + (t2 + 1) * 512],
                        ctx.t("fprs")[:, sl, :])))(),
                        reads=(ctx.b("fprs_%d" % sl),), writes=(), dma=True)


STAGE_INPUTS = {"ffn1": ["W1c", "W2c", "b1c", "b2c", "qth", "qwr", "qwi"]}
STAGE_ALLOC = {"ffn1": alloc_stage_ffn1}
STAGE_EMIT = {"ffn1": emit_stage_ffn1}


def build_program(stages, dshapes):
    nc = bass.Bass("TRN2", target_bir_lowering=False)
    din = {}
    din["xin"] = nc.dram_tensor("xin", [SEQS, D, T], F32, kind="ExternalInput").ap()
    xout = nc.dram_tensor("xout", [SEQS, D, T], F32, kind="ExternalOutput").ap()
    needed = set()
    for s in stages:
        needed.update(STAGE_INPUTS[s])
    for nm in sorted(needed):
        shape, dtype = dshapes[nm]
        din[nm] = nc.dram_tensor(nm, list(shape), dtype, kind="ExternalInput").ap()
    xmid = []
    for i in range(min(max(0, len(stages) - 1), 2)):
        xmid.append(nc.dram_tensor("xmid%d" % i, [SEQS, D, T], F32).ap())

    sch = Sched()
    semnames = sorted(set(ENG_SEM.values()) | set(DMA_LANE_SEMS))
    with ExitStack() as stack:
        ctx = Ctx(nc, sch, stack)
        alloc_common(ctx)
        for s in range(2):
            ctx.buf["tsub_%d" % s] = Buf()
        for c in range(NCH):
            ctx.buf["yf_out_%d" % c] = Buf()
        emit_init(ctx)
        cur_in = din["xin"]
        for i, s in enumerate(stages):
            cur_out = xout if i == len(stages) - 1 else xmid[i % 2]
            ctx.activate(i)
            sch.cur_inst = i
            with ExitStack() as sstack:
                old_stack, ctx.stack = ctx.stack, sstack
                STAGE_ALLOC[s](ctx, din)
                STAGE_EMIT[s](ctx, din, cur_in, cur_out)
                ctx.stack = old_stack
            sch.cur_inst = -1
            ctx.activate(-1)
            sch.full_barrier()
            cur_in = cur_out
        fw = [(s, sch.count[s]) for s in DMA_LANE_SEMS if sch.count.get(s, 0)]
        sch.cmds["sync"].append((fw, [], None, -1))
        sems = {}
        for nm in semnames:
            sems[nm] = stack.enter_context(nc.semaphore(nm))
        sch.finalize(nc, sems, ctx=ctx)
    return nc


def input_shapes(prepped):
    return {k: (v.shape, F32) for k, v in prepped.items()}


def run_stages(stages, inputs, x_np=None, per_core=None):
    if per_core is None:
        per_core = [prep_inputs(inputs, core) for core in range(NCORES)]
    if x_np is not None:
        for core in range(NCORES):
            xs = x_np[core * SEQS:(core + 1) * SEQS]
            per_core[core]["xin"] = np.ascontiguousarray(xs.transpose(0, 2, 1)).astype(np.float32)
    dshapes = input_shapes(per_core[0])
    nc = build_program(stages, dshapes)
    needed = set(["xin"])
    for s in stages:
        needed.update(STAGE_INPUTS[s])
    in_maps = [{k: v for k, v in d.items() if k in needed} for d in per_core]
    res = run_bass_kernel_spmd(nc, in_maps, core_ids=list(range(NCORES)))
    outs = []
    for r in res.results:
        outs.append(np.ascontiguousarray(r["xout"].transpose(0, 2, 1)))
    return np.concatenate(outs, axis=0).astype(np.float32)


ORDER = ["attn_a0", "moe", "attn_b0", "moe", "attn_a1", "ffn1", "attn_b1", "ffn1"]


def kernel(**inputs):
    return run_stages(ORDER, inputs)


def alloc_stage_attn(ctx, din, cfg):
    p = cfg["prefix"]
    sch = ctx.sch
    ctx.sbuf(p + "V65", [128, 16, 16 * 65], BF16)
    for sl in range(2):
        ctx.sbuf(p + "QA%d" % sl, [65, T], BF16)
        ctx.sbuf(p + "KA%d" % sl, [65, T], BF16)
    ctx.sbuf(p + "NST", [33, 2, T], BF16)
    if "rows_bf" not in ctx.sb:
        ctx.sb["rows_bf"] = ctx.sb[p + "NST"]
        ctx.buf["rows_bf"] = ctx.buf[p + "NST"]
    ctx.sbuf(p + "KM", [1, 64], F32)
    ctx.sbuf(p + "sqn", [128, 512], BF16)
    ctx.sbuf(p + "pt", [128, 2, 512], BF16)
    ctx.sbuf(p + "prs", [128, 2, 512], F32)
    ctx.sbuf(p + "nbs", [64, 2, 512], BF16)
    ctx.sbuf(p + "rrow", [1, 2, 512], BF16)
    ctx.sbuf(p + "wqo", [128, 2, NCH, 4, 128], BF16)
    ctx.sbuf(p + "wsA", [128, 8, 1024], BF16)
    t = ctx.sbuf(p + "tril", [128, 128], BF16)
    sch.step("gpsimd", (lambda t=t: (lambda e: e.dma_start(t[:, :], din["tril"])))(),
             writes=(ctx.b(p + "tril"),), dma=True)
    if cfg.get("rope"):
        ctx.sb["_din_theta64"] = din["theta64"]
        ctx.sbuf(p + "Atab", [128, T], BF16)
        ctx.sbuf(p + "Btab", [128, T], BF16)
        ctx.sbuf(p + "Wth", [128, T], F32)
        t = ctx.sbuf(p + "sgn", [128, 2], F32)
        sch.step("sync", (lambda t=t: (lambda e: e.dma_start(t[:, :], din["sign128"])))(),
                 writes=(ctx.b(p + "sgn"),), dma=True)
        ctx.sbuf(p + "SCf", [128, 16], F32)
        t = ctx.sbuf(p + "lsrow", [1, 16], F32)
        sch.step("sync", (lambda t=t: (lambda e: e.dma_start(t[0:1, :], din[cfg["logscale"]])))(),
                 writes=(ctx.b(p + "lsrow"),), dma=True)
        ctx.sbuf(p + "onesf", [1, 128], F32)
        sch.step("vector", lambda e: e.memset(ctx.t(p + "onesf")[0:1, :], 1.0),
                 writes=(ctx.b(p + "onesf"),), small=True)
        ctx.sbuf(p + "hpi", [128, 1], F32)
        sch.step("vector", lambda e: e.memset(ctx.t(p + "hpi")[:, 0:1], math.pi / 2),
                 writes=(ctx.b(p + "hpi"),), small=True)
        ctx.sbuf(p + "e1", [128, 512], BF16)
        ctx.sbuf(p + "e2", [128, 512], BF16)
    if cfg.get("bias"):
        t = ctx.sbuf(p + "bia", [128, 8], F32)
        sch.step("sync", (lambda t=t: (lambda e: e.dma_start(t[:, :], din[cfg["bias"]])))(),
                 writes=(ctx.b(p + "bia"),), dma=True)
    ctx.sb[p + "OUT8"] = ctx.sb["xbf"]
    shared = Buf(p + "out8_alias")
    ctx.buf[p + "OUT8"] = shared
    for c in range(NCH):
        ctx.buf["xbf_%d" % c] = shared
    ctx.psum(p + "pA", [128, 512], F32)
    ctx.psum(p + "pB", [128, 512], F32)
    ctx.psum(p + "pC", [128, 512], F32)
    ctx.psum(p + "pD", [128, 512], F32)


def _recip_bf(ctx, e, out_ap, in_ap):
    with ctx.nc.allow_low_precision(reason="bf16 softmax normalizer; bf16 blessed here"):
        return e.reciprocal(out_ap, in_ap)


def _rope_tables(ctx, p, oc):
    sch = ctx.sch
    A, B = ctx.t(p + "Atab"), ctx.t(p + "Btab")
    Ab, Bb = ctx.b(p + "Atab"), ctx.b(p + "Btab")
    W, Wb = ctx.t(p + "Wth"), ctx.b(p + "Wth")
    SC, SCb = ctx.t(p + "SCf"), ctx.b(p + "SCf")
    PR, PRb = ctx.t(p + "prs"), ctx.b(p + "prs")
    din = ctx.sb["_din_theta64"]
    for r in range(2):
        sch.step("sync", (lambda r=r: (lambda e: e.dma_start(W[64 * r:64 * r + 64, :], din)))(),
                 writes=(Wb,), dma=True, tag_writes=(r == 1))
    for r in range(2):
        sch.step("vector", (lambda r=r, oc=oc: (lambda e: e.tensor_scalar(
            W[64 * r:64 * r + 64, :], W[64 * r:64 * r + 64, :],
            SC[0:64, 2 * oc + r:2 * oc + r + 1], None, op0=ALU.mult)))(),
            reads=(SCb,), writes=(Wb,), tag_writes=(r == 1))
    for tt in range(NTT):
        ws = (lambda tt=tt: W[:, tt * 512:(tt + 1) * 512])()
        p0, p1 = PR[:, 0, :], PR[:, 1, :]
        sch.step("vector", (lambda ws=ws, p0=p0: (lambda e: e.tensor_scalar(
            p0, ws, 1.0 / TWO_PI, MAGIC, op0=ALU.mult, op1=ALU.add)))(),
            reads=(Wb,), writes=(PRb,))
        sch.step("vector", (lambda p0=p0: (lambda e: e.tensor_scalar(
            p0, p0, MAGIC, None, op0=ALU.subtract)))(),
            reads=(PRb,), writes=(PRb,))
        sch.step("vector", (lambda ws=ws, p0=p0, p1=p1: (lambda e: e.scalar_tensor_tensor(
            p1, p0, -TWO_PI, ws, op0=ALU.mult, op1=ALU.add)))(),
            reads=(PRb, Wb), writes=(PRb,))
        sch.step("vector", (lambda p0=p0, p1=p1: (lambda e: e.tensor_scalar(
            p0, p1, math.pi / 2, None, op0=ALU.is_gt)))(),
            reads=(PRb,), writes=(PRb,))
        sch.step("vector", (lambda p0=p0, p1=p1: (lambda e: e.scalar_tensor_tensor(
            p0, p0, -TWO_PI, p1, op0=ALU.mult, op1=ALU.add)))(),
            reads=(PRb,), writes=(PRb,))
        sch.step("scalar", (lambda p0=p0, tt=tt: (lambda e: e.activation(
            A[:, tt * 512:(tt + 1) * 512], p0, AF.Sin,
            bias=ctx.t(p + "hpi")[:, 0:1], scale=1.0)))(),
            reads=(PRb, ctx.b(p + "hpi")), writes=(Ab,), tag_writes=(tt == NTT - 1))
        sch.step("scalar", (lambda p0=p0, p1=p1: (lambda e: e.activation(
            p0, p1, AF.Sin)))(),
            reads=(PRb,), writes=(PRb,))
        sch.step("vector", (lambda p0=p0, tt=tt: (lambda e: e.tensor_scalar(
            B[:, tt * 512:(tt + 1) * 512], p0, ctx.t(p + "sgn")[:, 0:1], None, op0=ALU.mult)))(),
            reads=(PRb, ctx.b(p + "sgn")), writes=(Bb,), tag_writes=(tt == NTT - 1))


def _emit_vphase(ctx, din, p, wv_name):
    sch = ctx.sch
    V65 = ctx.t(p + "V65")
    V65b = ctx.b(p + "V65")
    V65r = V65.ap().rearrange("p t (h c) -> p t h c", h=16)
    pA, pB = ctx.t(p + "pA"), ctx.t(p + "pB")
    pAb, pBb = ctx.b(p + "pA"), ctx.b(p + "pB")
    sch.step("gpsimd", lambda e: e.dma_start(
        ctx.t(p + "wsA")[:, :, :], din[wv_name].rearrange("c p o -> p c o")),
        writes=(ctx.b(p + "wsA"),), dma=True)
    sch.step("vector", lambda e: e.memset(V65r[:, :, :, 64:65], 1.0), writes=(V65b,), small=True)
    for tk in range(16):
        for o2 in range(2):
            pb, pbb = (pA, pAb) if (tk * 2 + o2) % 2 == 0 else (pB, pBb)
            fns = []
            for dc in range(NCH):
                fns.append((lambda dc=dc, tk=tk, o2=o2, pb=pb: (lambda e: e.matmul(
                    pb[:, :], ctx.t("xln")[:, dc, tk * 128:(tk + 1) * 128],
                    ctx.t(p + "wsA")[:, dc, o2 * 512:(o2 + 1) * 512],
                    start=(dc == 0), stop=(dc == NCH - 1))))())
            sch.step("tensor", fns, reads=(ctx.b("xln"), ctx.b(p + "wsA")), writes=(pbb,))
            sch.step("vector", (lambda tk=tk, o2=o2, pb=pb: (lambda e: e.tensor_copy(
                V65r[:, tk, o2 * 8:(o2 + 1) * 8, 0:64], pb[:, :])))(),
                reads=(pbb,), writes=(V65b,), tag_writes=(tk == 15 and o2 == 1))


def emit_attn(ctx, din, xin_ap, xout_ap, cfg):
    sch = ctx.sch
    p = cfg["prefix"]
    rope = cfg.get("rope")
    causal = cfg["causal"]
    scale = cfg["scale"]
    V65 = ctx.t(p + "V65")
    V65b = ctx.b(p + "V65")
    V65r = V65.ap().rearrange("p t (h c) -> p t h c", h=16)
    pA, pB, pC, pD = (ctx.t(p + x) for x in ("pA", "pB", "pC", "pD"))
    pAb, pBb, pCb, pDb = (ctx.b(p + x) for x in ("pA", "pB", "pC", "pD"))

    for sl in range(2):
        sch.step("vector", (lambda sl=sl: (lambda e: e.memset(ctx.t(p + "KA%d" % sl)[64:65, :], 1.0)))(),
                 writes=(ctx.b(p + "KA%d" % sl),))
    if rope:
        sch.step("scalar", lambda e: e.activation(ctx.t(p + "lsrow")[0:1, :], ctx.t(p + "lsrow")[0:1, :], AF.Exp),
                 reads=(ctx.b(p + "lsrow"),), writes=(ctx.b(p + "lsrow"),), small=True)
        sch.step("tensor", lambda e: e.matmul(pD[:, 0:16], ctx.t(p + "onesf")[0:1, :],
                                              ctx.t(p + "lsrow")[0:1, :], start=True, stop=True),
                 reads=(ctx.b(p + "lsrow"), ctx.b(p + "onesf")), writes=(pDb,))
        sch.step("vector", lambda e: e.tensor_copy(ctx.t(p + "SCf")[:, :], pD[:, 0:16]),
                 reads=(pDb,), writes=(ctx.b(p + "SCf"),), small=True)

    for seq in range(SEQS):
        emit_ln(ctx, xin_ap, seq)
        _emit_vphase(ctx, din, p, cfg["wv"])
        for oc in range(NCH):
            wsl = oc % 2
            names = ([cfg["wq"], cfg["wq_sw"], cfg["wk"], cfg["wk_sw"]] if rope
                     else [cfg["wq"], cfg["wk"]])
            for wi, wn in enumerate(names):
                sch.step("gpsimd", (lambda wn=wn, wi=wi, oc=oc, wsl=wsl: (lambda e: e.dma_start(
                    ctx.t(p + "wqo")[:, wsl, :, wi, :],
                    din[wn][:, :, oc * 128:(oc + 1) * 128].rearrange("c p o -> p c o"))))(),
                    writes=(ctx.b(p + "wqo"),), dma=True, tag_writes=(wi == len(names) - 1))
            if rope:
                _rope_tables(ctx, p, oc)
            for phase in range(2):
                base_wi = (2 if rope else 1) * phase
                dname = (p + "QA%d") if phase == 0 else (p + "KA%d")
                for tt in range(NTT):
                    fns = []
                    for dc in range(NCH):
                        fns.append((lambda dc=dc, tt=tt, wi=base_wi, wsl=wsl: (lambda e: e.matmul(
                            pA[:, :], ctx.t(p + "wqo")[:, wsl, dc, wi, :],
                            ctx.t("xln")[:, dc, tt * 512:(tt + 1) * 512],
                            start=(dc == 0), stop=(dc == NCH - 1))))())
                    sch.step("tensor", fns, reads=(ctx.b(p + "wqo"), ctx.b("xln")), writes=(pAb,))
                    if rope:
                        fns = []
                        for dc in range(NCH):
                            fns.append((lambda dc=dc, tt=tt, wi=base_wi + 1, wsl=wsl: (lambda e: e.matmul(
                                pB[:, :], ctx.t(p + "wqo")[:, wsl, dc, wi, :],
                                ctx.t("xln")[:, dc, tt * 512:(tt + 1) * 512],
                                start=(dc == 0), stop=(dc == NCH - 1))))())
                        sch.step("tensor", fns, reads=(ctx.b(p + "wqo"), ctx.b("xln")), writes=(pBb,))
                    sch.step("scalar", lambda e: e.activation(ctx.t(p + "sqn")[:, :], pA[:, :], AF.Square),
                             reads=(pAb,), writes=(ctx.b(p + "sqn"),))
                    for r in range(2):
                        sch.step("tensor", (lambda r=r: (lambda e: e.matmul(
                            pD[32 * r:32 * r + 1, :], ctx.t("onesb")[64 * r:64 * r + 64, 0:1],
                            ctx.t(p + "sqn")[64 * r:64 * r + 64, :], start=True, stop=True)))(),
                            reads=(ctx.b(p + "sqn"), ctx.b("onesb")), writes=(pDb,), tag_writes=(r == 1))
                    for r in range(2):
                        sch.step("scalar", (lambda r=r, tt=tt, phase=phase: (lambda e: e.copy(
                            ctx.t(p + "NST")[32 * r:32 * r + 1, phase, tt * 512:(tt + 1) * 512],
                            pD[32 * r:32 * r + 1, :])))(),
                            reads=(pDb,), writes=(ctx.b(p + "NST"),), tag_writes=(r == 1))
                    if rope:
                        sch.step("vector", (lambda tt=tt: (lambda e: e.tensor_mul(
                            ctx.t(p + "e1")[:, :], pA[:, :], ctx.t(p + "Atab")[:, tt * 512:(tt + 1) * 512])))(),
                            reads=(pAb, ctx.b(p + "Atab")), writes=(ctx.b(p + "e1"),))
                        sch.step("vector", (lambda tt=tt: (lambda e: e.tensor_mul(
                            ctx.t(p + "e2")[:, :], pB[:, :], ctx.t(p + "Btab")[:, tt * 512:(tt + 1) * 512])))(),
                            reads=(pBb, ctx.b(p + "Btab")), writes=(ctx.b(p + "e2"),))
                        for r in range(2):
                            sch.step("vector", (lambda r=r, tt=tt, dname=dname: (lambda e: e.tensor_add(
                                ctx.t(dname % r)[0:64, tt * 512:(tt + 1) * 512],
                                ctx.t(p + "e1")[64 * r:64 * r + 64, :],
                                ctx.t(p + "e2")[64 * r:64 * r + 64, :])))(),
                                reads=(ctx.b(p + "e1"), ctx.b(p + "e2")), writes=(ctx.b(dname % r),))
                    else:
                        for r in range(2):
                            sch.step("scalar", (lambda r=r, tt=tt, dname=dname: (lambda e: e.copy(
                                ctx.t(dname % r)[0:64, tt * 512:(tt + 1) * 512],
                                pA[64 * r:64 * r + 64, :])))(),
                                reads=(pAb,), writes=(ctx.b(dname % r),))
            for r in range(2):
                rb = 32 * r
                j = oc * 2 + r
                sch.step("scalar", (lambda rb=rb: (lambda e: e.activation(
                    ctx.t(p + "NST")[rb:rb + 1, 0, :], ctx.t(p + "NST")[rb:rb + 1, 0, :], AF.Sqrt)))(),
                    reads=(ctx.b(p + "NST"),), writes=(ctx.b(p + "NST"),))
                sch.step("vector", (lambda rb=rb, j=j: (lambda e: e.reduce_max(
                    ctx.t(p + "KM")[0:1, j:j + 1], ctx.t(p + "NST")[rb:rb + 1, 1, :], axis=AX.X)))(),
                    reads=(ctx.b(p + "NST"),), writes=(ctx.b(p + "KM"),), small=True)
                sch.step("scalar", (lambda j=j: (lambda e: e.activation(
                    ctx.t(p + "KM")[0:1, j + 32:j + 33], ctx.t(p + "KM")[0:1, j:j + 1], AF.Sqrt)))(),
                    reads=(ctx.b(p + "KM"),), writes=(ctx.b(p + "KM"),), small=True)
                sch.step("vector", (lambda rb=rb, j=j, r=r: (lambda e: e.tensor_scalar(
                    ctx.t(p + "QA%d" % r)[64:65, :], ctx.t(p + "NST")[rb:rb + 1, 0, :],
                    ctx.t(p + "KM")[0:1, j + 32:j + 33], -1.0, op0=ALU.mult, op1=ALU.mult)))(),
                    reads=(ctx.b(p + "NST"), ctx.b(p + "KM")), writes=(ctx.b(p + "QA%d" % r),))
            for qt in range(NTT):
                kts = list(range(4 * qt + 4 if causal else 16))
                for ki, kt in enumerate(kts):
                    off = max(0, kt * 128 - qt * 512) if causal else 0
                    for r in range(2):
                        h = 2 * oc + r
                        QA, QAb = ctx.t(p + "QA%d" % r), ctx.b(p + "QA%d" % r)
                        KA, KAb = ctx.t(p + "KA%d" % r), ctx.b(p + "KA%d" % r)
                        psb, psbb = (pA, pAb) if r == 0 else (pB, pBb)
                        pvp, pvpb = (pC, pCb) if r == 0 else (pD, pDb)
                        sch.step("tensor", (lambda psb=psb, kt=kt, qt=qt, off=off, QA=QA, KA=KA: (lambda e: e.matmul(
                            psb[:, off:512], KA[:, kt * 128:(kt + 1) * 128],
                            QA[:, qt * 512 + off:(qt + 1) * 512], start=True, stop=True)))(),
                            reads=(QAb, KAb), writes=(psbb,))
                        sch.step("scalar", (lambda psb=psb, r=r, off=off: (lambda e: e.activation(
                            ctx.t(p + "pt")[:, r, off:512], psb[:, off:512], AF.Exp, scale=scale)))(),
                            reads=(psbb,), writes=(ctx.b(p + "pt"),))
                        if causal and kt >= 4 * qt:
                            sch.step("vector", (lambda r=r, off=off: (lambda e: e.tensor_mul(
                                ctx.t(p + "pt")[:, r, off:off + 128],
                                ctx.t(p + "pt")[:, r, off:off + 128],
                                ctx.t(p + "tril")[:, :])))(),
                                reads=(ctx.b(p + "pt"), ctx.b(p + "tril")), writes=(ctx.b(p + "pt"),))
                        sch.step("tensor", (lambda kt=kt, r=r, off=off, h=h, pvp=pvp, first=(ki == 0), last=(ki == len(kts) - 1): (lambda e: e.matmul(
                            pvp[0:65, off:512], V65[:, kt, h * 65:h * 65 + 65],
                            ctx.t(p + "pt")[:, r, off:512], start=first, stop=last)))(),
                            reads=(V65b, ctx.b(p + "pt")), writes=(pvpb,), tag_writes=(ki == len(kts) - 1))
                for r in range(2):
                    h = 2 * oc + r
                    pvp, pvpb = (pC, pCb) if r == 0 else (pD, pDb)
                    sch.step("vector", (lambda pvp=pvp, r=r: (lambda e: _recip_bf(
                        ctx, e, ctx.t(p + "rrow")[0:1, r, :], pvp[64:65, :])))(),
                        reads=(pvpb,), writes=(ctx.b(p + "rrow"),))
                    sch.step("tensor", (lambda r=r: (lambda e: e.matmul(
                        pA[0:64, :], ctx.t("onesrow")[0:1, 0:64],
                        ctx.t(p + "rrow")[0:1, r, :], start=True, stop=True)))(),
                        reads=(ctx.b(p + "rrow"), ctx.b("onesrow")), writes=(pAb,))
                    sch.step("scalar", (lambda r=r: (lambda e: e.copy(
                        ctx.t(p + "nbs")[:, r, :], pA[0:64, :])))(),
                        reads=(pAb,), writes=(ctx.b(p + "nbs"),))
                    sch.step("vector", (lambda h=h, qt=qt, pvp=pvp, pvpb=pvpb, r=r: (lambda e: e.tensor_mul(
                        ctx.t(p + "OUT8")[(h % 2) * 64:(h % 2) * 64 + 64, h // 2, qt * 512:(qt + 1) * 512],
                        pvp[0:64, :], ctx.t(p + "nbs")[:, r, :])))(),
                        reads=(pvpb, ctx.b(p + "nbs")), writes=(ctx.b(p + "OUT8"),),
                        tag_writes=(qt == NTT - 1 and oc == NCH - 1 and r == 1))
        _emit_proj(ctx, din, p, xin_ap, xout_ap, seq, cfg["wo"], cfg.get("bias"))


def _emit_proj(ctx, din, p, xin_ap, xout_ap, seq, wo_name, bias):
    sch = ctx.sch
    second = (p + "pB") if (p + "pB") in ctx.sb else (p + "pC")
    pA, pB = ctx.t(p + "pA"), ctx.t(second)
    pAb, pBb = ctx.b(p + "pA"), ctx.b(second)
    if True:
        sch.step("gpsimd", lambda e: e.dma_start(
            ctx.t(p + "wsA")[:, :, :], din[wo_name].rearrange("c p o -> p c o")),
            writes=(ctx.b(p + "wsA"),), dma=True)
        for oc2 in range(NCH):
            for tt in range(NTT):
                sl = (oc2 * NTT + tt) % 2
                sch.step("sync", (lambda oc2=oc2, seq=seq, sl=sl, tt=tt: (lambda e: e.dma_start(
                    ctx.t("xio")[:, sl, :],
                    xin_ap[seq, oc2 * 128:(oc2 + 1) * 128, tt * 512:(tt + 1) * 512])))(),
                    writes=(ctx.b("xio_%d" % sl),), dma=True)
                pb, pbb = (pA, pAb) if tt % 2 == 0 else (pB, pBb)
                fns = []
                for c in range(NCH):
                    fns.append((lambda c=c, oc2=oc2, tt=tt, pb=pb: (lambda e: e.matmul(
                        pb[:, :], ctx.t(p + "wsA")[:, c, oc2 * 128:(oc2 + 1) * 128],
                        ctx.t(p + "OUT8")[:, c, tt * 512:(tt + 1) * 512],
                        start=(c == 0), stop=(c == NCH - 1))))())
                sch.step("tensor", fns, reads=(ctx.b(p + "wsA"), ctx.b(p + "OUT8")), writes=(pbb,))
                psl = tt % 2
                if bias:
                    sch.step("vector", (lambda pb=pb, oc2=oc2, tt=tt, sl=sl, psl=psl: (lambda e: e.scalar_tensor_tensor(
                        ctx.t(p + "prs")[:, psl, :], pb[:, :], ctx.t(p + "bia")[:, oc2:oc2 + 1],
                        ctx.t("xio")[:, sl, :], op0=ALU.add, op1=ALU.add)))(),
                        reads=(pbb, ctx.b(p + "bia"), ctx.b("xio_%d" % sl)), writes=(ctx.b(p + "prs"),))
                else:
                    sch.step("vector", (lambda pb=pb, oc2=oc2, tt=tt, sl=sl, psl=psl: (lambda e: e.tensor_add(
                        ctx.t(p + "prs")[:, psl, :], pb[:, :],
                        ctx.t("xio")[:, sl, :])))(),
                        reads=(pbb, ctx.b("xio_%d" % sl)), writes=(ctx.b(p + "prs"),))
                sch.step("sync", (lambda oc2=oc2, tt=tt, seq=seq, psl=psl: (lambda e: e.dma_start(
                    xout_ap[seq, oc2 * 128:(oc2 + 1) * 128, tt * 512:(tt + 1) * 512],
                    ctx.t(p + "prs")[:, psl, :])))(),
                    reads=(ctx.b(p + "prs"),), writes=(), dma=True)


CFG_ATTN_A0 = dict(prefix="a0", rope=True, causal=True, scale=0.125,
                   wq="s1_wq", wq_sw="s1_wq_sw", wk="s1_wk", wk_sw="s1_wk_sw",
                   wv="s1_wv", wo="s1_wo", logscale="s1_logscale", bias=None)
CFG_ATTN_B1 = dict(prefix="b1", rope=False, causal=False, scale=1.0 / 32.0,
                   wq="s7_wq", wk="s7_wk", wv="s7_wv", wo="s7_wu", bias="s7_bu")

STAGE_INPUTS["attn_a0"] = ["s1_wq", "s1_wq_sw", "s1_wk", "s1_wk_sw", "s1_wv", "s1_wo",
                           "s1_logscale", "theta64", "sign128", "tril"]
STAGE_INPUTS["attn_b1"] = ["s7_wq", "s7_wk", "s7_wv", "s7_wu", "s7_bu", "tril"]
STAGE_ALLOC["attn_a0"] = lambda ctx, din: alloc_stage_attn(ctx, din, CFG_ATTN_A0)
STAGE_ALLOC["attn_b1"] = lambda ctx, din: alloc_stage_attn(ctx, din, CFG_ATTN_B1)
STAGE_EMIT["attn_a0"] = lambda ctx, din, xi, xo: emit_attn(ctx, din, xi, xo, CFG_ATTN_A0)
STAGE_EMIT["attn_b1"] = lambda ctx, din, xi, xo: emit_attn(ctx, din, xi, xo, CFG_ATTN_B1)


def alloc_stage_attn_b0(ctx, din):
    p = "b0"
    sch = ctx.sch
    ctx.sbuf(p + "V65", [128, 16, 16 * 65], BF16)
    for j in range(4):
        ctx.sbuf(p + "QA%d" % j, [33, T], BF16)
        ctx.sbuf(p + "KA%d" % j, [33, T], BF16)
    ctx.sbuf(p + "NST", [33, 2, T], BF16)
    if "rows_bf" not in ctx.sb:
        ctx.sb["rows_bf"] = ctx.sb[p + "NST"]
        ctx.buf["rows_bf"] = ctx.buf[p + "NST"]
    ctx.sbuf(p + "KM", [1, 64], F32)
    ctx.sbuf(p + "sqn", [128, 512], BF16)
    ctx.sbuf(p + "pt", [128, 2, 512], BF16)
    ctx.sbuf(p + "prs", [128, 2, 512], F32)
    ctx.sbuf(p + "nbs", [64, 3, 512], BF16)
    ctx.sbuf(p + "rrow", [1, 3, 512], BF16)
    ctx.sbuf(p + "srw", [1, 512], F32)
    wp = ctx.sbuf(p + "wpool", [128, 8192], BF16)
    ctx.sb[p + "wsA"] = wp.ap().rearrange("p (c o) -> p c o", c=8)
    ctx.buf[p + "wsA"] = ctx.buf[p + "wpool"]
    ctx.sb[p + "wqo"] = wp.ap().rearrange("p (s c w o) -> p s c w o", s=2, c=8, w=4)
    ctx.buf[p + "wqo"] = ctx.buf[p + "wpool"]
    t = ctx.sbuf(p + "tril", [128, 128], BF16)
    sch.step("gpsimd", (lambda t=t: (lambda e: e.dma_start(t[:, :], din["tril"])))(),
             writes=(ctx.b(p + "tril"),), dma=True)
    t = ctx.sbuf(p + "A3", [128, T], BF16)
    sch.step("gpsimd", (lambda t=t: (lambda e: e.dma_start(t[:, :], din["s3_A"])))(),
             writes=(ctx.b(p + "A3"),), dma=True)
    t = ctx.sbuf(p + "B3", [128, T], BF16)
    sch.step("gpsimd", (lambda t=t: (lambda e: e.dma_start(t[:, :], din["s3_B"])))(),
             writes=(ctx.b(p + "B3"),), dma=True)
    t = ctx.sbuf(p + "lam", [1, 128], F32)
    sch.step("sync", (lambda t=t: (lambda e: e.dma_start(t[0:1, :], din["s3_lam"])))(),
             writes=(ctx.b(p + "lam"),), dma=True)
    t = ctx.sbuf(p + "gam", [64, 1], F32)
    sch.step("sync", (lambda t=t: (lambda e: e.dma_start(t[:, :], din["s3_gamma"])))(),
             writes=(ctx.b(p + "gam"),), dma=True)
    ctx.sbuf(p + "lsc", [1, 8], F32)
    t = ctx.sbuf(p + "bc3", [128, 1], BF16)
    sch.step("vector", (lambda t=t: (lambda e: e.memset(t[:, 0:1], 0.0)))(),
             writes=(ctx.b(p + "bc3"),), small=True)
    sch.step("vector", (lambda t=t: (lambda e: e.memset(t[96:128, 0:1], 1.0)))(),
             writes=(ctx.b(p + "bc3"),), small=True)
    ctx.sbuf(p + "sq64", [64, 512], BF16)
    ctx.sbuf(p + "wvs", [64, 2, 512], F32)
    ctx.sbuf(p + "e1", [128, 512], BF16)
    ctx.sbuf(p + "e2", [128, 512], BF16)
    ctx.sb[p + "OUT8"] = ctx.sb["xbf"]
    shared = Buf(p + "out8_alias")
    ctx.buf[p + "OUT8"] = shared
    for c in range(NCH):
        ctx.buf["xbf_%d" % c] = shared
    ctx.psum(p + "pA", [128, 512], F32)
    ctx.psum(p + "pB", [128, 512], F32)
    ctx.psum(p + "pC", [128, 512], F32)
    ctx.psum(p + "pD", [128, 512], F32)


def emit_attn_b0(ctx, din, xin_ap, xout_ap):
    p = "b0"
    sch = ctx.sch
    scale = 1.0 / math.sqrt(32.0)
    V65 = ctx.t(p + "V65")
    V65b = ctx.b(p + "V65")
    pA, pB, pC, pD = (ctx.t(p + x) for x in ("pA", "pB", "pC", "pD"))
    pAb, pBb, pCb, pDb = (ctx.b(p + x) for x in ("pA", "pB", "pC", "pD"))
    lsc, lscb = ctx.t(p + "lsc"), ctx.b(p + "lsc")
    lam, lamb = ctx.t(p + "lam"), ctx.b(p + "lam")

    sch.step("vector", lambda e: e.tensor_mul(lam[0:1, 0:32], lam[0:1, 0:32], lam[0:1, 32:64]),
             reads=(lamb,), writes=(lamb,), small=True)
    sch.step("vector", lambda e: e.tensor_mul(lam[0:1, 64:96], lam[0:1, 64:96], lam[0:1, 96:128]),
             reads=(lamb,), writes=(lamb,), small=True)
    sch.step("vector", lambda e: e.reduce_sum(lsc[0:1, 0:1], lam[0:1, 0:32], axis=AX.X),
             reads=(lamb,), writes=(lscb,), small=True)
    sch.step("vector", lambda e: e.reduce_sum(lsc[0:1, 1:2], lam[0:1, 64:96], axis=AX.X),
             reads=(lamb,), writes=(lscb,), small=True)
    sch.step("scalar", lambda e: e.activation(lsc[0:1, 2:4], lsc[0:1, 0:2], AF.Exp),
             reads=(lscb,), writes=(lscb,), small=True)
    sch.step("vector", lambda e: e.tensor_sub(lsc[0:1, 4:5], lsc[0:1, 2:3], lsc[0:1, 3:4]),
             reads=(lscb,), writes=(lscb,), small=True)
    sch.step("vector", lambda e: e.tensor_scalar(lsc[0:1, 5:6], lsc[0:1, 4:5], LAMBDA_INIT, -1.0,
                                                 op0=ALU.add, op1=ALU.mult),
             reads=(lscb,), writes=(lscb,), small=True)
    sch.step("vector", lambda e: e.tensor_scalar(ctx.t(p + "gam")[:, 0:1], ctx.t(p + "gam")[:, 0:1],
                                                 1.0 - LAMBDA_INIT, None, op0=ALU.mult),
             reads=(ctx.b(p + "gam"),), writes=(ctx.b(p + "gam"),), small=True)
    for j in range(4):
        sch.step("vector", (lambda j=j: (lambda e: e.memset(ctx.t(p + "KA%d" % j)[32:33, :], 1.0)))(),
                 writes=(ctx.b(p + "KA%d" % j),))

    for seq in range(SEQS):
        emit_ln(ctx, xin_ap, seq)
        _emit_vphase(ctx, din, p, "s3_wv")
        for oc in range(NCH):
            wsl = oc % 2
            for wi, wn in enumerate(["s3_wq", "s3_wq_sw", "s3_wk", "s3_wk_sw"]):
                sch.step("gpsimd", (lambda wn=wn, wi=wi, oc=oc, wsl=wsl: (lambda e: e.dma_start(
                    ctx.t(p + "wqo")[:, wsl, :, wi, :],
                    din[wn][:, :, oc * 128:(oc + 1) * 128].rearrange("c p o -> p c o"))))(),
                    writes=(ctx.b(p + "wqo"),), dma=True, tag_writes=(wi == 3))
            sch.step("vector", lambda e: e.memset(ctx.t(p + "KM")[0:1, 0:4], 0.0),
                     reads=(), writes=(ctx.b(p + "KM"),), small=True)
            for phase in range(2):
                dname = (p + "QA%d") if phase == 0 else (p + "KA%d")
                for tt in range(NTT):
                    fns = []
                    for dc in range(NCH):
                        fns.append((lambda dc=dc, tt=tt, wi=2 * phase, wsl=wsl: (lambda e: e.matmul(
                            pA[:, :], ctx.t(p + "wqo")[:, wsl, dc, wi, :],
                            ctx.t("xln")[:, dc, tt * 512:(tt + 1) * 512],
                            start=(dc == 0), stop=(dc == NCH - 1))))())
                    sch.step("tensor", fns, reads=(ctx.b(p + "wqo"), ctx.b("xln")), writes=(pAb,))
                    fns = []
                    for dc in range(NCH):
                        fns.append((lambda dc=dc, tt=tt, wi=2 * phase + 1, wsl=wsl: (lambda e: e.matmul(
                            pB[:, :], ctx.t(p + "wqo")[:, wsl, dc, wi, :],
                            ctx.t("xln")[:, dc, tt * 512:(tt + 1) * 512],
                            start=(dc == 0), stop=(dc == NCH - 1))))())
                    sch.step("tensor", fns, reads=(ctx.b(p + "wqo"), ctx.b("xln")), writes=(pBb,))
                    sch.step("scalar", lambda e: e.activation(ctx.t(p + "sqn")[:, :], pA[:, :], AF.Square),
                             reads=(pAb,), writes=(ctx.b(p + "sqn"),))
                    NR = [(pD, pDb, 0), (pC, pCb, 0), (pD, pDb, 32), (pC, pCb, 32)]
                    for j in range(4):
                        tgt, tgtb, row = NR[j]
                        if j < 3:
                            sch.step("tensor", (lambda j=j, tgt=tgt, row=row: (lambda e: e.matmul(
                                tgt[row:row + 1, :], ctx.t("onesb")[32 * j:32 * j + 32, 0:1],
                                ctx.t(p + "sqn")[32 * j:32 * j + 32, :], start=True, stop=True)))(),
                                reads=(ctx.b(p + "sqn"), ctx.b("onesb")), writes=(tgtb,),
                                tag_writes=(j == 2))
                        else:
                            sch.step("tensor", (lambda tgt=tgt, row=row: (lambda e: e.matmul(
                                tgt[row:row + 1, :], ctx.t(p + "bc3")[:, 0:1],
                                ctx.t(p + "sqn")[:, :], start=True, stop=True)))(),
                                reads=(ctx.b(p + "sqn"), ctx.b(p + "bc3")), writes=(tgtb,))
                    if phase == 0:
                        for j in range(4):
                            tgt, tgtb, row = NR[j]
                            sch.step("scalar", (lambda j=j, tgt=tgt, row=row, tt=tt: (lambda e: e.copy(
                                ctx.t(p + "NST")[32 * (j // 2):32 * (j // 2) + 1,
                                                 j % 2, tt * 512:(tt + 1) * 512],
                                tgt[row:row + 1, :])))(),
                                reads=(tgtb,), writes=(ctx.b(p + "NST"),), tag_writes=(j == 3))
                    else:
                        for j in range(4):
                            tgt, tgtb, row = NR[j]
                            sch.step("vector", (lambda j=j, tgt=tgt, row=row: (lambda e: e.reduce_max(
                                ctx.t(p + "KM")[0:1, 8 + j:9 + j], tgt[row:row + 1, :], axis=AX.X)))(),
                                reads=(tgtb,), writes=(ctx.b(p + "KM"),), small=True)
                            sch.step("vector", (lambda j=j: (lambda e: e.tensor_max(
                                ctx.t(p + "KM")[0:1, j:j + 1], ctx.t(p + "KM")[0:1, j:j + 1],
                                ctx.t(p + "KM")[0:1, 8 + j:9 + j])))(),
                                reads=(ctx.b(p + "KM"),), writes=(ctx.b(p + "KM"),), small=True)
                    sch.step("vector", (lambda tt=tt: (lambda e: e.tensor_mul(
                        ctx.t(p + "e1")[:, :], pA[:, :], ctx.t(p + "A3")[:, tt * 512:(tt + 1) * 512])))(),
                        reads=(pAb, ctx.b(p + "A3")), writes=(ctx.b(p + "e1"),))
                    sch.step("vector", (lambda tt=tt: (lambda e: e.tensor_mul(
                        ctx.t(p + "e2")[:, :], pB[:, :], ctx.t(p + "B3")[:, tt * 512:(tt + 1) * 512])))(),
                        reads=(pBb, ctx.b(p + "B3")), writes=(ctx.b(p + "e2"),))
                    for j in range(4):
                        sch.step("vector", (lambda j=j, tt=tt, dname=dname: (lambda e: e.tensor_add(
                            ctx.t(dname % j)[0:32, tt * 512:(tt + 1) * 512],
                            ctx.t(p + "e1")[32 * j:32 * j + 32, :],
                            ctx.t(p + "e2")[32 * j:32 * j + 32, :])))(),
                            reads=(ctx.b(p + "e1"), ctx.b(p + "e2")), writes=(ctx.b(dname % j),))
            for j in range(4):
                rb = 32 * (j // 2)
                pl = j % 2
                sch.step("scalar", (lambda rb=rb, pl=pl: (lambda e: e.activation(
                    ctx.t(p + "NST")[rb:rb + 1, pl, :], ctx.t(p + "NST")[rb:rb + 1, pl, :], AF.Sqrt)))(),
                    reads=(ctx.b(p + "NST"),), writes=(ctx.b(p + "NST"),))
                sch.step("scalar", (lambda j=j: (lambda e: e.activation(
                    ctx.t(p + "KM")[0:1, 16 + j:17 + j],
                    ctx.t(p + "KM")[0:1, j:j + 1], AF.Sqrt)))(),
                    reads=(ctx.b(p + "KM"),), writes=(ctx.b(p + "KM"),), small=True)
                sch.step("vector", (lambda rb=rb, pl=pl, j=j: (lambda e: e.tensor_scalar(
                    ctx.t(p + "QA%d" % j)[32:33, :], ctx.t(p + "NST")[rb:rb + 1, pl, :],
                    ctx.t(p + "KM")[0:1, 16 + j:17 + j], -1.0, op0=ALU.mult, op1=ALU.mult)))(),
                    reads=(ctx.b(p + "NST"), ctx.b(p + "KM")), writes=(ctx.b(p + "QA%d" % j),))
            for r in range(2):
                h = 2 * oc + r
                for qt in range(NTT):
                    kts = list(range(4 * qt + 4))
                    for ki, kt in enumerate(kts):
                        off = max(0, kt * 128 - qt * 512)
                        for br in range(2):
                            j = 2 * r + br
                            QA, QAb = ctx.t(p + "QA%d" % j), ctx.b(p + "QA%d" % j)
                            KA, KAb = ctx.t(p + "KA%d" % j), ctx.b(p + "KA%d" % j)
                            psb, psbb = (pA, pAb) if br == 0 else (pB, pBb)
                            pvp, pvpb = (pC, pCb) if br == 0 else (pD, pDb)
                            sch.step("tensor", (lambda psb=psb, kt=kt, qt=qt, off=off, QA=QA, KA=KA: (lambda e: e.matmul(
                                psb[:, off:512], KA[:, kt * 128:(kt + 1) * 128],
                                QA[:, qt * 512 + off:(qt + 1) * 512], start=True, stop=True)))(),
                                reads=(QAb, KAb), writes=(psbb,))
                            sch.step("scalar", (lambda psb=psb, br=br, off=off: (lambda e: e.activation(
                                ctx.t(p + "pt")[:, br, off:512], psb[:, off:512], AF.Exp, scale=scale)))(),
                                reads=(psbb,), writes=(ctx.b(p + "pt"),))
                            if kt >= 4 * qt:
                                sch.step("vector", (lambda br=br, off=off: (lambda e: e.tensor_mul(
                                    ctx.t(p + "pt")[:, br, off:off + 128],
                                    ctx.t(p + "pt")[:, br, off:off + 128],
                                    ctx.t(p + "tril")[:, :])))(),
                                    reads=(ctx.b(p + "pt"), ctx.b(p + "tril")), writes=(ctx.b(p + "pt"),))
                            sch.step("tensor", (lambda kt=kt, br=br, off=off, h=h, pvp=pvp, first=(ki == 0), last=(ki == len(kts) - 1): (lambda e: e.matmul(
                                pvp[0:65, off:512], V65[:, kt, h * 65:h * 65 + 65],
                                ctx.t(p + "pt")[:, br, off:512], start=first, stop=last)))(),
                                reads=(V65b, ctx.b(p + "pt")), writes=(pvpb,), tag_writes=(ki == len(kts) - 1))
                    A1, A1b = pC, pCb
                    sch.step("vector", lambda e: _recip_bf(ctx, e, ctx.t(p + "rrow")[0:1, 0, :], A1[64:65, :]),
                             reads=(A1b,), writes=(ctx.b(p + "rrow"),))
                    sch.step("vector", lambda e: _recip_bf(ctx, e, ctx.t(p + "rrow")[0:1, 1, :], pD[64:65, :]),
                             reads=(pDb,), writes=(ctx.b(p + "rrow"),))
                    sch.step("vector", lambda e: e.tensor_scalar(
                        ctx.t(p + "rrow")[0:1, 1, :], ctx.t(p + "rrow")[0:1, 1, :],
                        lsc[0:1, 5:6], None, op0=ALU.mult),
                        reads=(ctx.b(p + "rrow"), lscb), writes=(ctx.b(p + "rrow"),), small=True)
                    sch.step("tensor", lambda e: e.matmul(pA[0:64, :], ctx.t("onesrow")[0:1, 0:64],
                                                          ctx.t(p + "rrow")[0:1, 0, :], start=True, stop=True),
                             reads=(ctx.b(p + "rrow"), ctx.b("onesrow")), writes=(pAb,))
                    sch.step("scalar", lambda e: e.copy(ctx.t(p + "nbs")[:, 0, :], pA[0:64, :]),
                             reads=(pAb,), writes=(ctx.b(p + "nbs"),))
                    sch.step("tensor", lambda e: e.matmul(pA[0:64, :], ctx.t("onesrow")[0:1, 0:64],
                                                          ctx.t(p + "rrow")[0:1, 1, :], start=True, stop=True),
                             reads=(ctx.b(p + "rrow"), ctx.b("onesrow")), writes=(pAb,))
                    sch.step("scalar", lambda e: e.copy(ctx.t(p + "nbs")[:, 1, :], pA[0:64, :]),
                             reads=(pAb,), writes=(ctx.b(p + "nbs"),))
                    sch.step("vector", lambda e: e.tensor_mul(ctx.t(p + "wvs")[:, 0, :],
                                                              A1[0:64, :], ctx.t(p + "nbs")[:, 0, :]),
                             reads=(A1b, ctx.b(p + "nbs")), writes=(ctx.b(p + "wvs"),))
                    sch.step("vector", lambda e: e.tensor_mul(ctx.t(p + "wvs")[:, 1, :],
                                                              pD[0:64, :], ctx.t(p + "nbs")[:, 1, :]),
                             reads=(pDb, ctx.b(p + "nbs")), writes=(ctx.b(p + "wvs"),))
                    sch.step("vector", lambda e: e.tensor_add(ctx.t(p + "wvs")[:, 0, :],
                                                              ctx.t(p + "wvs")[:, 0, :], ctx.t(p + "wvs")[:, 1, :]),
                             reads=(ctx.b(p + "wvs"),), writes=(ctx.b(p + "wvs"),))
                    sch.step("scalar", lambda e: e.activation(ctx.t(p + "sq64")[:, :],
                                                              ctx.t(p + "wvs")[:, 0, :], AF.Square),
                             reads=(ctx.b(p + "wvs"),), writes=(ctx.b(p + "sq64"),))
                    sch.step("tensor", lambda e: e.matmul(pA[0:1, :], ctx.t("onesb")[0:64, 0:1],
                                                          ctx.t(p + "sq64")[:, :], start=True, stop=True),
                             reads=(ctx.b(p + "sq64"), ctx.b("onesb")), writes=(pAb,))
                    sch.step("vector", lambda e: e.tensor_scalar(ctx.t(p + "srw")[0:1, :], pA[0:1, :],
                                                                 1.0 / 64.0, None, op0=ALU.mult),
                             reads=(pAb,), writes=(ctx.b(p + "srw"),))
                    sch.step("scalar", lambda e: e.activation(ctx.t(p + "srw")[0:1, :], ctx.t(p + "srw")[0:1, :],
                                                              AF.Sqrt, bias=ctx.t("epscol")[0:1, 0:1], scale=1.0),
                             reads=(ctx.b(p + "srw"), ctx.b("epscol")), writes=(ctx.b(p + "srw"),))
                    sch.step("vector", lambda e: _recip_bf(ctx, e, ctx.t(p + "rrow")[0:1, 2, :], ctx.t(p + "srw")[0:1, :]),
                             reads=(ctx.b(p + "srw"),), writes=(ctx.b(p + "rrow"),))
                    sch.step("tensor", lambda e: e.matmul(pD[0:64, :], ctx.t("onesrow")[0:1, 0:64],
                                                          ctx.t(p + "rrow")[0:1, 2, :], start=True, stop=True),
                             reads=(ctx.b(p + "rrow"), ctx.b("onesrow")), writes=(pDb,))
                    sch.step("scalar", lambda e: e.copy(ctx.t(p + "nbs")[:, 2, :], pD[0:64, :]),
                             reads=(pDb,), writes=(ctx.b(p + "nbs"),))
                    sch.step("vector", lambda e: e.tensor_mul(ctx.t(p + "wvs")[:, 1, :],
                                                              ctx.t(p + "wvs")[:, 0, :], ctx.t(p + "nbs")[:, 2, :]),
                             reads=(ctx.b(p + "wvs"), ctx.b(p + "nbs")), writes=(ctx.b(p + "wvs"),))
                    sch.step("vector", (lambda h=h, qt=qt: (lambda e: e.tensor_scalar(
                        ctx.t(p + "OUT8")[(h % 2) * 64:(h % 2) * 64 + 64, h // 2, qt * 512:(qt + 1) * 512],
                        ctx.t(p + "wvs")[:, 1, :], ctx.t(p + "gam")[:, 0:1], None, op0=ALU.mult)))(),
                        reads=(ctx.b(p + "wvs"), ctx.b(p + "gam")), writes=(ctx.b(p + "OUT8"),),
                        tag_writes=(qt == NTT - 1 and oc == NCH - 1 and r == 1))
        _emit_proj(ctx, din, p, xin_ap, xout_ap, seq, "s3_wo", None)


STAGE_INPUTS["attn_b0"] = ["s3_wq", "s3_wq_sw", "s3_wk", "s3_wk_sw", "s3_wv", "s3_wo",
                           "s3_lam", "s3_gamma", "s3_A", "s3_B", "tril"]
STAGE_ALLOC["attn_b0"] = alloc_stage_attn_b0
STAGE_EMIT["attn_b0"] = emit_attn_b0


def alloc_stage_attn_a1(ctx, din):
    p = "a1"
    sch = ctx.sch
    ctx.sbuf(p + "QT", [128, T], BF16)
    ctx.sbuf(p + "KT", [128, T], BF16)
    ctx.sbuf(p + "VT", [128, T], BF16)
    ctx.sbuf(p + "Vr", [128, 2, 520], BF16)
    ctx.sbuf(p + "ptile", [128, 2, 256], BF16)
    ctx.sbuf(p + "NST", [33, 4, T], BF16)
    if "rows_bf" not in ctx.sb:
        ctx.sb["rows_bf"] = ctx.sb[p + "NST"]
        ctx.buf["rows_bf"] = ctx.buf[p + "NST"]
    ctx.sbuf(p + "wsc", [1, 2048], BF16)
    ctx.sbuf(p + "KMx", [1, 8], F32)
    ctx.sbuf(p + "sqn", [128, 512], BF16)
    ctx.sbuf(p + "prs", [128, 2, 512], F32)
    ctx.sbuf(p + "nbs", [128, 256], BF16)
    ctx.sbuf(p + "rrow", [1, 256], BF16)
    wp = ctx.sbuf(p + "wpool", [128, 8192], BF16)
    ctx.sb[p + "wsA"] = wp.ap().rearrange("p (c o) -> p c o", c=8)
    ctx.buf[p + "wsA"] = ctx.buf[p + "wpool"]
    ctx.sb[p + "wqo"] = wp.ap().rearrange("p (s c w o) -> p s c w o", s=2, c=8, w=4)
    ctx.buf[p + "wqo"] = ctx.buf[p + "wpool"]
    t = ctx.sbuf(p + "idb", [128, 64], BF16)
    sch.step("gpsimd", (lambda t=t: (lambda e: e.dma_start(t[:, :], din["ident64d"])))(),
             writes=(ctx.b(p + "idb"),), dma=True)
    ctx.sb[p + "OUT8"] = ctx.sb["xbf"]
    shared = Buf(p + "out8_alias")
    ctx.buf[p + "OUT8"] = shared
    for c in range(NCH):
        ctx.buf["xbf_%d" % c] = shared
    ctx.psum(p + "pA", [128, 512], F32)
    ctx.psum(p + "pC", [128, 512], F32)
    ctx.psum(p + "pD", [128, 512], F32)
    ctx.psum(p + "pT", [128, 1024], BF16)


def emit_attn_a1(ctx, din, xin_ap, xout_ap):
    p = "a1"
    sch = ctx.sch
    pA, pC, pD, pT = (ctx.t(p + x) for x in ("pA", "pC", "pD", "pT"))
    pAb, pCb, pDb, pTb = (ctx.b(p + x) for x in ("pA", "pC", "pD", "pT"))
    QT, KT, VT = ctx.t(p + "QT"), ctx.t(p + "KT"), ctx.t(p + "VT")
    QTb, KTb, VTb = ctx.b(p + "QT"), ctx.b(p + "KT"), ctx.b(p + "VT")
    Vr, Vrb = ctx.t(p + "Vr"), ctx.b(p + "Vr")
    W = ctx.t(p + "wsc")
    Wb = ctx.b(p + "wsc")
    for seq in range(SEQS):
        emit_ln(ctx, xin_ap, seq)
        sch.step("vector", lambda e: e.memset(Vr[:, :, 512:513], 1.0), writes=(Vrb,), small=True)
        for oc in range(NCH):
            wsl = oc % 2
            for wi, wn in enumerate(["s5_wq", "s5_wk", "s5_wv"]):
                sch.step("gpsimd", (lambda wn=wn, wi=wi, oc=oc, wsl=wsl: (lambda e: e.dma_start(
                    ctx.t(p + "wqo")[:, wsl, :, wi, :],
                    din[wn][:, :, oc * 128:(oc + 1) * 128].rearrange("c p o -> p c o"))))(),
                    writes=(ctx.b(p + "wqo"),), dma=True, tag_writes=(wi == 2))
            for wi, (dst, dstb) in enumerate(((QT, QTb), (KT, KTb), (VT, VTb))):
                for tt in range(NTT):
                    fns = []
                    for dc in range(NCH):
                        fns.append((lambda dc=dc, tt=tt, wi=wi, wsl=wsl: (lambda e: e.matmul(
                            pA[:, :], ctx.t(p + "wqo")[:, wsl, dc, wi, :],
                            ctx.t("xln")[:, dc, tt * 512:(tt + 1) * 512],
                            start=(dc == 0), stop=(dc == NCH - 1))))())
                    sch.step("tensor", fns, reads=(ctx.b(p + "wqo"), ctx.b("xln")), writes=(pAb,))
                    sch.step("scalar", (lambda tt=tt, dst=dst: (lambda e: e.copy(
                        dst[:, tt * 512:(tt + 1) * 512], pA[:, :])))(),
                        reads=(pAb,), writes=(dstb,), tag_writes=(tt == NTT - 1))
                    if wi < 2:
                        sch.step("scalar", (lambda tt=tt: (lambda e: e.activation(
                            ctx.t(p + "sqn")[:, :], pA[:, :], AF.Square)))(),
                            reads=(pAb,), writes=(ctx.b(p + "sqn"),))
                        for r in range(2):
                            tgt, tgtb = (pD, pDb) if r == 0 else (pC, pCb)
                            sch.step("tensor", (lambda r=r, tgt=tgt: (lambda e: e.matmul(
                                tgt[0:1, :], ctx.t("onesb")[64 * r:64 * r + 64, 0:1],
                                ctx.t(p + "sqn")[64 * r:64 * r + 64, :], start=True, stop=True)))(),
                                reads=(ctx.b(p + "sqn"), ctx.b("onesb")), writes=(tgtb,))
                        for r in range(2):
                            tgt, tgtb = (pD, pDb) if r == 0 else (pC, pCb)
                            sch.step("scalar", (lambda r=r, tt=tt, wi=wi, tgt=tgt: (lambda e: e.copy(
                                ctx.t(p + "NST")[0:1, 2 * wi + r, tt * 512:(tt + 1) * 512],
                                tgt[0:1, :])))(),
                                reads=(tgtb,), writes=(ctx.b(p + "NST"),), tag_writes=(r == 1))
            for r in range(2):
                h = 2 * oc + r
                rb = 32 * r
                for wi in range(2):
                    pl = 2 * wi + r
                    dstseg = 0 if wi == 0 else 256
                    sch.step("vector", (lambda pl=pl, dstseg=dstseg: (lambda e: e.tensor_copy(
                        W[0:1, dstseg:dstseg + 256], ctx.t(p + "NST")[0:1, pl, 0:T:8])))(),
                        reads=(ctx.b(p + "NST"),), writes=(Wb,), small=True)
                    for p8 in range(1, 8):
                        sch.step("vector", (lambda pl=pl, p8=p8, dstseg=dstseg: (lambda e: e.tensor_add(
                            W[0:1, dstseg:dstseg + 256], W[0:1, dstseg:dstseg + 256],
                            ctx.t(p + "NST")[0:1, pl, p8:T:8])))(),
                            reads=(ctx.b(p + "NST"), Wb), writes=(Wb,), small=True)
                sch.step("vector", lambda e: e.reduce_max(ctx.t(p + "KMx")[0:1, 0:1], W[0:1, 256:512], axis=AX.X),
                         reads=(Wb,), writes=(ctx.b(p + "KMx"),), small=True)
                sch.step("scalar", lambda e: e.activation(ctx.t(p + "KMx")[0:1, 1:2], ctx.t(p + "KMx")[0:1, 0:1], AF.Sqrt),
                         reads=(ctx.b(p + "KMx"),), writes=(ctx.b(p + "KMx"),), small=True)
                sch.step("scalar", lambda e: e.activation(W[0:1, 512:768], W[0:1, 0:256], AF.Sqrt),
                         reads=(Wb,), writes=(Wb,), small=True)
                sch.step("vector", lambda e: e.tensor_scalar(
                    W[0:1, 512:768], W[0:1, 512:768], ctx.t(p + "KMx")[0:1, 1:2], -1.0,
                    op0=ALU.mult, op1=ALU.mult),
                    reads=(Wb, ctx.b(p + "KMx")), writes=(Wb,), small=True)
                for wt in range(2):
                    for p8 in range(8):
                        sch.step("tensor", (lambda p8=p8, wt=wt, r=r: (lambda e: e.transpose(
                            pT[:, p8 * 64:(p8 + 1) * 64],
                            VT[64 * r:64 * r + 64, 1024 * wt + p8:1024 * (wt + 1):8],
                            ctx.t(p + "idb")[64 * r:64 * r + 64, 0:64])))(),
                            reads=(VTb, ctx.b(p + "idb")), writes=(pTb,), tag_writes=(p8 == 7))
                    sch.step("vector", (lambda wt=wt: (lambda e: e.tensor_copy(
                        Vr[:, wt, 0:512], pT[:, 0:512])))(),
                        reads=(pTb,), writes=(Vrb,), tag_writes=(wt == 1))
                for wpt in range(2):
                    fns = []
                    for p8 in range(8):
                        fns.append((lambda p8=p8, wpt=wpt, r=r: (lambda e: e.matmul(
                            pA[:, wpt * 256:(wpt + 1) * 256],
                            KT[64 * r:64 * r + 64, 1024 * wpt + p8:1024 * (wpt + 1):8],
                            QT[64 * r:64 * r + 64, p8:T:8],
                            start=(p8 == 0), stop=False)))())
                    fns.append((lambda wpt=wpt: (lambda e: e.matmul(
                        pA[:, wpt * 256:(wpt + 1) * 256], ctx.t("onesrow")[0:1, :],
                        W[0:1, 512:768], start=False, stop=True)))())
                    sch.step("tensor", fns, reads=(KTb, QTb, Wb, ctx.b("onesrow")), writes=(pAb,))
                    sch.step("scalar", (lambda wpt=wpt: (lambda e: e.activation(
                        ctx.t(p + "ptile")[:, wpt, :], pA[:, wpt * 256:(wpt + 1) * 256],
                        AF.Exp, scale=0.125)))(),
                        reads=(pAb,), writes=(ctx.b(p + "ptile"),), tag_writes=(wpt == 1))
                fns = []
                for wpt in range(2):
                    fns.append((lambda wpt=wpt: (lambda e: e.matmul(
                        pD[0:1, 0:256], Vr[:, wpt, 512:513], ctx.t(p + "ptile")[:, wpt, :],
                        start=(wpt == 0), stop=(wpt == 1))))())
                sch.step("tensor", fns, reads=(Vrb, ctx.b(p + "ptile")), writes=(pDb,))
                sch.step("vector", lambda e: _recip_bf(ctx, e, ctx.t(p + "rrow")[0:1, :], pD[0:1, 0:256]),
                         reads=(pDb,), writes=(ctx.b(p + "rrow"),))
                sch.step("tensor", lambda e: e.matmul(pD[:, 256:512], ctx.t("onesrow")[0:1, :],
                                                      ctx.t(p + "rrow")[0:1, :], start=True, stop=True),
                         reads=(ctx.b(p + "rrow"), ctx.b("onesrow")), writes=(pDb,))
                sch.step("scalar", lambda e: e.copy(ctx.t(p + "nbs")[:, :], pD[:, 256:512]),
                         reads=(pDb,), writes=(ctx.b(p + "nbs"),))
                for fc in range(4):
                    fns = []
                    for wpt in range(2):
                        fns.append((lambda fc=fc, wpt=wpt: (lambda e: e.matmul(
                            pC[:, 0:256], Vr[:, wpt, fc * 128:(fc + 1) * 128],
                            ctx.t(p + "ptile")[:, wpt, :],
                            start=(wpt == 0), stop=(wpt == 1))))())
                    sch.step("tensor", fns, reads=(Vrb, ctx.b(p + "ptile")), writes=(pCb,))
                    for e2 in range(2):
                        p8 = 2 * fc + e2
                        sch.step("vector", (lambda fc=fc, e2=e2, p8=p8, h=h: (lambda e: e.tensor_mul(
                            ctx.t(p + "OUT8")[(h % 2) * 64:(h % 2) * 64 + 64, h // 2, p8:T:8],
                            pC[64 * e2:64 * e2 + 64, 0:256],
                            ctx.t(p + "nbs")[64 * e2:64 * e2 + 64, :])))(),
                            reads=(pCb, ctx.b(p + "nbs")), writes=(ctx.b(p + "OUT8"),),
                            tag_writes=(oc == NCH - 1 and r == 1 and fc == 3 and e2 == 1))
        _emit_proj(ctx, din, p, xin_ap, xout_ap, seq, "s5_wproj", None)


STAGE_INPUTS["attn_a1"] = ["s5_wq", "s5_wk", "s5_wv", "s5_wproj", "ident64d"]
STAGE_ALLOC["attn_a1"] = alloc_stage_attn_a1
STAGE_EMIT["attn_a1"] = emit_attn_a1


def alloc_stage_moe(ctx, din):
    p = "mo"
    sch = ctx.sch
    if "rows_bf" not in ctx.sb:
        ctx.sbuf("rows_bf", [1, 2, T], BF16)
    ctx.sbuf(p + "ys", [128, NCH, T], F32)
    ctx.sbuf(p + "hs", [128, 6, 512], BF16)
    ctx.sbuf(p + "w1s0", [128, 8, 768], BF16)
    ctx.sbuf(p + "w3s0", [128, 8, 768], BF16)
    ctx.sbuf(p + "w2h0_0", [128, 6, 512], BF16)
    ctx.sbuf(p + "w2h1_0", [128, 6, 512], BF16)
    ctx.sbuf(p + "w2h0_1", [128, 6, 512], BF16)
    xbf_t = ctx.sb["xbf"]
    ctx.sb[p + "w1s1"] = xbf_t.ap()[:, :, 0:768]
    ctx.sb[p + "w3s1"] = xbf_t.ap()[:, :, 768:1536]
    ctx.sb[p + "w2h1_1"] = xbf_t.ap()[:, 0:6, 1536:2048]
    wal = Buf(p + "w13alias")
    ctx.buf[p + "w1s1"] = wal
    ctx.buf[p + "w3s1"] = wal
    ctx.buf[p + "w2h1_1"] = wal
    for c in range(NCH):
        ctx.buf["xbf_%d" % c] = wal
    ctx.sb[p + "cbs"] = ctx.sb["tsub"].ap()
    cal = Buf(p + "cbsalias")
    ctx.buf[p + "cbs"] = cal
    ctx.buf["tsub_0"] = cal
    ctx.buf["tsub_1"] = cal
    sq_t = ctx.sb["sqtt"]
    ctx.sb[p + "CT"] = sq_t.ap()[0:16, 3:7, :].rearrange("p a b -> p (a b)")
    ctx.buf[p + "CT"] = ctx.buf["sqtt"]
    t = ctx.sbuf(p + "gwTs", [128, 8, 16], BF16)
    sch.step("gpsimd", (lambda t=t: (lambda e: e.dma_start(
        t[:, :, :], din["gwT"].rearrange("(c p) e -> p c e", p=128))))(),
        writes=(ctx.b(p + "gwTs"),), dma=True)
    t = ctx.sbuf(p + "gbr", [1, 16], F32)
    sch.step("sync", (lambda t=t: (lambda e: e.dma_start(t[0:1, :], din["gbias"])))(),
             writes=(ctx.b(p + "gbr"),), dma=True)
    ctx.sb[p + "idf"] = ctx.sb["xio"].ap()[:, 0, 0:128]
    ctx.buf[p + "idf"] = ctx.buf["xio_0"]
    ctx.sbuf(p + "GBB", [128, 16], BF16)
    for nm, w in ((p + "SC", 16), (p + "SB2", 16), (p + "PS", 4), (p + "G2", 4),
                  (p + "EM", 12), (p + "MS", 16), (p + "MS2", 16), (p + "CMB", 16),
                  (p + "rsc", 8)):
        ctx.sbuf(nm, [128, w], F32)
    ctx.psum(p + "pA", [128, 512], F32)
    ctx.psum(p + "pB", [128, 512], F32)
    ctx.psum(p + "pC", [128, 512], F32)
    ctx.psum(p + "pD", [128, 512], F32)


def _emit_routing_tile(ctx, p, tl):
    sch = ctx.sch
    pA, pD = ctx.t(p + "pA"), ctx.t(p + "pD")
    pAb, pDb = ctx.b(p + "pA"), ctx.b(p + "pD")
    SC, SB2, PS, G2 = (ctx.t(p + x) for x in ("SC", "SB2", "PS", "G2"))
    EM, MS, MS2, CMB, rsc = (ctx.t(p + x) for x in ("EM", "MS", "MS2", "CMB", "rsc"))
    SCb, SB2b, PSb, G2b = (ctx.b(p + x) for x in ("SC", "SB2", "PS", "G2"))
    EMb, MSb, MS2b, CMBb, rscb = (ctx.b(p + x) for x in ("EM", "MS", "MS2", "CMB", "rsc"))
    fns = []
    for dc in range(NCH):
        fns.append((lambda dc=dc, tl=tl: (lambda e: e.matmul(
            pA[:, 0:16], ctx.t("xln")[:, dc, tl * 128:(tl + 1) * 128],
            ctx.t(p + "gwTs")[:, dc, :], start=(dc == 0), stop=(dc == NCH - 1))))())
    sch.step("tensor", fns, reads=(ctx.b("xln"), ctx.b(p + "gwTs")), writes=(pAb,))
    S = sch.step
    S("scalar", lambda e: e.activation(SC[:, :], pA[:, 0:16], AF.Sigmoid),
      reads=(pAb,), writes=(SCb,), small=True)
    S("vector", lambda e: e.tensor_add(SB2[:, :], SC[:, :], ctx.t(p + "GBB")[:, :]),
      reads=(SCb, ctx.b(p + "GBB")), writes=(SB2b,), small=True)
    pairs = [(0, 1), (0, 2), (0, 3), (1, 2), (1, 3), (2, 3)]
    for i, (a, b) in enumerate(pairs):
        if i == 0:
            S("vector", (lambda a=a, b=b: (lambda e: e.tensor_add(
                G2[:, :], SB2[:, a:16:4], SB2[:, b:16:4])))(),
              reads=(SB2b,), writes=(G2b,), small=True)
        else:
            S("vector", (lambda a=a, b=b: (lambda e: e.tensor_add(
                PS[:, 0:4], SB2[:, a:16:4], SB2[:, b:16:4])))(),
              reads=(SB2b,), writes=(PSb,), small=True)
            S("vector", lambda e: e.tensor_max(G2[:, :], G2[:, :], PS[:, 0:4]),
              reads=(PSb, G2b), writes=(G2b,), small=True)
    S("vector", lambda e: e.reduce_max(rsc[:, 0:1], G2[:, :], axis=AX.X),
      reads=(G2b,), writes=(rscb,), small=True)
    S("vector", lambda e: e.tensor_scalar(EM[:, 0:4], G2[:, :], rsc[:, 0:1], None, op0=ALU.is_ge),
      reads=(G2b, rscb), writes=(EMb,), small=True)
    S("vector", lambda e: e.scalar_tensor_tensor(EM[:, 4:8], EM[:, 0:4], -1e9, G2[:, :],
                                                 op0=ALU.mult, op1=ALU.add),
      reads=(EMb, G2b), writes=(EMb,), small=True)
    S("vector", lambda e: e.reduce_max(rsc[:, 1:2], EM[:, 4:8], axis=AX.X),
      reads=(EMb,), writes=(rscb,), small=True)
    S("vector", lambda e: e.tensor_scalar(EM[:, 8:12], G2[:, :], rsc[:, 1:2], None, op0=ALU.is_ge),
      reads=(G2b, rscb), writes=(EMb,), small=True)
    for a in range(4):
        S("vector", (lambda a=a: (lambda e: e.tensor_copy(MS2[:, a:16:4], EM[:, 8:12])))(),
          reads=(EMb,), writes=(MS2b,), small=True, tag_writes=(a == 3))
    S("vector", lambda e: e.tensor_scalar(MS[:, :], MS2[:, :], 1e9, -1e9, op0=ALU.mult, op1=ALU.add),
      reads=(MS2b,), writes=(MSb,), small=True)
    S("vector", lambda e: e.tensor_add(MS[:, :], MS[:, :], SB2[:, :]),
      reads=(MSb, SB2b), writes=(MSb,), small=True)
    S("vector", lambda e: e.tensor_copy(MS2[:, :], MS[:, :]),
      reads=(MSb,), writes=(MS2b,), small=True)
    for it in range(3):
        S("vector", (lambda it=it: (lambda e: e.reduce_max(rsc[:, 2 + it:3 + it], MS2[:, :], axis=AX.X)))(),
          reads=(MS2b,), writes=(rscb,), small=True)
        S("vector", (lambda it=it: (lambda e: e.tensor_scalar(
            EM[:, 12:16] if False else CMB[:, :], MS2[:, :], rsc[:, 2 + it:3 + it], None, op0=ALU.is_ge)))(),
          reads=(MS2b, rscb), writes=(CMBb,), small=True)
        S("vector", lambda e: e.scalar_tensor_tensor(MS2[:, :], CMB[:, :], -1e9, MS2[:, :],
                                                     op0=ALU.mult, op1=ALU.add),
          reads=(CMBb, MS2b), writes=(MS2b,), small=True)
    S("vector", lambda e: e.reduce_max(rsc[:, 5:6], MS2[:, :], axis=AX.X),
      reads=(MS2b,), writes=(rscb,), small=True)
    S("vector", lambda e: e.tensor_scalar(CMB[:, :], MS[:, :], rsc[:, 5:6], None, op0=ALU.is_ge),
      reads=(MSb, rscb), writes=(CMBb,), small=True)
    S("vector", lambda e: e.tensor_mul(CMB[:, :], CMB[:, :], SC[:, :]),
      reads=(CMBb, SCb), writes=(CMBb,), small=True)
    S("vector", lambda e: e.reduce_sum(rsc[:, 6:7], CMB[:, :], axis=AX.X),
      reads=(CMBb,), writes=(rscb,), small=True)
    S("vector", lambda e: e.tensor_scalar(rsc[:, 7:8], rsc[:, 6:7], 1e-20, None, op0=ALU.add),
      reads=(rscb,), writes=(rscb,), small=True)
    S("vector", lambda e: e.reciprocal(rsc[:, 6:7], rsc[:, 7:8]),
      reads=(rscb,), writes=(rscb,), small=True)
    S("vector", lambda e: e.tensor_scalar(CMB[:, :], CMB[:, :], rsc[:, 6:7], 2.5, op0=ALU.mult, op1=ALU.mult),
      reads=(CMBb, rscb), writes=(CMBb,), small=True)
    S("tensor", (lambda tl=tl: (lambda e: e.transpose(
        pD[0:16, 0:128], CMB[:, :], ctx.t(p + "idf")[:, :])))(),
      reads=(CMBb, ctx.b(p + "idf")), writes=(pDb,))
    S("scalar", (lambda tl=tl: (lambda e: e.copy(
        ctx.t(p + "CT")[0:16, tl * 128:(tl + 1) * 128], pD[0:16, 0:128])))(),
      reads=(pDb,), writes=(ctx.b(p + "CT"),), small=True)


def emit_moe(ctx, din, xin_ap, xout_ap):
    p = "mo"
    sch = ctx.sch
    pA, pB, pC, pD = (ctx.t(p + x) for x in ("pA", "pB", "pC", "pD"))
    pAb, pBb, pCb, pDb = (ctx.b(p + x) for x in ("pA", "pB", "pC", "pD"))
    ys, ysb = ctx.t(p + "ys"), ctx.b(p + "ys")
    hs, hsb = ctx.t(p + "hs"), ctx.b(p + "hs")
    sch.step("scalar", lambda e: e.copy(ctx.t("rows_bf")[0:1, 0, 0:16], ctx.t(p + "gbr")[0:1, :]),
             reads=(ctx.b(p + "gbr"),), writes=(ctx.b("rows_bf"),), small=True)
    sch.step("tensor", lambda e: e.matmul(pD[:, 0:16], ctx.t("onesrow")[0:1, :],
                                          ctx.t("rows_bf")[0:1, 0, 0:16], start=True, stop=True),
             reads=(ctx.b("rows_bf"), ctx.b("onesrow")), writes=(pDb,))
    sch.step("vector", lambda e: e.tensor_copy(ctx.t(p + "GBB")[:, :], pD[:, 0:16]),
             reads=(pDb,), writes=(ctx.b(p + "GBB"),), small=True)
    for seq in range(SEQS):
        emit_ln(ctx, xin_ap, seq)
        sch.step("sync", lambda e: e.dma_start(ctx.t(p + "idf")[:, :], din["ident"]),
                 writes=(ctx.b("xio_0"),), dma=True)
        for tl in range(16):
            _emit_routing_tile(ctx, p, tl)
        def w_dmas(e_j):
            if e_j < 0:
                w1ap, w3ap, w2ap = din["ws1T"], din["ws3T"], din["ws2T"]
            else:
                w1ap = din["moe_w1T"][e_j]
                w3ap = din["moe_w3T"][e_j]
                w2ap = din["moe_w2T"][e_j]
            sl = (e_j + 1) % 2
            sch.step("gpsimd", (lambda w1ap=w1ap, sl=sl: (lambda e: e.dma_start(
                ctx.t(p + "w1s%d" % sl)[:, :, :], w1ap.rearrange("c p o -> p c o"))))(),
                writes=(ctx.b(p + "w1s%d" % sl),), dma=True, waw=(sl == 0))
            sch.step("gpsimd", (lambda w3ap=w3ap, sl=sl: (lambda e: e.dma_start(
                ctx.t(p + "w3s%d" % sl)[:, :, :], w3ap.rearrange("c p o -> p c o"))))(),
                writes=(ctx.b(p + "w3s%d" % sl),), dma=True, waw=(sl == 0))
            for hh in range(2):
                sch.step("gpsimd", (lambda w2ap=w2ap, hh=hh, e_j=e_j: (lambda e: e.dma_start(
                    ctx.t(p + "w2h%d_%d" % (hh, (e_j + 1) % 2))[:, :, :],
                    w2ap[:, :, hh * 512:(hh + 1) * 512].rearrange("c p o -> p c o"))))(),
                    writes=(ctx.b(p + "w2h%d_%d" % (hh, (e_j + 1) % 2)),), dma=True,
                    waw=(hh == 0 or (e_j + 1) % 2 == 0))
        w_dmas(-1)
        for e_i in range(-1, 16):
            shared = (e_i < 0)
            sl = (e_i + 1) % 2
            if e_i < 15:
                w_dmas(e_i + 1)
            if not shared:
                sch.step("sync", (lambda e_i=e_i: (lambda e: e.dma_start(
                    ctx.t("rows_bf")[0:1, 0, :], ctx.t(p + "CT")[e_i:e_i + 1, :])))(),
                    reads=(ctx.b(p + "CT"),), writes=(ctx.b("rows_bf"),), dma=True)
            for tt in range(NTT):
                hsl = tt % 2
                if not shared:
                    sch.step("tensor", (lambda tt=tt: (lambda e: e.matmul(
                        pD[:, 0:512], ctx.t("onesrow")[0:1, :],
                        ctx.t("rows_bf")[0:1, 0, tt * 512:(tt + 1) * 512], start=True, stop=True)))(),
                        reads=(ctx.b("rows_bf"), ctx.b("onesrow")), writes=(pDb,))
                    sch.step("scalar", (lambda tt=tt, hsl=hsl: (lambda e: e.copy(
                        ctx.t(p + "cbs")[:, hsl, :], pD[:, 0:512])))(),
                        reads=(pDb,), writes=(ctx.b(p + "cbs"),))
                for ic in range(6):
                    gb_ = (pA, pAb) if ic % 2 == 0 else (pB, pBb)
                    fns = []
                    for dc in range(NCH):
                        fns.append((lambda dc=dc, ic=ic, tt=tt, gp=gb_[0], sl=sl: (lambda e: e.matmul(
                            gp[:, :], ctx.t(p + "w1s%d" % sl)[:, dc, ic * 128:(ic + 1) * 128],
                            ctx.t("xln")[:, dc, tt * 512:(tt + 1) * 512],
                            start=(dc == 0), stop=(dc == NCH - 1))))())
                    sch.step("tensor", fns, reads=(ctx.b(p + "w1s%d" % sl), ctx.b("xln")), writes=(gb_[1],))
                    sch.step("scalar", (lambda gp=gb_[0]: (lambda e: e.activation(
                        ctx.t("sqtt")[:, 0, :], gp[:, :], AF.Silu)))(),
                        reads=(gb_[1],), writes=(ctx.b("sqtt"),))
                    ub_ = (pB, pBb) if ic % 2 == 0 else (pA, pAb)
                    fns = []
                    for dc in range(NCH):
                        fns.append((lambda dc=dc, ic=ic, tt=tt, up=ub_[0], sl=sl: (lambda e: e.matmul(
                            up[:, :], ctx.t(p + "w3s%d" % sl)[:, dc, ic * 128:(ic + 1) * 128],
                            ctx.t("xln")[:, dc, tt * 512:(tt + 1) * 512],
                            start=(dc == 0), stop=(dc == NCH - 1))))())
                    sch.step("tensor", fns, reads=(ctx.b(p + "w3s%d" % sl), ctx.b("xln")), writes=(ub_[1],))
                    if shared:
                        sch.step("scalar", (lambda up=ub_[0]: (lambda e: e.copy(
                            ctx.t("sqtt")[:, 1, :], up[:, :])))(),
                            reads=(ub_[1],), writes=(ctx.b("sqtt"),))
                        sch.step("vector", (lambda ic=ic, hsl=hsl: (lambda e: e.tensor_mul(
                            hs[:, ic, :], ctx.t("sqtt")[:, 0, :], ctx.t("sqtt")[:, 1, :])))(),
                            reads=(ctx.b("sqtt"), ctx.b("sqtt")), writes=(hsb,),
                            tag_writes=(ic == 5))
                    else:
                        sch.step("vector", (lambda hsl=hsl, up=ub_[0]: (lambda e: e.tensor_mul(
                            ctx.t("sqtt")[:, 2, :], up[:, :], ctx.t(p + "cbs")[:, hsl, :])))(),
                            reads=(ub_[1], ctx.b(p + "cbs")), writes=(ctx.b("sqtt"),))
                        sch.step("vector", (lambda ic=ic, hsl=hsl: (lambda e: e.tensor_mul(
                            hs[:, ic, :], ctx.t("sqtt")[:, 0, :], ctx.t("sqtt")[:, 2, :])))(),
                            reads=(ctx.b("sqtt"), ctx.b("sqtt")), writes=(hsb,),
                            tag_writes=(ic == 5))
                for oc in range(NCH):
                    yb_ = (pC, pCb) if oc % 2 == 0 else (pD, pDb)
                    fns = []
                    hh = oc // 4
                    w2n = p + "w2h%d_%d" % (hh, sl)
                    for ic in range(6):
                        fns.append((lambda ic=ic, oc=oc, tt=tt, hsl=hsl, yp=yb_[0], w2n=w2n: (lambda e: e.matmul(
                            yp[:, :], ctx.t(w2n)[:, ic, (oc % 4) * 128:(oc % 4 + 1) * 128],
                            hs[:, ic, :], start=(ic == 0), stop=(ic == 5))))())
                    sch.step("tensor", fns, reads=(ctx.b(w2n), hsb), writes=(yb_[1],))
                    if shared:
                        sch.step("scalar", (lambda oc=oc, tt=tt, yp=yb_[0]: (lambda e: e.copy(
                            ys[:, oc, tt * 512:(tt + 1) * 512], yp[:, :])))(),
                            reads=(yb_[1],), writes=(ysb,), tag_writes=(oc == NCH - 1 and tt == NTT - 1))
                    else:
                        sch.step("vector", (lambda oc=oc, tt=tt, yp=yb_[0]: (lambda e: e.tensor_add(
                            ys[:, oc, tt * 512:(tt + 1) * 512], yp[:, :],
                            ys[:, oc, tt * 512:(tt + 1) * 512])))(),
                            reads=(yb_[1], ysb), writes=(ysb,), tag_writes=(oc == NCH - 1 and tt == NTT - 1))
        for oc in range(NCH):
            for tt in range(NTT):
                sl = (oc * NTT + tt) % 2
                sch.step("sync", (lambda oc=oc, seq=seq, sl=sl, tt=tt: (lambda e: e.dma_start(
                    ctx.t("xio")[:, sl, :],
                    xin_ap[seq, oc * 128:(oc + 1) * 128, tt * 512:(tt + 1) * 512])))(),
                    writes=(ctx.b("xio_%d" % sl),), dma=True)
                sch.step("vector", (lambda oc=oc, tt=tt, sl=sl: (lambda e: e.tensor_add(
                    ctx.t("xio")[:, sl, :], ys[:, oc, tt * 512:(tt + 1) * 512],
                    ctx.t("xio")[:, sl, :])))(),
                    reads=(ysb,), writes=(ctx.b("xio_%d" % sl),))
                sch.step("sync", (lambda oc=oc, tt=tt, seq=seq, sl=sl: (lambda e: e.dma_start(
                    xout_ap[seq, oc * 128:(oc + 1) * 128, tt * 512:(tt + 1) * 512],
                    ctx.t("xio")[:, sl, :])))(),
                    reads=(ctx.b("xio_%d" % sl),), writes=(), dma=True)


def alloc_stage_moe_wrap(ctx, din):
    alloc_stage_moe(ctx, din)


STAGE_INPUTS["moe"] = ["gwT", "gbias", "moe_w1T", "moe_w3T", "moe_w2T",
                       "ws1T", "ws3T", "ws2T", "ident"]
STAGE_ALLOC["moe"] = alloc_stage_moe_wrap
STAGE_EMIT["moe"] = emit_moe
